# version 11
# baseline (speedup 1.0000x reference)
import numpy as np
import ml_dtypes
import concourse.bass as bass
import concourse.mybir as mybir
from concourse.bass_utils import run_bass_kernel_spmd

F32 = mybir.dt.float32
BF16 = mybir.dt.bfloat16
AF = mybir.ActivationFunctionType
ALU = mybir.AluOpType
AX = mybir.AxisListType

MAGIC = float(np.float32(1.5 * 2**23))
PHASE_LIMIT = 6
CHUNK_PARTS = 3
BIG = 1e30

B, T, D = 2, 8192, 1024
NCORE = 8
TQ = 2048
NT = TQ // 128
H, DH = 16, 64
C = 128
IF = 2816
NKC = D // 128

ENGINES = ("v", "a", "t", "g", "s")


class Tok:
    __slots__ = ("sem", "val")

    def __init__(self, sem, val):
        self.sem = sem
        self.val = val


class Sched:
    def __init__(self, nc, pool_sizes=None):
        self.nc = nc
        self._ctx = []
        self.sems = {}
        self.cnt = {}
        self.prog = {e: [] for e in ENGINES}
        self.last_waited = {e: {} for e in ENGINES}
        for e in ENGINES:
            self.sems[e] = self._mksem(f"sem_{e}")
            self.cnt[e] = 0
        self.sems["c"] = self._mksem("sem_c")
        self.cnt["c"] = 0
        ps = pool_sizes or {"s": 24, "a": 10, "g": 10}
        self.pools = {}
        for q, n in ps.items():
            self.pools[q] = {
                "sems": [self._mksem(f"sem_{q}dma{i}") for i in range(n)],
                "cnt": [0] * n,
                "next": 0,
            }

    def _mksem(self, name):
        cm = self.nc.semaphore(name)
        sem = cm.__enter__()
        self._ctx.append(cm)
        return sem

    def _emit_waits(self, eng_key, deps):
        waits = []
        lw = self.last_waited[eng_key]
        for d in deps:
            if d is None:
                continue
            key = id(d.sem)
            if lw.get(key, -1) >= d.val:
                continue
            lw[key] = d.val
            waits.append((d.sem, d.val))
        return waits

    def op(self, e, fn, deps=()):
        waits = self._emit_waits(e, deps)
        sem = self.sems[e]
        self.cnt[e] += 1
        self.prog[e].append((waits, fn, sem, 1))
        return Tok(sem, self.cnt[e])

    def cop(self, fn, deps=()):
        waits = self._emit_waits("g", deps)
        sem = self.sems["c"]
        self.cnt["c"] += 1
        self.prog["g"].append((waits, fn, sem, 1))
        return Tok(sem, self.cnt["c"])

    def dma(self, q, fn, deps=()):
        pool = self.pools[q]
        i = pool["next"]
        pool["next"] = (i + 1) % len(pool["sems"])
        sem = pool["sems"][i]
        prev = pool["cnt"][i]
        deps = list(deps)
        if prev:
            deps.append(Tok(sem, prev))
        waits = self._emit_waits(q, deps)
        pool["cnt"][i] = prev + 16
        self.prog[q].append((waits, fn, sem, 16))
        return Tok(sem, pool["cnt"][i])

    def gdma(self, fn, deps=()):
        return self.dma("g", fn, deps)

    def etok(self, e):
        return Tok(self.sems[e], self.cnt[e])

    def barrier(self):
        deps = [Tok(self.sems[e], self.cnt[e]) for e in ENGINES]
        if self.cnt["c"]:
            deps.append(Tok(self.sems["c"], self.cnt["c"]))
        for q, pool in self.pools.items():
            for i, cv in enumerate(pool["cnt"]):
                if cv:
                    deps.append(Tok(pool["sems"][i], cv))
        hub = self.op("v", lambda e: e.engine_nop(), deps=deps)
        for e in ENGINES:
            if e != "v":
                self.op(e, lambda en: en.engine_nop()
                        if hasattr(en, "engine_nop") else en.nop(),
                        deps=[hub])
            self.last_waited[e] = {}
        self.hub = hub
        return hub

    def finalize(self, block):
        def mk(e):
            def body(eng):
                for waits, fn, sem, inc in self.prog[e]:
                    for (s, vv) in waits:
                        eng.wait_ge(s, vv)
                    inst = fn(eng)
                    if inst is None:
                        raise RuntimeError(f"op fn on {e} returned None")
                    inst.then_inc(sem, inc)
            return body

        if self.prog["g"]:
            block.gpsimd(mk("g"))
        if self.prog["s"]:
            block.sync(mk("s"))
        if self.prog["t"]:
            block.tensor(mk("t"))
        if self.prog["v"]:
            block.vector(mk("v"))
        if self.prog["a"]:
            block.scalar(mk("a"))


class Arena:
    def __init__(self, nc, name, cols_f32):
        self.t = nc.alloc_sbuf_tensor(name, [128, cols_f32], F32).ap()
        self.cols = cols_f32
        self.off = 0

    def reset(self):
        self.off = 0

    def f32(self, p, n):
        a = self.t[:, self.off:self.off + n]
        self.off += n
        assert self.off <= self.cols, f"arena overflow {self.off}>{self.cols}"
        return a[0:p, :] if p != 128 else a

    def bf16(self, p, n):
        slots = (n + 1) // 2
        a = self.t[:, self.off:self.off + slots].bitcast(BF16)
        self.off += slots
        assert self.off <= self.cols, f"arena overflow {self.off}>{self.cols}"
        return a[0:p, 0:n] if p != 128 else a[:, 0:n]


def build_nc(debug=()):
    nc = bass.Bass("TRN2", target_bir_lowering=False)
    io = {}

    def inp(name, shape, dt=F32):
        return nc.dram_tensor(name, list(shape), dt, kind="ExternalInput").ap()

    io["x_in"] = inp("x_slice", (TQ, D))
    io["wqkv"] = inp("Wqkv", (3072, D))
    io["wg"] = inp("Wg", (H, D))
    io["bgv"] = inp("bg", (1, H))
    io["wb"] = inp("Wb", (H, D))
    io["bbv"] = inp("bb", (1, H))
    io["wout"] = inp("Wout", (D, D))
    io["wgate_sh"] = inp("wgate_sh", (IF, 128))
    io["wup_sh"] = inp("wup_sh", (IF, 128))
    io["wdown_sh"] = inp("wdown_sh", (D, IF // NCORE))
    io["c_in"] = inp("c_vec", (B, D))
    io["wmod"] = inp("wmod", (D, 6 * D))
    io["sel128"] = inp("sel128", (B, 128))
    io["hopmask"] = inp("hopmask", (1, NCORE))
    io["ident128"] = inp("ident128", (128, 128))
    io["mask_ts"] = inp("mask_ts", (128, 128))
    io["mask_sts"] = inp("mask_sts", (128, 128))
    io["mask_sti"] = inp("mask_sti", (128, 128))
    io["i64two"] = inp("i64two", (128, 64))
    io["i64rep8"] = inp("i64rep8", (128, 512))
    io["bdmask"] = inp("bdmask", (128, 128))
    io["ipat"] = inp("ipat", (128, H * 2 * DH))
    io["out_x"] = nc.dram_tensor("out_x", [TQ, D], F32, kind="ExternalOutput").ap()

    dbg = {}

    def dbgout(name, shape=None, dt=F32):
        if name in debug and name not in dbg:
            dbg[name] = nc.dram_tensor("dbg_" + name, list(shape), dt,
                                       kind="ExternalOutput").ap()
        return dbg.get(name)

    dr = lambda name, shape, dt: nc.dram_tensor(name, list(shape), dt).ap()
    io["h_dram"] = dr("h_dram", (NKC, TQ, 128), BF16)
    io["wall_dram"] = dr("wall_dram", (NKC, 3104, 128), BF16)
    io["wout_dram"] = dr("wout_dram", (NKC, D, 128), BF16)
    io["kpl_dram"] = dr("kpl_dram", (NKC, TQ, 128), BF16)
    io["kb_dram"] = dr("kb_dram", (NKC, TQ, 128), BF16)
    io["q_dram"] = dr("q_dram", (NKC, TQ, 128), BF16)
    io["qg_dram"] = dr("qg_dram", (NKC, TQ, 128), BF16)
    io["bv_dram"] = dr("bv_dram", (NKC, TQ, 128), BF16)
    io["l_dram"] = dr("l_dram", (NT, H * C), F32)
    io["lc_dram"] = dr("lc_dram", (1, NT * H), F32)
    io["qpt_dram"] = dr("qpt_dram", (NT * H, DH * C), BF16)
    io["x2_dram"] = dr("x2_dram", (TQ, D), F32)
    io["h2q_dram"] = dr("h2q_dram", (NKC, TQ, 128), BF16)
    io["hmid_dram"] = dr("hmid_dram", (IF // 128, TQ, 128), BF16)
    io["agw_in"] = dr("agw_in", (128, 2 * IF), BF16)
    io["agw_out"] = dr("agw_out", (NCORE * 128, 2 * IF), BF16)
    io["agd_in"] = dr("agd_in", (IF // NCORE, D), BF16)
    io["agd_out"] = dr("agd_out", (IF, D), BF16)
    io["ags_in"] = dr("ags_in", (DH, H * 2 * DH), BF16)
    io["ags_out"] = dr("ags_out", (NCORE * DH, H * 2 * DH), BF16)

    with nc.Block() as block:
        S = Sched(nc)
        K = Kern(nc, S, io, dbgout)
        K.emit()
        S.finalize(block)
    return nc, dbg


class Kern:
    def __init__(self, nc, S, io, dbgout):
        self.nc = nc
        self.S = S
        self.io = io
        self.dbgout = dbgout
        self.sb = lambda name, shape, dt=F32: nc.alloc_sbuf_tensor(
            name, list(shape), dt).ap()
        self.pb = [nc.alloc_psum_tensor(f"pb{i}", [128, 512], F32).ap()
                   for i in range(8)]
        self.arF = Arena(nc, "arF", 6200)
        self.arB = Arena(nc, "arB", 18800)

    def epoch(self):
        self.S.barrier()
        self.arF.reset()
        self.arB.reset()

    def emit(self):
        lim = PHASE_LIMIT
        self.persist()
        self.prologue()
        if lim >= 1:
            self.ffn_weight_quant()
            self.weights_attn()
        self.epoch()
        if lim >= 2:
            self.hphase()
        if lim >= 3:
            self.qkvphase()
        self.epoch()
        if lim >= 4:
            self.chunkphase()
        self.epoch()
        if lim >= 5:
            self.corrphase()
            self.woutphase()
        self.epoch()
        if lim >= 6:
            self.ffnphase()
            self.finish()
        else:
            self.dummy_out()

    def dummy_out(self):
        S, io = self.S, self.io
        z = S.op("v", lambda e: e.memset(self.x_t, 0.0),
                 deps=[S.hub])
        toks = []
        for i in range(NT):
            toks.append(S.dma("s", (lambda i_: lambda e: e.dma_start(
                out=io["out_x"][128 * i_:128 * (i_ + 1), :],
                in_=self.x_t))(i), deps=[z]))
        S.op("v", lambda e: e.engine_nop(), deps=toks)
        S.op("s", lambda e: e.nop(), deps=toks)

    def persist(self):
        S, io, sb = self.S, self.io, self.sb
        self.ident_sb = sb("ident_sb", (128, 128))
        self.mask_ts_sb = sb("mask_ts_sb", (128, 128))
        self.mask_sts_sb = sb("mask_sts_sb", (128, 128))
        self.mask_sti_sb = sb("mask_sti_sb", (128, 128))
        self.i64two_sb = sb("i64two_sb", (128, 64))
        self.i64two_bf = sb("i64two_bf", (128, 64), BF16)
        self.i64rep8f = sb("i64rep8f", (128, 512))
        self.i64rep8b = sb("i64rep8b", (128, 512), BF16)
        self.selF_sb = sb("selF_sb", (B, 128))
        self.hm_sb = sb("hm_sb", (128, NCORE))
        self.hm1_sb = sb("hm1_sb", (128, NCORE))
        self.bgbb_rep = sb("bgbb_rep", (128, 2 * H))
        t = {}
        t["id"] = S.dma("s", lambda e: e.dma_start(out=self.ident_sb, in_=io["ident128"]))
        t["mts"] = S.dma("s", lambda e: e.dma_start(out=self.mask_ts_sb, in_=io["mask_ts"]))
        t["msts"] = S.dma("s", lambda e: e.dma_start(out=self.mask_sts_sb, in_=io["mask_sts"]))
        t["msti"] = S.dma("s", lambda e: e.dma_start(out=self.mask_sti_sb, in_=io["mask_sti"]))
        t["i2"] = S.dma("s", lambda e: e.dma_start(out=self.i64two_sb, in_=io["i64two"]))
        t["i8"] = S.dma("s", lambda e: e.dma_start(out=self.i64rep8f, in_=io["i64rep8"]))
        self.bdm_f = sb("bdm_f", (128, 128))
        self.bdm_bf = sb("bdm_bf", (128, 128), BF16)
        self.odm_bf = sb("odm_bf", (128, 128), BF16)
        t["bdm"] = S.dma("s", lambda e: e.dma_start(out=self.bdm_f, in_=io["bdmask"]))
        t["sel"] = S.dma("s", lambda e: e.dma_start(out=self.selF_sb, in_=io["sel128"]))
        t["hm"] = S.dma("s", lambda e: e.dma_start(
            out=self.hm_sb, in_=io["hopmask"].partition_broadcast(128)))
        t["bgr"] = S.dma("s", lambda e: e.dma_start(
            out=self.bgbb_rep[:, 0:H], in_=io["bgv"].partition_broadcast(128)))
        t["bbr"] = S.dma("s", lambda e: e.dma_start(
            out=self.bgbb_rep[:, H:2 * H], in_=io["bbv"].partition_broadcast(128)))
        self.t_hm1 = S.op("v", lambda e: e.tensor_scalar(
            out=self.hm1_sb, in0=self.hm_sb, scalar1=-1.0, scalar2=1.0,
            op0=ALU.mult, op1=ALU.add), deps=[t["hm"]])
        self.ident_bf = self.sb("ident_bf", (128, 128), BF16)
        self.t_idb = S.op("v", lambda e: e.tensor_copy(
            out=self.ident_bf, in_=self.ident_sb), deps=[t["id"]])
        self.t_i64b = S.op("v", lambda e: e.tensor_copy(
            out=self.i64two_bf, in_=self.i64two_sb), deps=[t["i2"]])
        self.t_i8b = S.op("v", lambda e: e.tensor_copy(
            out=self.i64rep8b, in_=self.i64rep8f), deps=[t["i8"]])
        self.t_bdm = S.op("v", lambda e: e.tensor_copy(
            out=self.bdm_bf, in_=self.bdm_f), deps=[t["bdm"]])
        self.t_odm = S.op("v", lambda e: e.tensor_scalar(
            out=self.odm_bf, in0=self.ident_sb, scalar1=-1.0, scalar2=1.0,
            op0=ALU.mult, op1=ALU.add), deps=[t["id"]])
        self.t_pbz = [S.op("v", (lambda b_: lambda e: e.memset(
            self.pb[b_], 0.0))(b)) for b in range(8)]
        self.pt = t
        self.ln_m = sb("ln_m", (128, 1))
        self.ln_r = sb("ln_r", (128, 1))
        self.ln_t1 = sb("ln_t1", (128, 1))
        self.ln_t2 = sb("ln_t2", (128, 1))
        self.ln_xs = sb("ln_xs", (128, D))
        self.L_all = sb("L_all", (H, NT * C))
        self.LT = [sb(f"LT{i}", (128, H)) for i in range(NT)]
        self.expL = [sb(f"expL{i}", (128, H)) for i in range(NT)]
        self.b_sig = [sb(f"bsig{i}", (128, H)) for i in range(NT)]
        self.rnk_t = [sb(f"rnk{i}", (128, H)) for i in range(NT)]
        self.rnq_t = [sb(f"rnq{i}", (128, H)) for i in range(NT)]
        self.gcrep = sb("gcrep", (128, NT * H))
        self.gcp = sb("gcp", (128, NT * 8))
        self.zeroHC = sb("zeroHC", (H, C))
        self.t_zero = S.op("v", lambda e: e.memset(self.zeroHC, 0.0))
        self.Ot = [sb(f"Ot{p}", (128, TQ), BF16) for p in range(H // 2)]
        self.stloc = [sb(f"stloc{p}", (128, DH), BF16) for p in range(H // 2)]
        self.prun = [sb(f"prun{p}", (128, DH), BF16) for p in range(H // 2)]
        self.prunT = [sb(f"prunT{p}", (128, DH), BF16) for p in range(H // 2)]
        self.prun_store = sb("prun_store", (128, NT * (H // 2) * DH), BF16)
        self.rep = {nm: sb(f"rep_{nm}", (128, D))
                    for nm in ("sa", "ca", "ga", "sb2", "cb", "gb")}
        self.x_t = sb("x_t", (128, D))
        self.h_bf = sb("h_bf", (128, D), BF16)
        self.negLT = sb("negLT", (128, H))
        self.fdt = sb("fdt", (H, C))
        self.sc16c = sb("sc16c", (128, H))
        self.lcc = sb("lcc", (H, NT))
        self.lcct = sb("lcct", (NT, H))

    def set_qscr(self, width, dual=False):
        arF, arB = self.arF, self.arB

        def mk():
            return (arF.f32(128, width // 16), arF.f32(128, width // 16),
                    arF.f32(128, width), arB.bf16(128, width),
                    arB.bf16(128, width), arB.bf16(128, width))
        self.qscr_by_eng = {"v": mk()}
        self.qscr_by_eng["g"] = mk() if dual else self.qscr_by_eng["v"]

    def prologue(self):
        S, io, sb, pb = self.S, self.io, self.sb, self.pb
        arF, arB = self.arF, self.arB
        c_sb = arF.f32(B, D)
        mod_sb = self.arB.f32(B, 6 * D)
        sc_sb = arB.bf16(B, D)
        sct_sb = arB.bf16(128, 2 * NKC)
        wm_sb = [arB.bf16(128, 3 * D) for _ in range(2)]
        lc0 = S.dma("s", lambda e: e.dma_start(out=c_sb, in_=io["c_in"]))
        tsc = S.op("a", lambda e: e.activation(out=sc_sb, in_=c_sb,
                   func=AF.Silu), deps=[lc0])
        ps_t = pb[7][:, 0:8].bitcast(BF16)
        prev = None
        for k in range(NKC):
            tk = S.op("t", (lambda k_: lambda e: e.transpose(
                ps_t[:, 0:2], sc_sb[:, 128 * k_:128 * (k_ + 1)],
                self.ident_bf[0:2, 0:2]))(k),
                deps=[tsc, self.t_idb, self.t_pbz[7]] +
                     ([prev] if prev else []))
            prev = S.op("v", (lambda k_: lambda e: e.tensor_copy(
                out=sct_sb[:, 2 * k_:2 * k_ + 2], in_=ps_t[:, 0:2]))(k),
                deps=[tk])
        sct_tok = prev
        wm_tok = [None, None]
        mod_evacs = []
        for half in range(2):
            mm_last = [None] * 6
            prev_evacs = list(mod_evacs)
            for k in range(NKC):
                buf = k % 2
                lw = S.gdma((lambda k_, h_, b_: lambda e: e.dma_start(
                    out=wm_sb[b_],
                    in_=io["wmod"][128 * k_:128 * (k_ + 1),
                                   3072 * h_:3072 * (h_ + 1)]))(k, half, buf),
                    deps=[wm_tok[buf]] if wm_tok[buf] else [])
                for n in range(6):
                    deps = [lw, sct_tok] + (
                        prev_evacs + [self.t_pbz[n]] if k == 0 else [])
                    if mm_last[n]:
                        deps.append(mm_last[n])
                    mm_last[n] = S.op("t", (lambda k_, n_, b_: lambda e:
                        e.matmul(pb[n_][0:B, :],
                                 sct_sb[:, 2 * k_:2 * k_ + 2],
                                 wm_sb[b_][:, 512 * n_:512 * (n_ + 1)],
                                 start=(k_ == 0), stop=(k_ == NKC - 1)))(k, n, buf),
                        deps=deps)
                wm_tok[buf] = mm_last[5]
            mod_evacs = []
            for n in range(6):
                mod_evacs.append(S.op("v", (lambda h_, n_: lambda e:
                    e.tensor_copy(
                        out=mod_sb[:, 3072 * h_ + 512 * n_:
                                   3072 * h_ + 512 * (n_ + 1)],
                        in_=pb[n_][0:B, :]))(half, n), deps=[mm_last[n]]))
        names = ["sa", "ca", "ga", "sb2", "cb", "gb"]
        rep_toks = {}
        last = None
        for i, nm in enumerate(names):
            for n in range(2):
                mt = S.op("t", (lambda i_, n_: lambda e: e.matmul(
                    pb[6][:, 0:512], self.selF_sb,
                    mod_sb[:, D * i_ + 512 * n_:D * i_ + 512 * (n_ + 1)],
                    start=True, stop=True))(i, n),
                    deps=mod_evacs + [self.pt["sel"]] + ([last] if last else []))
                last = S.op("v", (lambda nm_, n_: lambda e: e.tensor_copy(
                    out=self.rep[nm_][:, 512 * n_:512 * (n_ + 1)],
                    in_=pb[6][:, 0:512]))(nm, n), deps=[mt])
            rep_toks[nm] = last
        self.t_ca1 = S.op("v", lambda e: e.tensor_scalar_add(
            out=self.rep["ca"], in0=self.rep["ca"], scalar1=1.0),
            deps=[rep_toks["ca"]])
        self.t_cb1 = S.op("v", lambda e: e.tensor_scalar_add(
            out=self.rep["cb"], in0=self.rep["cb"], scalar1=1.0),
            deps=[rep_toks["cb"]])
        sgn = arF.f32(128, D)
        gt = {}
        prev_g = None
        for nm in ("ga", "gb"):
            t1 = S.op("v", (lambda nm_: lambda e: e.tensor_scalar(
                out=sgn, in0=self.rep[nm_], scalar1=0.0, scalar2=2.0,
                op0=ALU.is_ge, op1=ALU.mult))(nm),
                deps=[rep_toks[nm]] + ([prev_g] if prev_g else []))
            t2 = S.op("v", lambda e: e.tensor_scalar_add(
                out=sgn, in0=sgn, scalar1=-1.0), deps=[t1])
            t3 = S.op("v", (lambda nm_: lambda e: e.tensor_mul(
                out=self.rep[nm_], in0=self.rep[nm_], in1=sgn))(nm),
                deps=[rep_toks[nm], t2])
            t4 = S.op("v", (lambda nm_: lambda e: e.tensor_scalar_max(
                out=self.rep[nm_], in0=self.rep[nm_], scalar1=0.25))(nm),
                deps=[t3])
            gt[nm] = S.op("v", (lambda nm_: lambda e: e.tensor_mul(
                out=self.rep[nm_], in0=self.rep[nm_], in1=sgn))(nm),
                deps=[t4])
            prev_g = gt[nm]
        self.t_ga = gt["ga"]
        self.t_gb = gt["gb"]

    def weights_attn(self):
        S, io = self.S, self.io
        wv = io["wall_dram"]
        self.w_wall = []
        for part in range(4):
            self.w_wall.append(S.gdma((lambda p_: lambda e: e.dma_start(
                out=wv[:, 768 * p_:768 * (p_ + 1), :]
                    .rearrange("c n k -> n c k"),
                in_=io["wqkv"][768 * p_:768 * (p_ + 1), :]))(part)))
        self.w_wall.append(S.gdma(lambda e: e.dma_start(
            out=wv[:, 3072:3088, :].rearrange("c n k -> n c k"),
            in_=io["wg"])))
        self.w_wall.append(S.gdma(lambda e: e.dma_start(
            out=wv[:, 3088:3104, :].rearrange("c n k -> n c k"),
            in_=io["wb"])))
        self.w_wout = []
        for part in range(2):
            self.w_wout.append(S.gdma((lambda p_: lambda e: e.dma_start(
                out=io["wout_dram"][:, 512 * p_:512 * (p_ + 1), :]
                    .rearrange("c n k -> n c k"),
                in_=io["wout"][512 * p_:512 * (p_ + 1), :]))(part)))

    def quant_seq(self, src, dst, toks, eng="v"):
        S = self.S
        p, n = src.shape[0], src.shape[1]
        nb = n // 16
        am0, r20, xn0, w10, w20, w30 = self.qscr_by_eng[eng]
        am, r2 = am0[0:p, 0:nb], r20[0:p, 0:nb]
        xn, w1, w2, w3 = (xn0[0:p, 0:n], w10[0:p, 0:n], w20[0:p, 0:n],
                          w30[0:p, 0:n])
        t0 = S.op("v", lambda e: e.tensor_reduce(
            out=am, in_=src.rearrange("p (a b) -> p a b", b=16),
            axis=AX.X, op=ALU.max, apply_absolute_value=True), deps=toks)
        t1 = S.op("v", lambda e: e.tensor_scalar(
            out=am, in0=am, scalar1=1.0 / 12.0, scalar2=0.5e-12,
            op0=ALU.mult, op1=ALU.max), deps=[t0])
        t2 = S.op("v", lambda e: e.reciprocal(out=r2, in_=am), deps=[t1])
        t3 = S.op("g", lambda e: e.tensor_tensor(
            out=xn.rearrange("p (a b) -> p a b", b=16),
            in0=src.rearrange("p (a b) -> p a b", b=16),
            in1=r2.unsqueeze(2).broadcast_to([p, nb, 16]), op=ALU.mult),
            deps=[t2] + list(toks))
        t4 = S.op(eng, lambda e: e.tensor_scalar(
            out=w1, in0=xn, scalar1=MAGIC, scalar2=MAGIC,
            op0=ALU.add, op1=ALU.subtract), deps=[t3])
        t5 = S.op(eng, lambda e: e.tensor_scalar(
            out=w1, in0=w1, scalar1=4.0, scalar2=-4.0,
            op0=ALU.min, op1=ALU.max), deps=[t4])
        t6 = S.op(eng, lambda e: e.tensor_scalar(
            out=w2, in0=xn, scalar1=10.0, scalar2=2.0,
            op0=ALU.is_gt, op1=ALU.mult), deps=[t3])
        t7 = S.op(eng, lambda e: e.scalar_tensor_tensor(
            out=w2, in0=xn, scalar=7.0, in1=w2, op0=ALU.is_gt, op1=ALU.add),
            deps=[t6])
        t8 = S.op(eng, lambda e: e.scalar_tensor_tensor(
            out=w2, in0=xn, scalar=5.0, in1=w2, op0=ALU.is_gt, op1=ALU.add),
            deps=[t7])
        t9 = S.op(eng, lambda e: e.tensor_scalar(
            out=w3, in0=xn, scalar1=-10.0, scalar2=2.0,
            op0=ALU.is_lt, op1=ALU.mult), deps=[t3])
        t10 = S.op(eng, lambda e: e.scalar_tensor_tensor(
            out=w3, in0=xn, scalar=-7.0, in1=w3, op0=ALU.is_lt, op1=ALU.add),
            deps=[t9])
        t11 = S.op(eng, lambda e: e.scalar_tensor_tensor(
            out=w3, in0=xn, scalar=-5.0, in1=w3, op0=ALU.is_lt, op1=ALU.add),
            deps=[t10])
        tq = S.op("g", lambda e: e.tensor_tensor(out=w2, in0=w2, in1=w3,
                  op=ALU.subtract), deps=[t8, t11])
        tq2 = S.op(eng, lambda e: e.scalar_tensor_tensor(
            out=w2, in0=w2, scalar=2.0, in1=w1, op0=ALU.mult, op1=ALU.add),
            deps=[tq, t5])
        tf = S.op("g", lambda e: e.tensor_tensor(
            out=dst.rearrange("p (a b) -> p a b", b=16),
            in0=w2.rearrange("p (a b) -> p a b", b=16),
            in1=am.unsqueeze(2).broadcast_to([p, nb, 16]), op=ALU.mult),
            deps=[tq2, t1])
        self._q_src = [t0, t3]
        return tf

    def ffn_weight_quant(self):
        S, io, pb = self.S, self.io, self.pb
        arB = self.arB
        self.set_qscr(IF // NCORE)
        gu_out = arB.bf16(128, 2 * IF)
        src_t = self.arF.f32(128, IF // NCORE)
        dq_t = arB.bf16(128, IF // NCORE)
        wdT_a = arB.bf16(128, D)
        wdT_b = arB.bf16(128, D)
        wdT_c = arB.bf16(96, D)
        ps = pb[7][:, 256:320].bitcast(BF16)
        toks1 = []
        prev_q = None
        ntile = IF // 128
        for which, wsrc in (("g", io["wgate_sh"]), ("u", io["wup_sh"])):
            for r in range(ntile):
                lt = S.dma("s", (lambda w_, r_: lambda e: e.dma_start(
                    out=src_t[:, 0:128],
                    in_=w_[128 * r_:128 * (r_ + 1), :]))(wsrc, r),
                    deps=[prev_q] if prev_q else [])
                qt = self.quant_seq(src_t[:, 0:128], dq_t[:, 0:128], [lt])
                off = (0 if which == "g" else IF) + 128 * r
                tt = S.op("t", (lambda: lambda e: e.transpose(
                    ps, dq_t[:, 0:128], self.ident_bf))(),
                    deps=[qt, self.t_idb, self.t_pbz[7]] +
                         ([prev_q] if prev_q else []))
                prev_q = S.op("v", (lambda off_: lambda e: e.tensor_copy(
                    out=gu_out[:, off_:off_ + 128], in_=ps))(off), deps=[tt])
                toks1.append(prev_q)
        st1 = S.dma("s", lambda e: e.dma_start(out=io["agw_in"], in_=gu_out),
                    deps=toks1)
        self.ag1 = S.cop(lambda e: e.collective_compute(
            "AllGather", ALU.bypass, replica_groups=[list(range(NCORE))],
            ins=[io["agw_in"].opt()], outs=[io["agw_out"].opt()]), deps=[st1])

        toks2 = []
        prev_q = None
        for r in range(NKC):
            lt = S.dma("s", (lambda r_: lambda e: e.dma_start(
                out=src_t, in_=io["wdown_sh"][128 * r_:128 * (r_ + 1), :]))(r),
                deps=[prev_q] if prev_q else [])
            qt = self.quant_seq(src_t, dq_t, [lt])
            for dst_t, c0, cw in ((wdT_a, 0, 128), (wdT_b, 128, 128),
                                  (wdT_c, 256, 96)):
                tt = S.op("t", (lambda c0_, cw_: lambda e: e.transpose(
                    ps[0:cw_, :], dq_t[:, c0_:c0_ + cw_],
                    self.ident_bf))(c0, cw),
                    deps=[qt, self.t_idb] + ([prev_q] if prev_q else []))
                prev_q = S.op("v", (lambda d_, r_, cw_: lambda e: e.tensor_copy(
                    out=d_[0:cw_, 128 * r_:128 * (r_ + 1)],
                    in_=ps[0:cw_, 0:128]))(dst_t, r, cw), deps=[tt])
                toks2.append(prev_q)
        st2 = [
            S.dma("s", lambda e: e.dma_start(out=io["agd_in"][0:128, :],
                  in_=wdT_a), deps=toks2),
            S.dma("s", lambda e: e.dma_start(out=io["agd_in"][128:256, :],
                  in_=wdT_b), deps=toks2),
            S.dma("s", lambda e: e.dma_start(out=io["agd_in"][256:352, :],
                  in_=wdT_c), deps=toks2),
        ]
        self.ag2 = S.cop(lambda e: e.collective_compute(
            "AllGather", ALU.bypass, replica_groups=[list(range(NCORE))],
            ins=[io["agd_in"].opt()], outs=[io["agd_out"].opt()]),
            deps=st2 + [self.ag1])

    def ln_mod(self, x_t, ca1, sa, out_t, deps):
        S = self.S
        m, rs, t1, t2, xs = (self.ln_m, self.ln_r, self.ln_t1, self.ln_t2,
                             self.ln_xs)
        ta = S.op("a", lambda e: e.activation(out=xs, in_=x_t, func=AF.Square),
                  deps=deps)
        t_m = S.op("v", lambda e: e.reduce_sum(out=m, in_=x_t, axis=AX.X),
                   deps=deps)
        t_q = S.op("v", lambda e: e.reduce_sum(out=t1, in_=xs, axis=AX.X),
                   deps=[ta])
        t_m2 = S.op("v", lambda e: e.tensor_scalar_mul(out=m, in0=m,
                    scalar1=1.0 / D), deps=[t_m])
        t_q2 = S.op("v", lambda e: e.tensor_scalar_mul(out=t1, in0=t1,
                    scalar1=1.0 / D), deps=[t_q])
        t_v = S.op("v", lambda e: e.tensor_mul(out=t2, in0=m, in1=m),
                   deps=[t_m2])
        t_v2 = S.op("v", lambda e: e.tensor_tensor(out=t1, in0=t1, in1=t2,
                    op=ALU.subtract), deps=[t_v, t_q2])
        t_v3 = S.op("v", lambda e: e.tensor_scalar_add(out=t1, in0=t1,
                    scalar1=1e-5), deps=[t_v2])
        t_sq = S.op("a", lambda e: e.activation(out=t2, in_=t1, func=AF.Sqrt),
                    deps=[t_v3])
        t_r = S.op("v", lambda e: e.reciprocal(out=rs, in_=t2), deps=[t_sq])
        t_n = S.op("v", lambda e: e.tensor_scalar(
            out=xs, in0=x_t, scalar1=m, scalar2=rs,
            op0=ALU.subtract, op1=ALU.mult), deps=[t_r, t_m2] + list(deps))
        t_h1 = S.op("v", lambda e: e.tensor_tensor(out=xs, in0=xs, in1=ca1,
                    op=ALU.mult), deps=[t_n, self.t_ca1, self.t_cb1])
        t_h2 = S.op("v", lambda e: e.tensor_tensor(out=out_t, in0=xs, in1=sa,
                    op=ALU.add), deps=[t_h1])
        return t_h2

    def hphase(self):
        S, io = self.S, self.io
        self.h_toks = []
        prev_store = None
        for i in range(NT):
            lx = S.dma("s", (lambda i_: lambda e: e.dma_start(
                out=self.x_t, in_=io["x_in"][128 * i_:128 * (i_ + 1), :]))(i),
                deps=[prev_store] if prev_store else [])
            th = self.ln_mod(self.x_t, self.rep["ca"], self.rep["sa"],
                             self.h_bf, [lx])
            prev_store = S.dma("s", (lambda i_: lambda e: e.dma_start(
                out=io["h_dram"][:, 128 * i_:128 * (i_ + 1), :]
                    .rearrange("c p k -> p c k"),
                in_=self.h_bf))(i), deps=[th])
            self.h_toks.append(prev_store)

    def qkvphase(self):
        S, io, pb = self.S, self.io, self.pb
        arB, arF = self.arB, self.arF
        wallT = [arB.bf16(128, 3104) for _ in range(NKC)]
        hT = [[arB.bf16(128, 512) for _ in range(NKC)] for _ in range(2)]
        var_bf = arB.bf16(128, D)
        q_raw = arF.f32(128, D)
        k_raw = arF.f32(128, D)
        sqs = arF.f32(128, D)
        ssq = arF.f32(128, H)
        ssk = arF.f32(128, H)
        zab = arF.f32(128, 2 * H)
        sab = arF.f32(128, 2 * H)
        la_t = arF.f32(128, H)
        laT = arF.f32(H, C)
        sc16 = arF.f32(128, H)
        wl = []
        for k in range(NKC):
            wl.append(S.dma("s", (lambda k_, d_: lambda e: e.dma_start(
                out=d_, in_=io["wall_dram"][k_], transpose=True))(k, wallT[k]),
                deps=self.w_wall))
        hl_tok = [[None] * NKC, [None] * NKC]
        store_toks = []
        evac_toks = []
        prev_lt = None
        prev_lt2 = None
        for i in range(NT):
            qtr, buf = i // 4, (i // 4) % 2
            if i % 4 == 0:
                for k in range(NKC):
                    hl_tok[buf][k] = S.dma("s", (lambda k_, d_, q_: lambda e:
                        e.dma_start(out=d_,
                                    in_=io["h_dram"][k_][512 * q_:512 * (q_ + 1), :],
                                    transpose=True))(k, hT[buf][k], qtr),
                        deps=[self.h_toks[4 * qtr + 3]] + store_toks +
                             ([hl_tok[buf][k]] if hl_tok[buf][k] else []))
            io_i = 128 * (i % 4)
            mm_ab = None
            mm_n = [None] * 6
            for k in range(NKC):
                for n in range(6):
                    deps = [wl[k], hl_tok[buf][k]]
                    if k == 0:
                        deps += evac_toks
                    if mm_n[n]:
                        deps.append(mm_n[n])
                    mm_n[n] = S.op("t", (lambda k_, n_, b2, o_: lambda e:
                        e.matmul(pb[n_], hT[b2][k_][:, o_:o_ + 128],
                                 wallT[k_][:, 512 * n_:512 * (n_ + 1)],
                                 start=(k_ == 0), stop=(k_ == NKC - 1)))(k, n, buf, io_i),
                        deps=deps)
                mm_ab = S.op("t", (lambda k_, b2, o_: lambda e: e.matmul(
                    pb[6][:, 0:32], hT[b2][k_][:, o_:o_ + 128],
                    wallT[k_][:, 3072:3104],
                    start=(k_ == 0), stop=(k_ == NKC - 1)))(k, buf, io_i),
                    deps=[wl[k], hl_tok[buf][k]] +
                         ([mm_ab] if mm_ab else evac_toks))
            tq0 = S.op("a", lambda e: e.copy(out=q_raw[:, 0:512],
                       in_=pb[0]), deps=[mm_n[0]])
            tq1 = S.op("a", lambda e: e.copy(out=q_raw[:, 512:1024],
                       in_=pb[1]), deps=[mm_n[1]])
            tk0 = S.op("a", lambda e: e.copy(out=k_raw[:, 0:512],
                       in_=pb[2]), deps=[mm_n[2]])
            tk1 = S.op("a", lambda e: e.copy(out=k_raw[:, 512:1024],
                       in_=pb[3]), deps=[mm_n[3]])
            tz = S.op("v", lambda e: e.tensor_tensor(
                out=zab, in0=self.bgbb_rep, in1=pb[6][:, 0:32], op=ALU.add),
                deps=[mm_ab, self.pt["bgr"], self.pt["bbr"]])
            ts_ = S.op("a", lambda e: e.activation(out=sab, in_=zab,
                       func=AF.Sigmoid), deps=[tz])
            tb = S.op("v", (lambda i_: lambda e: e.tensor_copy(
                out=self.b_sig[i_], in_=sab[:, H:2 * H]))(i), deps=[ts_])
            tcl = S.op("v", lambda e: e.tensor_scalar_max(
                out=la_t, in0=sab[:, 0:H], scalar1=1e-38), deps=[ts_])
            tla = S.op("a", lambda e: e.activation(out=la_t, in_=la_t,
                       func=AF.Ln), deps=[tcl])
            ttr = S.op("t", (lambda: lambda e: e.transpose(
                pb[7][0:H, 0:128], la_t, self.ident_sb))(),
                deps=[tla, self.pt["id"]] + ([prev_lt] if prev_lt else [])
                     + ([prev_lt2] if prev_lt2 else []))
            tev = S.op("v", lambda e: e.tensor_copy(out=laT,
                       in_=pb[7][0:H, 0:128]), deps=[ttr])
            tsc_ = S.op("v", (lambda i_: lambda e: e.tensor_tensor_scan(
                out=self.L_all[:, C * i_:C * (i_ + 1)], data0=laT,
                data1=self.zeroHC, initial=0.0, op0=ALU.add, op1=ALU.add))(i),
                deps=[tev, self.t_zero])
            tlt = S.op("t", (lambda i_: lambda e: e.transpose(
                pb[7][:, 128:128 + H], self.L_all[:, C * i_:C * (i_ + 1)],
                self.ident_sb[0:H, 0:H]))(i),
                deps=[tsc_] + ([prev_lt2] if prev_lt2 else []))
            tltv = S.op("v", (lambda i_: lambda e: e.tensor_copy(
                out=self.LT[i_], in_=pb[7][:, 128:128 + H]))(i), deps=[tlt])
            texp = S.op("a", (lambda i_: lambda e: e.activation(
                out=self.expL[i_], in_=self.LT[i_], func=AF.Exp))(i),
                deps=[tltv])
            prev_lt = tev
            prev_lt2 = tltv
            tsq = S.op("a", lambda e: e.activation(out=sqs, in_=q_raw,
                       func=AF.Square), deps=[tq0, tq1])
            trq = S.op("v", lambda e: e.tensor_reduce(
                out=ssq, in_=sqs.rearrange("p (h d) -> p h d", d=DH),
                axis=AX.X, op=ALU.add), deps=[tsq])
            tsk = S.op("a", lambda e: e.activation(out=sqs, in_=k_raw,
                       func=AF.Square), deps=[tk0, tk1, trq])
            trk = S.op("v", lambda e: e.tensor_reduce(
                out=ssk, in_=sqs.rearrange("p (h d) -> p h d", d=DH),
                axis=AX.X, op=ALU.add), deps=[tsk])
            tq2 = S.op("a", lambda e: e.activation(out=ssq, in_=ssq,
                       func=AF.Sqrt), deps=[trq])
            tq3 = S.op("v", lambda e: e.tensor_scalar_max(out=ssq, in0=ssq,
                       scalar1=1e-12), deps=[tq2])
            tq4 = S.op("v", (lambda i_: lambda e: e.reciprocal(
                out=self.rnq_t[i_], in_=ssq))(i), deps=[tq3])
            tk2 = S.op("a", lambda e: e.activation(out=ssk, in_=ssk,
                       func=AF.Sqrt), deps=[trk])
            tk3 = S.op("v", lambda e: e.tensor_scalar_max(out=ssk, in0=ssk,
                       scalar1=1e-12), deps=[tk2])
            tk4 = S.op("v", (lambda i_: lambda e: e.reciprocal(
                out=self.rnk_t[i_], in_=ssk))(i), deps=[tk3])

            def bc16(ap):
                return ap.unsqueeze(2).broadcast_to([128, H, DH])

            def vstore(dst, tok, i_=i):
                return S.dma("s", (lambda d_: lambda e: e.dma_start(
                    out=d_[:, 128 * i_:128 * (i_ + 1), :]
                        .rearrange("c p k -> p c k"),
                    in_=var_bf))(dst), deps=[tok])

            tv1 = S.op("v", (lambda i_: lambda e: e.tensor_tensor(
                out=var_bf.rearrange("p (h d) -> p h d", d=DH),
                in0=k_raw.rearrange("p (h d) -> p h d", d=DH),
                in1=bc16(self.rnk_t[i_]), op=ALU.mult))(i),
                deps=[tk4] + store_toks[4:5])
            s1 = vstore(io["kpl_dram"], tv1)
            tv2 = S.op("v", (lambda i_: lambda e: e.tensor_tensor(
                out=var_bf.rearrange("p (h d) -> p h d", d=DH),
                in0=var_bf.rearrange("p (h d) -> p h d", d=DH),
                in1=bc16(self.b_sig[i_]), op=ALU.mult))(i), deps=[tv1, tb, s1])
            s2 = vstore(io["kb_dram"], tv2)
            tv3 = S.op("v", (lambda i_: lambda e: e.tensor_tensor(
                out=var_bf.rearrange("p (h d) -> p h d", d=DH),
                in0=q_raw.rearrange("p (h d) -> p h d", d=DH),
                in1=bc16(self.rnq_t[i_]), op=ALU.mult))(i), deps=[tq4, s2])
            s3 = vstore(io["q_dram"], tv3)
            tg = S.op("v", (lambda i_: lambda e: e.tensor_tensor(
                out=sc16, in0=self.rnq_t[i_], in1=self.expL[i_],
                op=ALU.mult))(i), deps=[tq4, texp])
            tv4 = S.op("v", (lambda i_: lambda e: e.tensor_tensor(
                out=var_bf.rearrange("p (h d) -> p h d", d=DH),
                in0=q_raw.rearrange("p (h d) -> p h d", d=DH),
                in1=bc16(sc16), op=ALU.mult))(i), deps=[tg, s3])
            s4 = vstore(io["qg_dram"], tv4)
            tv5a = S.op("v", (lambda i_: lambda e: e.tensor_tensor(
                out=var_bf[:, 0:512].rearrange("p (h d) -> p h d", d=DH),
                in0=pb[4].rearrange("p (h d) -> p h d", d=DH),
                in1=self.b_sig[i_][:, 0:8].unsqueeze(2)
                    .broadcast_to([128, 8, DH]), op=ALU.mult))(i),
                deps=[mm_n[4], tb, s4])
            tv5b = S.op("v", (lambda i_: lambda e: e.tensor_tensor(
                out=var_bf[:, 512:1024].rearrange("p (h d) -> p h d", d=DH),
                in0=pb[5].rearrange("p (h d) -> p h d", d=DH),
                in1=self.b_sig[i_][:, 8:16].unsqueeze(2)
                    .broadcast_to([128, 8, DH]), op=ALU.mult))(i),
                deps=[mm_n[5], tb, tv5a])
            s5 = vstore(io["bv_dram"], tv5b)
            store_toks = [s1, s2, s3, s4, s5, tq0, tq1, tk0, tk1, tz,
                          tv5a, tv5b]
            evac_toks = [tq0, tq1, tk0, tk1, tz, tv5a, tv5b]
        self.qkv_last = list(store_toks)

        lst = []
        vtok = S.etok("v")
        for i in range(NT):
            lst.append(S.dma("s", (lambda i_: lambda e: e.dma_start(
                out=io["l_dram"][i_:i_ + 1, :]
                    .rearrange("x (h c) -> (x h) c", h=H),
                in_=self.L_all[:, C * i_:C * (i_ + 1)]))(i), deps=[vtok]))
        self.l_store = lst
        t1 = S.op("v", lambda e: e.tensor_copy(
            out=self.lcc, in_=self.L_all[:, C - 1::C]), deps=[vtok])
        t2 = S.op("t", lambda e: e.transpose(pb[7][0:NT, 256:256 + H],
                  self.lcc, self.ident_sb[0:H, 0:H]), deps=[t1, prev_lt2])
        t3 = S.op("v", lambda e: e.tensor_copy(out=self.lcct,
                  in_=pb[7][0:NT, 256:256 + H]), deps=[t2])
        t4 = S.dma("s", lambda e: e.dma_start(
            out=io["lc_dram"].rearrange("x (c h) -> (x c) h", h=H),
            in_=self.lcct), deps=[t3])
        t5 = S.dma("s", lambda e: e.dma_start(
            out=self.gcrep, in_=io["lc_dram"].partition_broadcast(128)),
            deps=[t4])
        self.t_gc = S.op("a", lambda e: e.activation(
            out=self.gcrep, in_=self.gcrep, func=AF.Exp), deps=[t5])
        gv = self.gcrep.rearrange("r (c p two) -> r c p two", p=8, two=2)
        gpv = self.gcp.rearrange("r (c p) -> r c p", p=8)
        tg1 = S.op("v", lambda e: e.tensor_copy(
            out=gpv[0:64], in_=gv[0:64, :, :, 0]), deps=[self.t_gc])
        self.t_gcp = S.op("v", lambda e: e.tensor_copy(
            out=gpv[64:128], in_=gv[64:128, :, :, 1]), deps=[tg1])

    def chunkphase(self):
        S, io, pb = self.S, self.io, self.pb
        arB, arF = self.arB, self.arF
        TQW = 256
        kT = [arB.bf16(128, TQW) for _ in range(NKC)]
        kbT = [arB.bf16(128, TQW) for _ in range(NKC)]
        qT = [arB.bf16(128, TQW) for _ in range(NKC)]
        qgT = [arB.bf16(128, TQW) for _ in range(NKC)]
        lrep = [arF.f32(128, H * C) for _ in range(2)]
        ein = arF.f32(128, H * C)
        e_ts = arB.bf16(128, H * C)
        e_sts = arB.bf16(128, H * C)
        t_full = arB.bf16(128, H * C)
        that_full = arB.bf16(128, H * C)
        rz_all = arB.bf16(128, H * 192)
        w_all = arB.bf16(128, H * 128)
        kpl_tm = arB.bf16(128, D)
        pw_ts = [arB.bf16(128, 1024) for _ in range(2)]
        pw_st = [arB.bf16(128, 1024) for _ in range(4)]
        g_all = arB.bf16(128, 1024)
        that_bd = arB.bf16(128, 1024)
        dg_scr = arB.bf16(128, 4 * C)
        z_all = rz_all
        qpt_sb = arB.bf16(128, 4 * C)
        tmp_odd = arB.bf16(128, 4 * C)
        bc_sb = arB.bf16(128, 4 * DH)
        bct_sb = arB.bf16(128, 4 * DH)
        mcs = arB.bf16(128, 4 * DH)
        self.chunk_tiles = dict(rz=rz_all, w=w_all, z=z_all, qpt=qpt_sb,
                                tmp=tmp_odd, bc=bc_sb, bct=bct_sb, mcs=mcs)

        rzv = rz_all.rearrange("p (h x) -> p h x", x=192)
        wav = w_all.rearrange("p (h x) -> p h x", x=128)

        zt = [self.t_i64b]
        for p in range(H // 2):
            zt.append(S.op("v", (lambda p_: lambda e: e.memset(
                self.stloc[p_], 0.0))(p)))
            zt.append(S.op("v", (lambda p_: lambda e: e.tensor_copy(
                out=self.prun[p_], in_=self.i64two_bf))(p), deps=[self.t_i64b]))
            zt.append(S.op("v", (lambda p_: lambda e: e.tensor_copy(
                out=self.prunT[p_], in_=self.i64two_bf))(p), deps=[self.t_i64b]))
        state_tok = zt

        ld_toks = []
        lr_tok = [None, None]
        prev_chunk_done = []
        prev_prods = []

        def emit_E(c, pre):
            pp = c % 2
            tl = S.dma("s", (lambda c_, p_: lambda e: e.dma_start(
                out=lrep[p_], in_=io["l_dram"][c_:c_ + 1, :]
                .partition_broadcast(128)))(c, pp),
                deps=[self.l_store[c]] + ([lr_tok[pp]] if lr_tok[pp] else []))
            emask = [(self.mask_ts_sb, e_ts, -1.0),
                     (self.mask_sts_sb, e_sts, 1.0)]
            ex_toks = []
            prev_ex = list(pre)
            last_ein = None
            for mask, edst, scl in emask:
                ein_toks = []
                for h in range(H):
                    ein_toks.append(S.op("v", (lambda c_, h_, p_, m_: lambda e:
                        e.scalar_tensor_tensor(
                            out=ein[:, C * h_:C * (h_ + 1)],
                            in0=lrep[p_][:, C * h_:C * (h_ + 1)],
                            scalar=self.LT[c_][:, h_:h_ + 1],
                            in1=m_, op0=ALU.subtract, op1=ALU.add))(c, h, pp, mask),
                        deps=[tl, self.pt["mts"], self.pt["msts"],
                              self.pt["msti"]] + prev_ex))
                ex = S.op("a", (lambda ed_, sc_: lambda e: e.activation(
                    out=ed_, in_=ein, func=AF.Exp, scale=sc_))(edst, scl),
                    deps=ein_toks)
                prev_ex = [ex]
                ex_toks.append(ex)
                last_ein = ein_toks[-1]
            lr_tok[pp] = last_ein
            return ex_toks

        ex_cache = {0: emit_E(0, [])}
        for c in range(NT):
            grp, cg = c // 2, c % 2
            if cg == 0:
                ld_toks = []
                for kk in range(NKC):
                    for dst, src in ((kT, "kpl_dram"), (kbT, "kb_dram"),
                                     (qT, "q_dram"), (qgT, "qg_dram")):
                        ld_toks.append(S.dma("s", (lambda kk_, d_, s_, g_:
                            lambda e: e.dma_start(
                                out=d_,
                                in_=io[s_][kk_][TQW * g_:TQW * (g_ + 1), :],
                                transpose=True))(kk, dst[kk], src, grp),
                            deps=self.qkv_last + prev_chunk_done))
            lkp = S.dma("s", (lambda c_: lambda e: e.dma_start(
                out=kpl_tm, in_=io["kpl_dram"][:, 128 * c_:128 * (c_ + 1), :]
                .rearrange("k p x -> p k x")))(c),
                deps=self.qkv_last + prev_chunk_done)
            wav4 = w_all.rearrange("p (hp two x) -> p hp two x", two=2,
                                   x=128)
            lbv = []
            for two in range(2):
                lbv.append(S.dma("s", (lambda c_, t2: lambda e: e.dma_start(
                    out=wav4[:, :, t2, 0:64],
                    in_=io["bv_dram"][:, 128 * c_:128 * (c_ + 1),
                                      64 * t2:64 * (t2 + 1)]
                    .rearrange("k p d -> p k d")))(c, two),
                    deps=self.qkv_last + prev_chunk_done))
            ex1, ex2 = ex_cache.pop(c)

            tfd = S.op("a", (lambda c_: lambda e: e.activation(
                out=self.fdt, in_=self.L_all[:, C * c_:C * (c_ + 1)],
                func=AF.Exp, scale=-1.0,
                bias=self.L_all[:, C * (c_ + 1) - 1:C * (c_ + 1)]))(c),
                deps=[self.l_store[c]] + prev_chunk_done)
            ttd = S.op("t", lambda e: e.transpose(
                pb[7][:, 256:256 + H], self.fdt, self.ident_sb[0:H, 0:H]),
                deps=[tfd] + prev_chunk_done)
            tfdv = S.op("v", lambda e: e.tensor_copy(out=self.sc16c,
                        in_=pb[7][:, 256:256 + H]), deps=[ttd])
            tkd = S.op("v", lambda e: e.tensor_tensor(
                out=rzv[:, :, 128:192],
                in0=kpl_tm.rearrange("p (h d) -> p h d", d=DH),
                in1=self.sc16c.unsqueeze(2).broadcast_to([128, H, DH]),
                op=ALU.mult), deps=[tfdv, lkp] + prev_chunk_done)
            tg2 = S.op("v", (lambda c_: lambda e: e.tensor_tensor(
                out=self.sc16c, in0=self.b_sig[c_], in1=self.expL[c_],
                op=ALU.mult))(c), deps=[tkd])
            tkbg = S.op("v", lambda e: e.tensor_tensor(
                out=wav[:, :, 64:128],
                in0=kpl_tm.rearrange("p (h d) -> p h d", d=DH),
                in1=self.sc16c.unsqueeze(2).broadcast_to([128, H, DH]),
                op=ALU.mult), deps=[tg2, lkp] + prev_chunk_done)

            prod_toks = []
            prev_fix = []
            ev4 = lambda ap: ap.rearrange("p (g two c) -> p g two c",
                                          two=2, c=C)
            rz4 = rz_all.rearrange("p (g two x) -> p g two x", two=2, x=192)
            pbv = [bk.rearrange("p (k one c) -> p k one c", one=1, c=C)
                   for bk in pb]
            for wv in range(2):
                mmt = []
                for hh in range(8):
                    h = 8 * wv + hh
                    r0 = 64 * (h % 2)
                    csl = slice(128 * cg, 128 * (cg + 1))
                    kt_sl = kT[h // 2][r0:r0 + 64, csl]
                    kbt_sl = kbT[h // 2][r0:r0 + 64, csl]
                    qt_sl = qT[h // 2][r0:r0 + 64, csl]
                    col = 128 * (hh // 2)
                    bA = pb[hh % 2]
                    bB = pb[2 + hh % 2]
                    bC = pb[4 + hh % 2]
                    dps = ld_toks + (list(prod_toks) if prod_toks else
                                     list(prev_chunk_done))
                    mmt.append(S.op("t", (lambda a, b_, o, co: lambda e:
                        e.matmul(o[:, co:co + 128], a, b_, start=True,
                                 stop=True))(kt_sl, kbt_sl, bA, col), deps=dps))
                    mmt.append(S.op("t", (lambda a, b_, o, co: lambda e:
                        e.matmul(o[:, co:co + 128], a, b_, start=True,
                                 stop=True))(kbt_sl, kt_sl, bB, col), deps=dps))
                    mmt.append(S.op("t", (lambda a, b_, o, co: lambda e:
                        e.matmul(o[:, co:co + 128], a, b_, start=True,
                                 stop=True))(kt_sl, qt_sl, bC, col), deps=dps))
                gsl = slice(4 * wv, 4 * wv + 4)
                for par in range(2):
                    psl = slice(par, par + 1)
                    pt1 = S.op("v", (lambda g_, p_, pa: lambda e:
                        e.tensor_tensor(
                            out=ev4(that_full)[:, g_, p_, :],
                            in0=ev4(e_sts)[:, g_, p_, :],
                            in1=pbv[pa][:, :, :, :],
                            op=ALU.mult))(gsl, psl, par),
                        deps=mmt + [ex2])
                    pt2 = S.op("v", (lambda g_, p_, pa: lambda e:
                        e.tensor_tensor(
                            out=ev4(t_full)[:, g_, p_, :],
                            in0=ev4(e_ts)[:, g_, p_, :],
                            in1=pbv[2 + pa][:, :, :, :],
                            op=ALU.mult))(gsl, psl, par),
                        deps=mmt + [ex1])
                    pt3 = S.op("v", (lambda g_, p_, pa: lambda e:
                        e.tensor_tensor(
                            out=rz4[:, g_, p_, 0:128],
                            in0=ev4(e_sts)[:, g_, p_, :],
                            in1=pbv[4 + pa][:, :, :, :],
                            op=ALU.mult))(gsl, psl, par),
                        deps=mmt + [ex2])
                    dg4 = dg_scr.rearrange("p (k one c) -> p k one c",
                                           one=1, c=C)
                    dg = S.op("v", (lambda pa: lambda e: e.tensor_tensor(
                        out=dg4, in0=pbv[4 + pa][:, :, :, :],
                        in1=self.ident_bf.unsqueeze(1).unsqueeze(1)
                            .broadcast_to([128, 4, 1, C]),
                        op=ALU.mult))(par),
                        deps=mmt + [self.t_idb] + prev_fix)
                    fx = S.op("v", (lambda g_, p_: lambda e: e.tensor_tensor(
                        out=rz4[:, g_, p_, 0:128],
                        in0=rz4[:, g_, p_, 0:128],
                        in1=dg4, op=ALU.add))(gsl, psl),
                        deps=[dg, pt3])
                    prev_fix = [fx]
                    prod_toks += [pt1, pt2, pt3, fx]
            if CHUNK_PARTS < 2:
                prev_chunk_done = [self.S.op("v", lambda e: e.engine_nop(),
                                   deps=prod_toks + [tkd, tkbg] + lbv)]
                state_tok = prev_chunk_done
                continue
            prev_prods = list(prod_toks)
            if c + 1 < NT:
                ex_cache[c + 1] = emit_E(c + 1, prev_prods)
            self._cur_c = c
            pre_solve = prod_toks + [tkd]
            td = self.dbgout("tdump", (NT, 128, H * C), BF16)
            rzd = self.dbgout("rzdump", (NT, 128, H * 192), BF16)
            if td is not None:
                pre_solve.append(S.dma("s", (lambda c_: lambda e: e.dma_start(
                    out=td[c_], in_=t_full))(c), deps=prod_toks))
                pre_solve.append(S.dma("s", (lambda c_: lambda e: e.dma_start(
                    out=rzd[c_], in_=rz_all))(c), deps=prod_toks + [tkd]))
            z_toks = self.solve_both(t_full, that_full, rz_all, pw_ts, pw_st,
                                     g_all, that_bd, z_all,
                                     pre_solve)
            zd = self.dbgout("zdump", (NT, 128, H * 192), BF16)
            if zd is not None:
                z_toks = z_toks + [S.dma("s", (lambda c_: lambda e:
                    e.dma_start(out=zd[c_], in_=z_all))(c), deps=z_toks)]
            if CHUNK_PARTS < 3:
                prev_chunk_done = [self.S.op("v", lambda e: e.engine_nop(),
                                   deps=z_toks + [tkbg] + lbv)]
                state_tok = prev_chunk_done
                continue
            prev_chunk_done = self.g2_state(
                c, z_all, w_all, qgT, qpt_sb, tmp_odd, bc_sb, bct_sb, mcs,
                z_toks, lbv + [tkbg], state_tok, cg)
            state_tok = prev_chunk_done

        sts = []
        for p in range(H // 2):
            for par in range(2):
                h = 2 * p + par
                sts.append(S.dma("s", (lambda p_, par_, h_: lambda e:
                    e.dma_start(
                        out=io["ags_in"][:, 2 * DH * h_:2 * DH * h_ + DH],
                        in_=self.prun[p_][64 * par_:64 * par_ + 64, :]))(p, par, h),
                    deps=prev_chunk_done))
                sts.append(S.dma("s", (lambda p_, par_, h_: lambda e:
                    e.dma_start(
                        out=io["ags_in"][:, 2 * DH * h_ + DH:2 * DH * (h_ + 1)],
                        in_=self.stloc[p_][64 * par_:64 * par_ + 64, :]))(p, par, h),
                    deps=prev_chunk_done))
        self.ag3 = S.cop(lambda e: e.collective_compute(
            "AllGather", ALU.bypass, replica_groups=[list(range(NCORE))],
            ins=[io["ags_in"].opt()], outs=[io["ags_out"].opt()]),
            deps=sts + [self.ag2])

    def solve_both(self, t_full, that_full, rz_all, pw_ts, pw_st,
                   g_all, that_bd, z_all, pre_toks):
        S, pb = self.S, self.pb
        rzv = rz_all.rearrange("p (h x) -> p h x", x=192)
        zv = z_all.rearrange("p (h x) -> p h x", x=192)
        wave_tok = []
        for wv in range(2):
            hs = [8 * wv + i for i in range(8)]
            tbd = S.op("v", (lambda w_: lambda e: e.tensor_tensor(
                out=that_bd.rearrange("p (g c) -> p g c", c=C),
                in0=that_full.rearrange("p (g c) -> p g c", c=C)
                    [:, 8 * w_:8 * w_ + 8, :],
                in1=self.bdm_bf.unsqueeze(1).broadcast_to([128, 8, C]),
                op=ALU.mult))(wv),
                deps=list(pre_toks) + [self.t_bdm] + wave_tok)
            sq_dep = list(pre_toks) + wave_tok
            for lv in range(5):
                mms_ts, mms_st = [], []
                for hh in range(8):
                    h = hs[hh]
                    bts = pb[6 + hh // 4]
                    bst = pb[4 + hh // 4]
                    co = 128 * (hh % 4)
                    if lv == 0:
                        for b_ in range(2):
                            r = slice(64 * b_, 64 * b_ + 64)
                            a_ts = t_full[r, C * h + 64 * b_:
                                          C * h + 64 * b_ + 64]
                            a_st = that_full[r, C * h + 64 * b_:
                                             C * h + 64 * b_ + 64]
                            mms_ts.append(S.op("t", (lambda a1, a2, b2, r_,
                                c_: lambda e: e.matmul(
                                    b2[r_, c_:c_ + 64], a1, a2, start=True,
                                    stop=True))(a_st, a_ts, bts, r,
                                                co + 64 * b_), deps=sq_dep))
                            mms_st.append(S.op("t", (lambda a1, a2, b2, r_,
                                c_: lambda e: e.matmul(
                                    b2[r_, c_:c_ + 64], a1, a2, start=True,
                                    stop=True))(a_ts, a_st, bst, r,
                                                co + 64 * b_), deps=sq_dep))
                    else:
                        src_ts = pw_ts[(lv - 1) % 2][:, 128 * hh:128 * hh + 128]
                        src_st = pw_st[lv - 1][:, 128 * hh:128 * hh + 128]
                        mms_ts.append(S.op("t", (lambda a1, a2, b2, c_:
                            lambda e: e.matmul(
                                b2[:, c_:c_ + 128], a1, a2, start=True,
                                stop=True))(src_st, src_ts, bts, co),
                            deps=sq_dep))
                        if lv < 4:
                            mms_st.append(S.op("t", (lambda a1, a2, b2, c_:
                                lambda e: e.matmul(
                                    b2[:, c_:c_ + 128], a1, a2, start=True,
                                    stop=True))(src_ts, src_st, bst, co),
                                deps=sq_dep))
                ev = []
                if lv == 0:
                    for half in range(2):
                        ev.append(S.op("v", (lambda hf: lambda e:
                            e.tensor_tensor(
                                out=pw_ts[0][:, 512 * hf:512 * hf + 512]
                                    .rearrange("p (k c) -> p k c", c=C),
                                in0=pb[6 + hf].rearrange(
                                    "p (k c) -> p k c", c=C),
                                in1=self.bdm_bf.unsqueeze(1)
                                    .broadcast_to([128, 4, C]),
                                op=ALU.mult))(half),
                            deps=mms_ts + [self.t_bdm]))
                        ev.append(S.op("v", (lambda hf: lambda e:
                            e.tensor_tensor(
                                out=pw_st[0][:, 512 * hf:512 * hf + 512]
                                    .rearrange("p (k c) -> p k c", c=C),
                                in0=pb[4 + hf].rearrange(
                                    "p (k c) -> p k c", c=C),
                                in1=self.bdm_bf.unsqueeze(1)
                                    .broadcast_to([128, 4, C]),
                                op=ALU.mult))(half),
                            deps=mms_st + [self.t_bdm]))
                else:
                    for half in range(2):
                        ev.append(S.op("a", (lambda hf, lv_: lambda e: e.copy(
                            out=pw_ts[lv_ % 2][:, 512 * hf:512 * hf + 512],
                            in_=pb[6 + hf]))(half, lv), deps=mms_ts))
                        if lv < 4:
                            ev.append(S.op("v", (lambda hf, lv_: lambda e:
                                e.tensor_copy(
                                    out=pw_st[lv_][:, 512 * hf:512 * hf + 512],
                                    in_=pb[4 + hf]))(half, lv), deps=mms_st))
                sq_dep = ev
            g0 = S.op("v", lambda e: e.tensor_tensor(
                out=g_all.rearrange("p (g c) -> p g c", c=C),
                in0=pw_ts[0].rearrange("p (g c) -> p g c", c=C),
                in1=self.ident_bf.unsqueeze(1).broadcast_to([128, 8, C]),
                op=ALU.add), deps=sq_dep + [self.t_idb])
            gtok = [g0]
            for lvk in (3, 2, 1, 0):
                mms = []
                for hh in range(8):
                    co = 128 * (hh % 4)
                    mms.append(S.op("t", (lambda hh_, co_, lv_: lambda e:
                        e.matmul(pb[6 + hh_ // 4][:, co_:co_ + 128],
                                 pw_st[lv_][:, 128 * hh_:128 * hh_ + 128],
                                 g_all[:, 128 * hh_:128 * hh_ + 128],
                                 start=True, stop=True))(hh, co, lvk),
                        deps=gtok))
                gtok = []
                for half in range(2):
                    gtok.append(S.op("v", (lambda hf: lambda e:
                        e.tensor_tensor(
                            out=g_all[:, 512 * hf:512 * hf + 512],
                            in0=g_all[:, 512 * hf:512 * hf + 512],
                            in1=pb[6 + hf], op=ALU.add))(half), deps=mms))
            mms = []
            for hh in range(8):
                co = 128 * (hh % 4)
                mms.append(S.op("t", (lambda hh_, co_: lambda e:
                    e.matmul(pb[6 + hh_ // 4][:, co_:co_ + 128],
                             that_bd[:, 128 * hh_:128 * hh_ + 128],
                             g_all[:, 128 * hh_:128 * hh_ + 128],
                             start=True, stop=True))(hh, co),
                    deps=gtok + [tbd]))
            gtok = []
            for half in range(2):
                gtok.append(S.op("v", (lambda hf: lambda e: e.tensor_tensor(
                    out=g_all[:, 512 * hf:512 * hf + 512],
                    in0=g_all[:, 512 * hf:512 * hf + 512],
                    in1=pb[6 + hf], op=ALU.subtract))(half), deps=mms))
            gd = self.dbgout("gdump", (NT, 2, 128, 1024), BF16)
            if gd is not None:
                gtok = gtok + [S.dma("s", (lambda c_, w_: lambda e:
                    e.dma_start(out=gd[c_][w_], in_=g_all))(self._cur_c, wv),
                    deps=gtok)]
            mms = []
            for hh in range(8):
                h = hs[hh]
                bank, co = 2 + hh // 2, 256 * (hh % 2)
                mms.append(S.op("t", (lambda h_, bk, co_, hh_: lambda e:
                    e.matmul(
                        self.pb[bk][64:128, co_:co_ + 192],
                        g_all[64:128, 128 * hh_ + 64:128 * hh_ + 128],
                        rzv[:, h_, :][64:128, :], start=True, stop=True))(
                        h, bank, co, hh),
                    deps=gtok + list(pre_toks)))
            evb = []
            for bank in range(4):
                evb.append(S.op("a", (lambda bk, w_: lambda e: e.copy(
                    out=zv[:, 2 * bk + 8 * w_:2 * bk + 8 * w_ + 2, :][64:128],
                    in_=self.pb[2 + bk][64:128, :]
                        .rearrange("p (h x) -> p h x", x=256)[:, :, 0:192]))(
                        bank, wv), deps=mms))
            z1d = self.dbgout("z1dump", (NT, 2, 128, H * 192), BF16)
            if z1d is not None:
                evb = evb + [S.dma("s", (lambda c_, w_: lambda e: e.dma_start(
                    out=z1d[c_][w_], in_=z_all))(self._cur_c, wv), deps=evb)]
            mms = []
            for hh in range(8):
                h = hs[hh]
                bank, co = 2 + hh // 2, 256 * (hh % 2)
                mms.append(S.op("t", (lambda h_, bk, co_: lambda e: e.matmul(
                    self.pb[bk][0:64, co_:co_ + 192],
                    t_full[64:128, C * h_:C * h_ + 64],
                    zv[:, h_, :][64:128, :], start=True, stop=True))(
                    h, bank, co), deps=evb))
            evy = []
            for bank in range(4):
                evy.append(S.op("v", (lambda bk, w_: lambda e: e.tensor_tensor(
                    out=rzv[:, 2 * bk + 8 * w_:2 * bk + 8 * w_ + 2, :][0:64],
                    in0=rzv[:, 2 * bk + 8 * w_:2 * bk + 8 * w_ + 2, :][0:64],
                    in1=self.pb[2 + bk][0:64, :]
                        .rearrange("p (h x) -> p h x", x=256)[:, :, 0:192],
                    op=ALU.subtract))(bank, wv), deps=mms))
            mms = []
            for hh in range(8):
                h = hs[hh]
                bank, co = 2 + hh // 2, 256 * (hh % 2)
                mms.append(S.op("t", (lambda h_, bk, co_, hh_: lambda e:
                    e.matmul(
                        self.pb[bk][0:64, co_:co_ + 192],
                        g_all[0:64, 128 * hh_:128 * hh_ + 64],
                        rzv[:, h_, :][0:64, :], start=True, stop=True))(
                        h, bank, co, hh), deps=evy))
            evt = []
            for bank in range(4):
                evt.append(S.op("a", (lambda bk, w_: lambda e: e.copy(
                    out=zv[:, 2 * bk + 8 * w_:2 * bk + 8 * w_ + 2, :][0:64],
                    in_=self.pb[2 + bk][0:64, :]
                        .rearrange("p (h x) -> p h x", x=256)[:, :, 0:192]))(
                        bank, wv), deps=mms))
            wave_tok = [S.op("v", lambda e: e.engine_nop(), deps=evt + evb)]
        return wave_tok

    def g2_state(self, c, z_all, w_all, qgT, qpt_sb, tmp_odd, bc_sb, bct_sb,
                 mcs, z_toks, w_toks, state_tok, cg):
        S, pb, io = self.S, self.pb, self.io
        zv = z_all.rearrange("p (h x) -> p h x", x=192)
        wav = w_all.rearrange("p (h x) -> p h x", x=128)
        qptv = qpt_sb.rearrange("p (pr x) -> p pr x", x=C)
        mcsv = mcs.rearrange("p (j x) -> p j x", x=DH)
        bcv = bc_sb.rearrange("p (j x) -> p j x", x=DH)
        tmpv = tmp_odd.rearrange("p (j x) -> p j x", x=C)
        bctv = bct_sb.rearrange("p (j x) -> p j x", x=DH)
        wave_done = list(z_toks) + list(w_toks) + list(state_tok)
        all_done = []
        for W in range(2):
            prs = [4 * W + j for j in range(4)]
            s4 = []
            s1l = []
            for j, p in enumerate(prs):
                bA = pb[j]
                bB = pb[4 + j]
                s0 = []
                for par in range(2):
                    h = 2 * p + par
                    r = slice(64 * par, 64 * par + 64)
                    s0.append(S.op("t", (lambda h_, r_, b_: lambda e:
                        e.matmul(b_[r_, 0:192], wav[:, h_, 0:64],
                                 zv[:, h_, :], start=True, stop=True))(
                                 h, r, bA), deps=wave_done))
                    s0.append(S.op("t", (lambda h_, r_, b_: lambda e:
                        e.matmul(b_[r_, 192:384], wav[:, h_, 64:128],
                                 zv[:, h_, :], start=True, stop=True))(
                                 h, r, bA), deps=wave_done))
                s1 = S.op("v", (lambda p_, j_, b_, cg_: lambda e:
                    e.tensor_tensor(
                        out=qptv[:, j_, :],
                        in0=qgT[p_][:, 128 * cg_:128 * (cg_ + 1)],
                        in1=b_[:, 192:320], op=ALU.subtract))(
                        p, j, bA, cg), deps=s0)
                s1l.append(s1)
                s2 = S.dma("s", (lambda c_, p_, j_: lambda e: e.dma_start(
                    out=io["qpt_dram"][c_ * H + 2 * p_:c_ * H + 2 * p_ + 2, :]
                        .rearrange("two (d t) -> (two d) t", d=DH),
                    in_=qptv[:, j_, :]))(c, p, j), deps=[s1])
                s3 = []
                for par in range(2):
                    r = slice(64 * par, 64 * par + 64)
                    s3.append(S.op("t", (lambda p_, r_, j_, b_: lambda e:
                        e.matmul(b_[r_, 0:128],
                                 self.stloc[p_][r_, :], qptv[:, j_, :][r_, :],
                                 start=True, stop=True))(p, r, j, bB),
                        deps=[s1] + list(state_tok)))
                t1 = S.op("a", (lambda j_, b_: lambda e: e.copy(
                    out=tmpv[:, j_, :], in_=b_[:, 0:128]))(j, bB), deps=s3)
                tot = S.op("v", (lambda p_, j_, b_, c_: lambda e:
                    e.tensor_tensor(
                        out=self.Ot[p_][:, C * c_:C * (c_ + 1)],
                        in0=tmpv[:, j_, :], in1=b_[:, 0:128],
                        op=ALU.add))(p, j, bA, c), deps=[t1] + s0)
                ac = S.op("v", (lambda c_, p_, j_, b_: lambda e:
                    e.scalar_tensor_tensor(
                        out=mcsv[:, j_, :], in0=self.i64two_sb,
                        scalar=self.gcp[:, c_ * 8 + p_:c_ * 8 + p_ + 1],
                        in1=b_[:, 320:384], op0=ALU.mult,
                        op1=ALU.subtract))(c, p, j, bA),
                    deps=s0 + [s1, self.t_gcp])
                bcx = S.op("v", (lambda j_, b_: lambda e: e.tensor_copy(
                    out=bcv[:, j_, :], in_=b_[:, 128:192]))(
                    j, bA), deps=s0)
                spr = S.op("v", (lambda p_, c_: lambda e: e.tensor_copy(
                    out=self.prun_store[:, (c_ * (H // 2) + p_) * DH:
                                        (c_ * (H // 2) + p_ + 1) * DH],
                    in_=self.prun[p_]))(p, c), deps=list(state_tok))
                bcts = []
                for par in range(2):
                    r = slice(64 * par, 64 * par + 64)
                    bcts.append(S.op("t", (lambda r_, j_, b_: lambda e:
                        e.transpose(
                            b_[r_, 128:160].bitcast(BF16),
                            bcv[:, j_, :][r_, :], self.i64two_bf[r_, :]))(
                            r, j, bB), deps=[bcx, t1]))
                ebt = S.op("a", (lambda j_, b_: lambda e: e.copy(
                    out=bctv[:, j_, :], in_=b_[:, 128:160]
                    .bitcast(BF16)))(j, bB), deps=bcts)
                mstc = []
                for par in range(2):
                    r = slice(64 * par, 64 * par + 64)
                    mstc.append(S.op("t", (lambda p_, r_, j_, b_: lambda e:
                        e.matmul(b_[r_, 192:256],
                                 mcsv[:, j_, :][r_, :],
                                 self.stloc[p_][r_, :],
                                 start=True, stop=True))(p, r, j, bB),
                        deps=[ac, ebt] + list(state_tok) + s3))
                evst = S.op("v", (lambda p_, j_, b_: lambda e: e.tensor_tensor(
                    out=self.stloc[p_], in0=bctv[:, j_, :],
                    in1=b_[:, 192:256],
                    op=ALU.add))(p, j, bB), deps=mstc + [ebt])
                mprs = []
                for par in range(2):
                    r = slice(64 * par, 64 * par + 64)
                    mprs.append(S.op("t", (lambda p_, r_, j_, b_: lambda e:
                        e.matmul(b_[r_, 384:448],
                                 self.prunT[p_][r_, :], mcsv[:, j_, :][r_, :],
                                 start=True, stop=True))(p, r, j, bA),
                        deps=[ac, spr, tot, bcx]))
                    mprs.append(S.op("t", (lambda p_, r_, j_, b_: lambda e:
                        e.matmul(b_[r_, 448:512],
                                 mcsv[:, j_, :][r_, :], self.prunT[p_][r_, :],
                                 start=True, stop=True))(p, r, j, bA),
                        deps=[ac, spr, tot, bcx]))
                evpr = S.op("a", (lambda p_, j_, b_: lambda e: e.copy(
                    out=self.prun[p_], in_=b_[:, 384:448]))(
                    p, j, bA), deps=mprs + [spr])
                evprt = S.op("a", (lambda p_, j_, b_: lambda e: e.copy(
                    out=self.prunT[p_], in_=b_[:, 448:512]))(p, j, bA),
                    deps=mprs)
                s4 += [tot, evst, evpr, evprt, s2, ac, bcx, ebt]
            wave_done = s4 + s1l
            all_done += s4
        return [S.op("v", lambda e: e.engine_nop(), deps=all_done)]

    def corrphase(self):
        S, io, pb = self.S, self.io, self.pb
        arB, arF = self.arB, self.arF
        ragf = [arB.bf16(128, H * 2 * DH) for _ in range(2)]
        ragb = [arB.bf16(128, H * 2 * DH) for _ in range(2)]
        ipat_f = arF.f32(128, H * 2 * DH)
        sint = [arB.bf16(128, DH) for _ in range(H // 2)]
        mc = arB.bf16(128, DH)
        qptl = arB.bf16(128, 8 * C)
        qv = qptl.rearrange("p (pr t) -> p pr t", t=C)
        lip = S.dma("s", lambda e: e.dma_start(out=ipat_f, in_=io["ipat"]))
        zi = [S.op("v", (lambda p_: lambda e: e.memset(sint[p_], 0.0))(p))
              for p in range(H // 2)]
        ptoks = list(zi)
        rag_busy = [None, None]
        for j in range(NCORE):
            pp = j % 2
            lt = [S.dma("s", (lambda j_, p_: lambda e: e.dma_start(
                      out=ragf[p_][0:64, :],
                      in_=io["ags_out"][64 * j_:64 * (j_ + 1), :]))(j, pp),
                  deps=[self.ag3] + ([rag_busy[pp]] if rag_busy[pp] else [])),
                  S.dma("s", (lambda j_, p_: lambda e: e.dma_start(
                      out=ragf[p_][64:128, :],
                      in_=io["ags_out"][64 * j_:64 * (j_ + 1), :]))(j, pp),
                  deps=[self.ag3] + ([rag_busy[pp]] if rag_busy[pp] else []))]
            t1 = S.op("v", (lambda j_, p_: lambda e: e.tensor_scalar(
                out=ragf[p_], in0=ragf[p_],
                scalar1=self.hm_sb[:, j_:j_ + 1], scalar2=None,
                op0=ALU.mult))(j, pp), deps=lt + [self.pt["hm"]])
            t2 = S.op("v", (lambda j_, p_: lambda e: e.scalar_tensor_tensor(
                out=ragb[p_], in0=ipat_f,
                scalar=self.hm1_sb[:, j_:j_ + 1], in1=ragf[p_],
                op0=ALU.mult, op1=ALU.add))(j, pp),
                deps=[t1, lip, self.t_hm1])
            mms = []
            for p in range(H // 2):
                for par in range(2):
                    h = 2 * p + par
                    r = slice(64 * par, 64 * par + 64)
                    mms.append(S.op("t", (lambda p2, h_, p_, r_: lambda e:
                        e.matmul(
                            pb[0][r_, 64 * p_:64 * (p_ + 1)],
                            ragb[p2][r_, 2 * DH * h_:2 * DH * h_ + DH],
                            sint[p_][r_, :], start=True, stop=True))(pp, h, p, r),
                        deps=[t2] + ptoks))
            ntoks = []
            for p in range(H // 2):
                for par in range(2):
                    h = 2 * p + par
                    r = slice(64 * par, 64 * par + 64)
                    ntoks.append(S.op("v", (lambda p2, h_, p_, r_: lambda e:
                        e.tensor_tensor(
                            out=sint[p_][r_, :],
                            in0=ragb[p2][r_, 2 * DH * h_ + DH:2 * DH * (h_ + 1)],
                            in1=pb[0][r_, 64 * p_:64 * (p_ + 1)],
                            op=ALU.add))(pp, h, p, r),
                        deps=mms))
            rag_busy[pp] = ntoks[-1]
            ptoks = ntoks
        mcl = arB.bf16(128, 512)
        mclv = mcl.rearrange("p (pr x) -> p pr x", x=DH)
        prev = ptoks
        for c in range(NT):
            lq = S.dma("s", (lambda c_: lambda e: e.dma_start(
                out=qptl,
                in_=io["qpt_dram"][16 * c_:16 * (c_ + 1), :]
                .rearrange("(pr two) (d t) -> two d pr t", two=2, d=DH)))(c),
                deps=prev)
            mm1s = []
            for p in range(H // 2):
                for par in range(2):
                    r = slice(64 * par, 64 * par + 64)
                    mm1s.append(S.op("t", (lambda c_, p_, r_: lambda e:
                        e.matmul(
                            pb[1][r_, 64 * p_:64 * (p_ + 1)],
                            self.prun_store[r_, (c_ * (H // 2) + p_) * DH:
                                            (c_ * (H // 2) + p_ + 1) * DH],
                            sint[p_][r_, :], start=True, stop=True))(c, p, r),
                        deps=prev))
            evm = S.op("a", lambda e: e.copy(out=mcl, in_=pb[1]),
                       deps=mm1s)
            mm2s = []
            for p in range(H // 2):
                for par in range(2):
                    r = slice(64 * par, 64 * par + 64)
                    mm2s.append(S.op("t", (lambda p_, r_: lambda e: e.matmul(
                        pb[2 + p_ // 4][r_, 128 * (p_ % 4):128 * (p_ % 4) + 128],
                        mclv[:, p_, :][r_, :], qv[:, p_, :][r_, :],
                        start=True, stop=True))(p, r), deps=[evm, lq]))
            ntoks = []
            for p in range(H // 2):
                ntoks.append(S.op("v", (lambda c_, p_: lambda e:
                    e.tensor_tensor(
                        out=self.Ot[p_][:, C * c_:C * (c_ + 1)],
                        in0=self.Ot[p_][:, C * c_:C * (c_ + 1)],
                        in1=pb[2 + p_ // 4][:, 128 * (p_ % 4):
                                            128 * (p_ % 4) + 128],
                        op=ALU.add))(c, p), deps=mm2s))
            prev = ntoks
        self.corr_done = prev

    def woutphase(self):
        S, io, pb = self.S, self.io, self.pb
        arB = self.arB
        woutT = [arB.bf16(128, D) for _ in range(NKC)]
        wl = []
        for k in range(NKC):
            wl.append(S.dma("s", (lambda k_, d_: lambda e: e.dma_start(
                out=d_, in_=io["wout_dram"][k_], transpose=True))(k, woutT[k]),
                deps=self.w_wout))
        wf = []
        for k in range(NKC):
            wf.append(S.op("v", (lambda k_: lambda e: e.tensor_tensor(
                out=woutT[k_], in0=woutT[k_], in1=self.rep["ga"],
                op=ALU.mult))(k), deps=[wl[k], self.t_ga]))
        x_t = self.x_t
        x2_t = self.ln_xs
        x2_toks = []
        prev = list(self.corr_done)
        for i in range(NT):
            lx = S.dma("s", (lambda i_: lambda e: e.dma_start(
                out=x_t, in_=io["x_in"][128 * i_:128 * (i_ + 1), :]))(i),
                deps=prev)
            mms = [None, None]
            for k in range(NKC):
                for n in range(2):
                    dd = [wf[k]] + list(self.corr_done)
                    if mms[n]:
                        dd.append(mms[n])
                    if k == 0:
                        dd += x2_toks[-3:]
                    mms[n] = S.op("t", (lambda i_, k_, n_: lambda e: e.matmul(
                        pb[n_], self.Ot[k_][:, 128 * i_:128 * (i_ + 1)],
                        woutT[k_][:, 512 * n_:512 * (n_ + 1)],
                        start=(k_ == 0), stop=(k_ == NKC - 1)))(i, k, n),
                        deps=dd)
            ev = []
            for n in range(2):
                ev.append(S.op("v", (lambda n_: lambda e: e.tensor_tensor(
                    out=x2_t[:, 512 * n_:512 * (n_ + 1)],
                    in0=x_t[:, 512 * n_:512 * (n_ + 1)], in1=pb[n_],
                    op=ALU.add))(n), deps=[mms[n], lx]))
            st = S.dma("s", (lambda i_: lambda e: e.dma_start(
                out=io["x2_dram"][128 * i_:128 * (i_ + 1), :], in_=x2_t))(i),
                deps=ev)
            x2_toks = ev + [st]
            prev = [st]
        self.x2_done = prev

    def ffnphase(self):
        S, io, pb = self.S, self.io, self.pb
        arB, arF = self.arB, self.arF
        self.set_qscr(1408, dual=True)
        x_t = self.x_t
        silu_sb = arF.f32(128, 1408)
        h2f = silu_sb[:, 0:D]
        h2q = arB.bf16(128, D)
        qtoks = []
        prev = list(self.x2_done)
        for i in range(NT):
            lx = S.dma("s", (lambda i_: lambda e: e.dma_start(
                out=x_t, in_=io["x2_dram"][128 * i_:128 * (i_ + 1), :]))(i),
                deps=prev)
            th = self.ln_mod(x_t, self.rep["cb"], self.rep["sb2"], h2f, [lx])
            tq = self.quant_seq(h2f, h2q, [th] + prev, eng="v")
            st = S.dma("s", (lambda i_: lambda e: e.dma_start(
                out=io["h2q_dram"][:, 128 * i_:128 * (i_ + 1), :]
                    .rearrange("c p k -> p c k"), in_=h2q))(i), deps=[tq])
            qtoks.append(st)
            prev = [st, tq]
        h2qT_1 = [arB.bf16(128, 512) for _ in range(NKC)]
        h2qT = [h2qT_1, h2qT_1]
        wgu = [arB.bf16(128, 2816) for _ in range(NKC)]
        hm_f = arF.f32(128, 1408)
        hmq = arB.bf16(128, 1408)
        hql_tok = [[None] * NKC, [None] * NKC]
        gstore = []
        prev_readers = []
        prev_q = []
        prev_mult = []
        nsub = ((0, 512), (512, 512), (1024, 384))
        for G in range(2):
            wls = []
            for k in range(NKC):
                wls.append(S.dma("s", (lambda k_, G_, d_: lambda e: e.dma_start(
                    out=d_[:, 0:1408],
                    in_=io["agw_out"][128 * k_:128 * (k_ + 1),
                                      1408 * G_:1408 * (G_ + 1)]))(k, G, wgu[k]),
                    deps=[self.ag1] + gstore[-2:]))
                wls.append(S.dma("s", (lambda k_, G_, d_: lambda e: e.dma_start(
                    out=d_[:, 1408:2816],
                    in_=io["agw_out"][128 * k_:128 * (k_ + 1),
                                      IF + 1408 * G_:IF + 1408 * (G_ + 1)]))(k, G, wgu[k]),
                    deps=[self.ag1] + gstore[-2:]))
            for i in range(NT):
                qtr, buf = i // 4, (i // 4) % 2
                if i % 4 == 0:
                    for k in range(NKC):
                        hql_tok[buf][k] = S.dma("s", (lambda k_, d_, q_:
                            lambda e: e.dma_start(
                                out=d_,
                                in_=io["h2q_dram"][k_][512 * q_:512 * (q_ + 1), :],
                                transpose=True))(k, h2qT[buf][k], qtr),
                            deps=qtoks + gstore[-12:] +
                                 ([hql_tok[buf][k]] if hql_tok[buf][k] else []))
                io_i = 128 * (i % 4)
                mms = [None] * 6
                for k in range(NKC):
                    for n in range(6):
                        half_gu, n3 = divmod(n, 3)
                        src_c = 1408 * half_gu + nsub[n3][0]
                        w_ = nsub[n3][1]
                        dd = [wls[2 * k], wls[2 * k + 1], hql_tok[buf][k]]
                        if mms[n]:
                            dd.append(mms[n])
                        if k == 0:
                            dd += prev_readers
                        mms[n] = S.op("t", (lambda k_, n_, sc_, w2_, b2, o_:
                            lambda e: e.matmul(
                                pb[n_][:, 0:w2_],
                                h2qT[b2][k_][:, o_:o_ + 128],
                                wgu[k_][:, sc_:sc_ + w2_],
                                start=(k_ == 0), stop=(k_ == NKC - 1)))(
                                k, n, src_c, w_, buf, io_i), deps=dd)
                sil = []
                for n3, (c0, w_) in enumerate(nsub):
                    sil.append(S.op("a", (lambda n_, c0_, w2_: lambda e:
                        e.activation(out=silu_sb[:, c0_:c0_ + w2_],
                                     in_=pb[n_][:, 0:w2_], func=AF.Silu))(
                                     n3, c0, w_), deps=[mms[n3]] + prev_mult))
                mult = []
                for n3, (c0, w_) in enumerate(nsub):
                    mult.append(S.op("v", (lambda n_, c0_, w2_: lambda e:
                        e.tensor_tensor(
                            out=hm_f[:, c0_:c0_ + w2_],
                            in0=silu_sb[:, c0_:c0_ + w2_],
                            in1=pb[3 + n_][:, 0:w2_], op=ALU.mult))(n3, c0, w_),
                        deps=[sil[n3], mms[3 + n3]] + prev_q))
                qeng = "v"
                tq = self.quant_seq(hm_f, hmq, mult + prev_q, eng=qeng)
                prev_q = self._q_src
                st = S.dma("s", (lambda i_, G_: lambda e: e.dma_start(
                    out=io["hmid_dram"][11 * G_:11 * (G_ + 1),
                                        128 * i_:128 * (i_ + 1), :]
                        .rearrange("c p k -> p c k"), in_=hmq))(i, G),
                    deps=[tq])
                gstore.append(st)
                prev_q = prev_q + [st]
                prev_readers = sil + mult
                prev_mult = mult
        self.epoch()
        NIF = IF // 128
        wdnT = [arB.bf16(128, D) for _ in range(NIF)]
        wdl = []
        for k in range(NIF):
            wdl.append(S.dma("s", (lambda k_, d_: lambda e: e.dma_start(
                out=d_, in_=io["agd_out"][128 * k_:128 * (k_ + 1), :]))(k, wdnT[k]),
                deps=[self.ag2]))
        wdf = []
        for k in range(NIF):
            wdf.append(S.op("v", (lambda k_: lambda e: e.tensor_tensor(
                out=wdnT[k_], in0=wdnT[k_], in1=self.rep["gb"],
                op=ALU.mult))(k), deps=[wdl[k], self.t_gb]))
        hmT = [arB.bf16(128, 512) for _ in range(NIF)]
        out_toks = []
        prev_ev = []
        for qtr in range(NT // 4):
            htl = []
            for k in range(NIF):
                htl.append(S.dma("s", (lambda k_, d_, q_: lambda e: e.dma_start(
                    out=d_,
                    in_=io["hmid_dram"][k_][512 * q_:512 * (q_ + 1), :],
                    transpose=True))(k, hmT[k], qtr),
                    deps=gstore + out_toks[-3:]))
            for i2 in range(4):
                i = 4 * qtr + i2
                lx = S.dma("s", (lambda i_: lambda e: e.dma_start(
                    out=x_t, in_=io["x2_dram"][128 * i_:128 * (i_ + 1), :]))(i),
                    deps=out_toks[-1:] if out_toks else gstore[-1:])
                mms = [None, None]
                for k in range(NIF):
                    for n in range(2):
                        dd = [htl[k], wdf[k]]
                        if mms[n]:
                            dd.append(mms[n])
                        if k == 0:
                            dd += prev_ev
                        mms[n] = S.op("t", (lambda i2_, k_, n_: lambda e:
                            e.matmul(pb[6 + n_],
                                     hmT[k_][:, 128 * i2_:128 * (i2_ + 1)],
                                     wdnT[k_][:, 512 * n_:512 * (n_ + 1)],
                                     start=(k_ == 0), stop=(k_ == NIF - 1)))(
                                     i2, k, n), deps=dd)
                ev = []
                for n in range(2):
                    ev.append(S.op("v", (lambda n_: lambda e: e.tensor_tensor(
                        out=x_t[:, 512 * n_:512 * (n_ + 1)],
                        in0=x_t[:, 512 * n_:512 * (n_ + 1)], in1=pb[6 + n_],
                        op=ALU.add))(n), deps=[mms[n], lx]))
                st = S.dma("s", (lambda i_: lambda e: e.dma_start(
                    out=io["out_x"][128 * i_:128 * (i_ + 1), :], in_=x_t))(i),
                    deps=ev)
                out_toks.append(st)
                prev_ev = list(ev)
        self.out_toks = out_toks

    def finish(self):
        S = self.S
        S.op("v", lambda e: e.engine_nop(), deps=self.out_toks)
        S.op("s", lambda e: e.nop(), deps=self.out_toks)


_NC_CACHE = {}


def _get_nc(debug=()):
    key = tuple(sorted(debug))
    if key not in _NC_CACHE:
        _NC_CACHE[key] = build_nc(key)
    return _NC_CACHE[key]


def host_inputs(full):
    f32 = np.float32
    x = np.ascontiguousarray(full["L0_attn_x"], dtype=f32)
    s_idx = np.arange(128)[:, None]
    t_idx = np.arange(128)[None, :]
    mask_ts = np.where(s_idx > t_idx, 0.0, BIG).astype(f32)
    mask_sts = np.where(s_idx < t_idx, 0.0, -BIG).astype(f32)
    mask_sti = np.where(s_idx <= t_idx, 0.0, -BIG).astype(f32)
    i64 = np.eye(64, dtype=f32)
    i64two = np.concatenate([i64, i64], axis=0)
    i64rep8 = np.tile(i64two, (1, 8))
    ipat = np.zeros((128, H * 2 * DH), f32)
    for hh in range(H):
        ipat[0:64, 2 * DH * hh:2 * DH * hh + DH] = i64
        ipat[64:128, 2 * DH * hh:2 * DH * hh + DH] = i64
    shared = {
        "Wqkv": np.ascontiguousarray(full["L0_attn_Wqkv"], dtype=f32),
        "Wg": np.ascontiguousarray(full["L0_attn_Wg"], dtype=f32),
        "bg": np.ascontiguousarray(full["L0_attn_bg"], dtype=f32).reshape(1, H),
        "Wb": np.ascontiguousarray(full["L0_attn_Wb"], dtype=f32),
        "bb": np.ascontiguousarray(full["L0_attn_bb"], dtype=f32).reshape(1, H),
        "Wout": np.ascontiguousarray(full["L0_attn_Wout"], dtype=f32),
        "c_vec": np.ascontiguousarray(full["L0_c"], dtype=f32),
        "wmod": np.ascontiguousarray(full["L0_wmod"], dtype=f32),
        "ident128": np.eye(128, dtype=f32),
        "mask_ts": mask_ts, "mask_sts": mask_sts, "mask_sti": mask_sti,
        "i64two": i64two, "i64rep8": i64rep8, "ipat": ipat,
        "bdmask": ((s_idx < 64) == (t_idx < 64)).astype(f32),
    }
    in_maps = []
    for core in range(NCORE):
        b, q = divmod(core, 4)
        m = dict(shared)
        m["x_slice"] = np.ascontiguousarray(x[b, TQ * q:TQ * (q + 1)])
        m["wgate_sh"] = np.ascontiguousarray(
            full["L0_ffn_w_gate"][:, 128 * core:128 * (core + 1)], dtype=f32)
        m["wup_sh"] = np.ascontiguousarray(
            full["L0_ffn_w_up"][:, 128 * core:128 * (core + 1)], dtype=f32)
        m["wdown_sh"] = np.ascontiguousarray(
            full["L0_ffn_w_down"][:, 352 * core:352 * (core + 1)], dtype=f32)
        sel = np.zeros((B, 128), f32)
        sel[b, :] = 1.0
        m["sel128"] = sel
        hm = np.zeros((1, NCORE), f32)
        hm[0, 4 * b:4 * b + q] = 1.0
        m["hopmask"] = hm
        in_maps.append(m)
    return in_maps


def kernel(**inputs):
    nc, dbg = _get_nc()
    in_maps = host_inputs(inputs)
    res = run_bass_kernel_spmd(nc, in_maps, core_ids=list(range(NCORE)))
    out = np.zeros((B, T, D), np.float32)
    for core in range(NCORE):
        b, q = divmod(core, 4)
        out[b, TQ * q:TQ * (q + 1)] = res.results[core]["out_x"]
    return out


# version 12
# speedup vs baseline: 1.0889x; 1.0889x over previous
import numpy as np
import ml_dtypes
import concourse.bass as bass
import concourse.mybir as mybir
from concourse.bass_utils import run_bass_kernel_spmd

F32 = mybir.dt.float32
BF16 = mybir.dt.bfloat16
AF = mybir.ActivationFunctionType
ALU = mybir.AluOpType
AX = mybir.AxisListType

MAGIC = float(np.float32(1.5 * 2**23))
PHASE_LIMIT = 6
CHUNK_PARTS = 3
BIG = 1e30

B, T, D = 2, 8192, 1024
NCORE = 8
TQ = 2048
NT = TQ // 128
H, DH = 16, 64
C = 128
IF = 2816
NKC = D // 128

ENGINES = ("v", "a", "t", "g", "s")


class Tok:
    __slots__ = ("sem", "val")

    def __init__(self, sem, val):
        self.sem = sem
        self.val = val


class Sched:
    def __init__(self, nc, pool_sizes=None):
        self.nc = nc
        self._ctx = []
        self.sems = {}
        self.cnt = {}
        self.prog = {e: [] for e in ENGINES}
        self.last_waited = {e: {} for e in ENGINES}
        for e in ENGINES:
            self.sems[e] = self._mksem(f"sem_{e}")
            self.cnt[e] = 0
        self.sems["c"] = self._mksem("sem_c")
        self.cnt["c"] = 0
        ps = pool_sizes or {"s": 24, "a": 10, "g": 10}
        self.pools = {}
        for q, n in ps.items():
            self.pools[q] = {
                "sems": [self._mksem(f"sem_{q}dma{i}") for i in range(n)],
                "cnt": [0] * n,
                "next": 0,
            }

    def _mksem(self, name):
        cm = self.nc.semaphore(name)
        sem = cm.__enter__()
        self._ctx.append(cm)
        return sem

    def _emit_waits(self, eng_key, deps):
        waits = []
        lw = self.last_waited[eng_key]
        for d in deps:
            if d is None:
                continue
            key = id(d.sem)
            if lw.get(key, -1) >= d.val:
                continue
            lw[key] = d.val
            waits.append((d.sem, d.val))
        return waits

    def op(self, e, fn, deps=()):
        waits = self._emit_waits(e, deps)
        sem = self.sems[e]
        self.cnt[e] += 1
        self.prog[e].append((waits, fn, sem, 1))
        return Tok(sem, self.cnt[e])

    def cop(self, fn, deps=()):
        waits = self._emit_waits("g", deps)
        sem = self.sems["c"]
        self.cnt["c"] += 1
        self.prog["g"].append((waits, fn, sem, 1))
        return Tok(sem, self.cnt["c"])

    def dma(self, q, fn, deps=()):
        pool = self.pools[q]
        i = pool["next"]
        pool["next"] = (i + 1) % len(pool["sems"])
        sem = pool["sems"][i]
        prev = pool["cnt"][i]
        deps = list(deps)
        if prev:
            deps.append(Tok(sem, prev))
        waits = self._emit_waits(q, deps)
        pool["cnt"][i] = prev + 16
        self.prog[q].append((waits, fn, sem, 16))
        return Tok(sem, pool["cnt"][i])

    def gdma(self, fn, deps=()):
        return self.dma("g", fn, deps)

    def etok(self, e):
        return Tok(self.sems[e], self.cnt[e])

    def barrier(self):
        deps = [Tok(self.sems[e], self.cnt[e]) for e in ENGINES]
        if self.cnt["c"]:
            deps.append(Tok(self.sems["c"], self.cnt["c"]))
        for q, pool in self.pools.items():
            for i, cv in enumerate(pool["cnt"]):
                if cv:
                    deps.append(Tok(pool["sems"][i], cv))
        hub = self.op("v", lambda e: e.engine_nop(), deps=deps)
        for e in ENGINES:
            if e != "v":
                self.op(e, lambda en: en.engine_nop()
                        if hasattr(en, "engine_nop") else en.nop(),
                        deps=[hub])
            self.last_waited[e] = {}
        self.hub = hub
        return hub

    def finalize(self, block):
        def mk(e):
            def body(eng):
                for waits, fn, sem, inc in self.prog[e]:
                    for (s, vv) in waits:
                        eng.wait_ge(s, vv)
                    inst = fn(eng)
                    if inst is None:
                        raise RuntimeError(f"op fn on {e} returned None")
                    inst.then_inc(sem, inc)
            return body

        if self.prog["g"]:
            block.gpsimd(mk("g"))
        if self.prog["s"]:
            block.sync(mk("s"))
        if self.prog["t"]:
            block.tensor(mk("t"))
        if self.prog["v"]:
            block.vector(mk("v"))
        if self.prog["a"]:
            block.scalar(mk("a"))


class Arena:
    def __init__(self, nc, name, cols_f32):
        self.t = nc.alloc_sbuf_tensor(name, [128, cols_f32], F32).ap()
        self.cols = cols_f32
        self.off = 0

    def reset(self):
        self.off = 0

    def f32(self, p, n):
        a = self.t[:, self.off:self.off + n]
        self.off += n
        assert self.off <= self.cols, f"arena overflow {self.off}>{self.cols}"
        return a[0:p, :] if p != 128 else a

    def bf16(self, p, n):
        slots = (n + 1) // 2
        a = self.t[:, self.off:self.off + slots].bitcast(BF16)
        self.off += slots
        assert self.off <= self.cols, f"arena overflow {self.off}>{self.cols}"
        return a[0:p, 0:n] if p != 128 else a[:, 0:n]


def build_nc(debug=()):
    nc = bass.Bass("TRN2", target_bir_lowering=False)
    io = {}

    def inp(name, shape, dt=F32):
        return nc.dram_tensor(name, list(shape), dt, kind="ExternalInput").ap()

    io["x_in"] = inp("x_slice", (TQ, D))
    io["wqkv"] = inp("Wqkv", (3072, D))
    io["wg"] = inp("Wg", (H, D))
    io["bgv"] = inp("bg", (1, H))
    io["wb"] = inp("Wb", (H, D))
    io["bbv"] = inp("bb", (1, H))
    io["wout"] = inp("Wout", (D, D))
    io["wgate_sh"] = inp("wgate_sh", (IF, 128))
    io["wup_sh"] = inp("wup_sh", (IF, 128))
    io["wdown_sh"] = inp("wdown_sh", (D, IF // NCORE))
    io["c_in"] = inp("c_vec", (B, D))
    io["wmod"] = inp("wmod", (D, 6 * D))
    io["sel128"] = inp("sel128", (B, 128))
    io["hopmask"] = inp("hopmask", (1, NCORE))
    io["ident128"] = inp("ident128", (128, 128))
    io["mask_ts"] = inp("mask_ts", (128, 128))
    io["mask_sts"] = inp("mask_sts", (128, 128))
    io["mask_sti"] = inp("mask_sti", (128, 128))
    io["i64two"] = inp("i64two", (128, 64))
    io["i64rep8"] = inp("i64rep8", (128, 512))
    io["bdmask"] = inp("bdmask", (128, 128))
    io["ipat"] = inp("ipat", (128, H * 2 * DH))
    io["out_x"] = nc.dram_tensor("out_x", [TQ, D], F32, kind="ExternalOutput").ap()

    dbg = {}

    def dbgout(name, shape=None, dt=F32):
        if name in debug and name not in dbg:
            dbg[name] = nc.dram_tensor("dbg_" + name, list(shape), dt,
                                       kind="ExternalOutput").ap()
        return dbg.get(name)

    dr = lambda name, shape, dt: nc.dram_tensor(name, list(shape), dt).ap()
    io["h_dram"] = dr("h_dram", (NKC, TQ, 128), BF16)
    io["wall_dram"] = dr("wall_dram", (NKC, 3104, 128), BF16)
    io["wout_dram"] = dr("wout_dram", (NKC, D, 128), BF16)
    io["kpl_dram"] = dr("kpl_dram", (NKC, TQ, 128), BF16)
    io["kb_dram"] = dr("kb_dram", (NKC, TQ, 128), BF16)
    io["q_dram"] = dr("q_dram", (NKC, TQ, 128), BF16)
    io["qg_dram"] = dr("qg_dram", (NKC, TQ, 128), BF16)
    io["bv_dram"] = dr("bv_dram", (NKC, TQ, 128), BF16)
    io["l_dram"] = dr("l_dram", (NT, H * C), F32)
    io["lc_dram"] = dr("lc_dram", (1, NT * H), F32)
    io["qpt_dram"] = dr("qpt_dram", (NT * H, DH * C), BF16)
    io["x2_dram"] = dr("x2_dram", (TQ, D), F32)
    io["h2q_dram"] = dr("h2q_dram", (NKC, TQ, 128), BF16)
    io["hmid_dram"] = dr("hmid_dram", (IF // 128, TQ, 128), BF16)
    io["agw_in"] = dr("agw_in", (128, 2 * IF), BF16)
    io["agw_out"] = dr("agw_out", (NCORE * 128, 2 * IF), BF16)
    io["agd_in"] = dr("agd_in", (IF // NCORE, D), BF16)
    io["agd_out"] = dr("agd_out", (IF, D), BF16)
    io["ags_in"] = dr("ags_in", (DH, H * 2 * DH), BF16)
    io["ags_out"] = dr("ags_out", (NCORE * DH, H * 2 * DH), BF16)

    with nc.Block() as block:
        S = Sched(nc)
        K = Kern(nc, S, io, dbgout)
        K.emit()
        S.finalize(block)
    return nc, dbg


class Kern:
    def __init__(self, nc, S, io, dbgout):
        self.nc = nc
        self.S = S
        self.io = io
        self.dbgout = dbgout
        self.sb = lambda name, shape, dt=F32: nc.alloc_sbuf_tensor(
            name, list(shape), dt).ap()
        self.pb = [nc.alloc_psum_tensor(f"pb{i}", [128, 512], F32).ap()
                   for i in range(8)]
        self.arF = Arena(nc, "arF", 6200)
        self.arB = Arena(nc, "arB", 18800)

    def epoch(self):
        self.S.barrier()
        self.arF.reset()
        self.arB.reset()

    def emit(self):
        lim = PHASE_LIMIT
        self.persist()
        self.prologue()
        if lim >= 1:
            self.ffn_weight_quant()
            self.weights_attn()
        self.epoch()
        if lim >= 2:
            self.hphase()
        if lim >= 3:
            self.qkvphase()
        self.epoch()
        if lim >= 4:
            self.chunkphase()
        self.epoch()
        if lim >= 5:
            self.corrphase()
            self.woutphase()
        self.epoch()
        if lim >= 6:
            self.ffnphase()
            self.finish()
        else:
            self.dummy_out()

    def dummy_out(self):
        S, io = self.S, self.io
        z = S.op("v", lambda e: e.memset(self.x_t, 0.0),
                 deps=[S.hub])
        toks = []
        for i in range(NT):
            toks.append(S.dma("s", (lambda i_: lambda e: e.dma_start(
                out=io["out_x"][128 * i_:128 * (i_ + 1), :],
                in_=self.x_t))(i), deps=[z]))
        S.op("v", lambda e: e.engine_nop(), deps=toks)
        S.op("s", lambda e: e.nop(), deps=toks)

    def persist(self):
        S, io, sb = self.S, self.io, self.sb
        self.ident_sb = sb("ident_sb", (128, 128))
        self.mask_ts_sb = sb("mask_ts_sb", (128, 128))
        self.mask_sts_sb = sb("mask_sts_sb", (128, 128))
        self.mask_sti_sb = sb("mask_sti_sb", (128, 128))
        self.i64two_sb = sb("i64two_sb", (128, 64))
        self.i64two_bf = sb("i64two_bf", (128, 64), BF16)
        self.i64rep8f = sb("i64rep8f", (128, 512))
        self.i64rep8b = sb("i64rep8b", (128, 512), BF16)
        self.selF_sb = sb("selF_sb", (B, 128))
        self.hm_sb = sb("hm_sb", (128, NCORE))
        self.hm1_sb = sb("hm1_sb", (128, NCORE))
        self.bgbb_rep = sb("bgbb_rep", (128, 2 * H))
        t = {}
        t["id"] = S.dma("s", lambda e: e.dma_start(out=self.ident_sb, in_=io["ident128"]))
        t["mts"] = S.dma("s", lambda e: e.dma_start(out=self.mask_ts_sb, in_=io["mask_ts"]))
        t["msts"] = S.dma("s", lambda e: e.dma_start(out=self.mask_sts_sb, in_=io["mask_sts"]))
        t["msti"] = S.dma("s", lambda e: e.dma_start(out=self.mask_sti_sb, in_=io["mask_sti"]))
        t["i2"] = S.dma("s", lambda e: e.dma_start(out=self.i64two_sb, in_=io["i64two"]))
        t["i8"] = S.dma("s", lambda e: e.dma_start(out=self.i64rep8f, in_=io["i64rep8"]))
        self.bdm_f = sb("bdm_f", (128, 128))
        self.bdm_bf = sb("bdm_bf", (128, 128), BF16)
        self.odm_bf = sb("odm_bf", (128, 128), BF16)
        t["bdm"] = S.dma("s", lambda e: e.dma_start(out=self.bdm_f, in_=io["bdmask"]))
        t["sel"] = S.dma("s", lambda e: e.dma_start(out=self.selF_sb, in_=io["sel128"]))
        t["hm"] = S.dma("s", lambda e: e.dma_start(
            out=self.hm_sb, in_=io["hopmask"].partition_broadcast(128)))
        t["bgr"] = S.dma("s", lambda e: e.dma_start(
            out=self.bgbb_rep[:, 0:H], in_=io["bgv"].partition_broadcast(128)))
        t["bbr"] = S.dma("s", lambda e: e.dma_start(
            out=self.bgbb_rep[:, H:2 * H], in_=io["bbv"].partition_broadcast(128)))
        self.t_hm1 = S.op("v", lambda e: e.tensor_scalar(
            out=self.hm1_sb, in0=self.hm_sb, scalar1=-1.0, scalar2=1.0,
            op0=ALU.mult, op1=ALU.add), deps=[t["hm"]])
        self.ident_bf = self.sb("ident_bf", (128, 128), BF16)
        self.t_idb = S.op("v", lambda e: e.tensor_copy(
            out=self.ident_bf, in_=self.ident_sb), deps=[t["id"]])
        self.t_i64b = S.op("v", lambda e: e.tensor_copy(
            out=self.i64two_bf, in_=self.i64two_sb), deps=[t["i2"]])
        self.t_i8b = S.op("v", lambda e: e.tensor_copy(
            out=self.i64rep8b, in_=self.i64rep8f), deps=[t["i8"]])
        self.t_bdm = S.op("v", lambda e: e.tensor_copy(
            out=self.bdm_bf, in_=self.bdm_f), deps=[t["bdm"]])
        self.t_odm = S.op("v", lambda e: e.tensor_scalar(
            out=self.odm_bf, in0=self.ident_sb, scalar1=-1.0, scalar2=1.0,
            op0=ALU.mult, op1=ALU.add), deps=[t["id"]])
        self.t_pbz = [S.op("v", (lambda b_: lambda e: e.memset(
            self.pb[b_], 0.0))(b)) for b in range(8)]
        self.pt = t
        self.ln_m = sb("ln_m", (128, 1))
        self.ln_r = sb("ln_r", (128, 1))
        self.ln_t1 = sb("ln_t1", (128, 1))
        self.ln_t2 = sb("ln_t2", (128, 1))
        self.ln_xs = sb("ln_xs", (128, D))
        self.L_all = sb("L_all", (H, NT * C))
        self.LT = [sb(f"LT{i}", (128, H)) for i in range(NT)]
        self.expL = [sb(f"expL{i}", (128, H)) for i in range(NT)]
        self.b_sig = [sb(f"bsig{i}", (128, H)) for i in range(NT)]
        self.rnk_t = [sb(f"rnk{i}", (128, H)) for i in range(NT)]
        self.rnq_t = [sb(f"rnq{i}", (128, H)) for i in range(NT)]
        self.gcrep = sb("gcrep", (128, NT * H))
        self.gcp = sb("gcp", (128, NT * 8))
        self.zeroHC = sb("zeroHC", (H, C))
        self.t_zero = S.op("v", lambda e: e.memset(self.zeroHC, 0.0))
        self.Ot = [sb(f"Ot{p}", (128, TQ), BF16) for p in range(H // 2)]
        self.stloc = [sb(f"stloc{p}", (128, DH), BF16) for p in range(H // 2)]
        self.prun = [sb(f"prun{p}", (128, DH), BF16) for p in range(H // 2)]
        self.prunT = [sb(f"prunT{p}", (128, DH), BF16) for p in range(H // 2)]
        self.prun_store = sb("prun_store", (128, NT * (H // 2) * DH), BF16)
        self.rep = {nm: sb(f"rep_{nm}", (128, D))
                    for nm in ("sa", "ca", "ga", "sb2", "cb", "gb")}
        self.x_t = sb("x_t", (128, D))
        self.h_bf = sb("h_bf", (128, D), BF16)
        self.negLT = sb("negLT", (128, H))
        self.fdt = sb("fdt", (H, C))
        self.sc16c = sb("sc16c", (128, H))
        self.lcc = sb("lcc", (H, NT))
        self.lcct = sb("lcct", (NT, H))

    def set_qscr(self, width, dual=False):
        arF, arB = self.arF, self.arB

        def mk():
            return (arF.f32(128, width // 16), arF.f32(128, width // 16),
                    arF.f32(128, width), arB.bf16(128, width),
                    arB.bf16(128, width), arB.bf16(128, width))
        self.qscr_by_eng = {"v": mk()}
        self.qscr_by_eng["g"] = mk() if dual else self.qscr_by_eng["v"]

    def prologue(self):
        S, io, sb, pb = self.S, self.io, self.sb, self.pb
        arF, arB = self.arF, self.arB
        c_sb = arF.f32(B, D)
        mod_sb = self.arB.f32(B, 6 * D)
        sc_sb = arB.bf16(B, D)
        sct_sb = arB.bf16(128, 2 * NKC)
        wm_sb = [arB.bf16(128, 3 * D) for _ in range(2)]
        lc0 = S.dma("s", lambda e: e.dma_start(out=c_sb, in_=io["c_in"]))
        tsc = S.op("a", lambda e: e.activation(out=sc_sb, in_=c_sb,
                   func=AF.Silu), deps=[lc0])
        ps_t = pb[7][:, 0:8].bitcast(BF16)
        prev = None
        for k in range(NKC):
            tk = S.op("t", (lambda k_: lambda e: e.transpose(
                ps_t[:, 0:2], sc_sb[:, 128 * k_:128 * (k_ + 1)],
                self.ident_bf[0:2, 0:2]))(k),
                deps=[tsc, self.t_idb, self.t_pbz[7]] +
                     ([prev] if prev else []))
            prev = S.op("v", (lambda k_: lambda e: e.tensor_copy(
                out=sct_sb[:, 2 * k_:2 * k_ + 2], in_=ps_t[:, 0:2]))(k),
                deps=[tk])
        sct_tok = prev
        wm_tok = [None, None]
        mod_evacs = []
        for half in range(2):
            mm_last = [None] * 6
            prev_evacs = list(mod_evacs)
            for k in range(NKC):
                buf = k % 2
                lw = S.gdma((lambda k_, h_, b_: lambda e: e.dma_start(
                    out=wm_sb[b_],
                    in_=io["wmod"][128 * k_:128 * (k_ + 1),
                                   3072 * h_:3072 * (h_ + 1)]))(k, half, buf),
                    deps=[wm_tok[buf]] if wm_tok[buf] else [])
                for n in range(6):
                    deps = [lw, sct_tok] + (
                        prev_evacs + [self.t_pbz[n]] if k == 0 else [])
                    if mm_last[n]:
                        deps.append(mm_last[n])
                    mm_last[n] = S.op("t", (lambda k_, n_, b_: lambda e:
                        e.matmul(pb[n_][0:B, :],
                                 sct_sb[:, 2 * k_:2 * k_ + 2],
                                 wm_sb[b_][:, 512 * n_:512 * (n_ + 1)],
                                 start=(k_ == 0), stop=(k_ == NKC - 1)))(k, n, buf),
                        deps=deps)
                wm_tok[buf] = mm_last[5]
            mod_evacs = []
            for n in range(6):
                mod_evacs.append(S.op("v", (lambda h_, n_: lambda e:
                    e.tensor_copy(
                        out=mod_sb[:, 3072 * h_ + 512 * n_:
                                   3072 * h_ + 512 * (n_ + 1)],
                        in_=pb[n_][0:B, :]))(half, n), deps=[mm_last[n]]))
        names = ["sa", "ca", "ga", "sb2", "cb", "gb"]
        rep_toks = {}
        last = None
        for i, nm in enumerate(names):
            for n in range(2):
                mt = S.op("t", (lambda i_, n_: lambda e: e.matmul(
                    pb[6][:, 0:512], self.selF_sb,
                    mod_sb[:, D * i_ + 512 * n_:D * i_ + 512 * (n_ + 1)],
                    start=True, stop=True))(i, n),
                    deps=mod_evacs + [self.pt["sel"]] + ([last] if last else []))
                last = S.op("v", (lambda nm_, n_: lambda e: e.tensor_copy(
                    out=self.rep[nm_][:, 512 * n_:512 * (n_ + 1)],
                    in_=pb[6][:, 0:512]))(nm, n), deps=[mt])
            rep_toks[nm] = last
        self.t_ca1 = S.op("v", lambda e: e.tensor_scalar_add(
            out=self.rep["ca"], in0=self.rep["ca"], scalar1=1.0),
            deps=[rep_toks["ca"]])
        self.t_cb1 = S.op("v", lambda e: e.tensor_scalar_add(
            out=self.rep["cb"], in0=self.rep["cb"], scalar1=1.0),
            deps=[rep_toks["cb"]])
        sgn = arF.f32(128, D)
        gt = {}
        prev_g = None
        for nm in ("ga", "gb"):
            t1 = S.op("v", (lambda nm_: lambda e: e.tensor_scalar(
                out=sgn, in0=self.rep[nm_], scalar1=0.0, scalar2=2.0,
                op0=ALU.is_ge, op1=ALU.mult))(nm),
                deps=[rep_toks[nm]] + ([prev_g] if prev_g else []))
            t2 = S.op("v", lambda e: e.tensor_scalar_add(
                out=sgn, in0=sgn, scalar1=-1.0), deps=[t1])
            t3 = S.op("v", (lambda nm_: lambda e: e.tensor_mul(
                out=self.rep[nm_], in0=self.rep[nm_], in1=sgn))(nm),
                deps=[rep_toks[nm], t2])
            t4 = S.op("v", (lambda nm_: lambda e: e.tensor_scalar_max(
                out=self.rep[nm_], in0=self.rep[nm_], scalar1=0.25))(nm),
                deps=[t3])
            gt[nm] = S.op("v", (lambda nm_: lambda e: e.tensor_mul(
                out=self.rep[nm_], in0=self.rep[nm_], in1=sgn))(nm),
                deps=[t4])
            prev_g = gt[nm]
        self.t_ga = gt["ga"]
        self.t_gb = gt["gb"]

    def weights_attn(self):
        S, io = self.S, self.io
        wv = io["wall_dram"]
        self.w_wall = []
        for part in range(4):
            self.w_wall.append(S.gdma((lambda p_: lambda e: e.dma_start(
                out=wv[:, 768 * p_:768 * (p_ + 1), :]
                    .rearrange("c n k -> n c k"),
                in_=io["wqkv"][768 * p_:768 * (p_ + 1), :]))(part)))
        self.w_wall.append(S.gdma(lambda e: e.dma_start(
            out=wv[:, 3072:3088, :].rearrange("c n k -> n c k"),
            in_=io["wg"])))
        self.w_wall.append(S.gdma(lambda e: e.dma_start(
            out=wv[:, 3088:3104, :].rearrange("c n k -> n c k"),
            in_=io["wb"])))
        self.w_wout = []
        for part in range(2):
            self.w_wout.append(S.gdma((lambda p_: lambda e: e.dma_start(
                out=io["wout_dram"][:, 512 * p_:512 * (p_ + 1), :]
                    .rearrange("c n k -> n c k"),
                in_=io["wout"][512 * p_:512 * (p_ + 1), :]))(part)))

    def quant_seq(self, src, dst, toks, eng="v"):
        S = self.S
        p, n = src.shape[0], src.shape[1]
        nb = n // 16
        am0, r20, xn0, w10, w20, w30 = self.qscr_by_eng[eng]
        am, r2 = am0[0:p, 0:nb], r20[0:p, 0:nb]
        xn, w1, w2, w3 = (xn0[0:p, 0:n], w10[0:p, 0:n], w20[0:p, 0:n],
                          w30[0:p, 0:n])
        t0 = S.op("v", lambda e: e.tensor_reduce(
            out=am, in_=src.rearrange("p (a b) -> p a b", b=16),
            axis=AX.X, op=ALU.max, apply_absolute_value=True), deps=toks)
        t1 = S.op("v", lambda e: e.tensor_scalar(
            out=am, in0=am, scalar1=1.0 / 12.0, scalar2=0.5e-12,
            op0=ALU.mult, op1=ALU.max), deps=[t0])
        t2 = S.op("v", lambda e: e.reciprocal(out=r2, in_=am), deps=[t1])
        t3 = S.op(eng, lambda e: e.tensor_tensor(
            out=xn.rearrange("p (a b) -> p a b", b=16),
            in0=src.rearrange("p (a b) -> p a b", b=16),
            in1=r2.unsqueeze(2).broadcast_to([p, nb, 16]), op=ALU.mult),
            deps=[t2] + list(toks))
        t4 = S.op(eng, lambda e: e.tensor_scalar(
            out=w1, in0=xn, scalar1=MAGIC, scalar2=MAGIC,
            op0=ALU.add, op1=ALU.subtract), deps=[t3])
        t5 = S.op(eng, lambda e: e.tensor_scalar(
            out=w1, in0=w1, scalar1=4.0, scalar2=-4.0,
            op0=ALU.min, op1=ALU.max), deps=[t4])
        t6 = S.op(eng, lambda e: e.tensor_scalar(
            out=w2, in0=xn, scalar1=10.0, scalar2=2.0,
            op0=ALU.is_gt, op1=ALU.mult), deps=[t3])
        t7 = S.op(eng, lambda e: e.scalar_tensor_tensor(
            out=w2, in0=xn, scalar=7.0, in1=w2, op0=ALU.is_gt, op1=ALU.add),
            deps=[t6])
        t8 = S.op(eng, lambda e: e.scalar_tensor_tensor(
            out=w2, in0=xn, scalar=5.0, in1=w2, op0=ALU.is_gt, op1=ALU.add),
            deps=[t7])
        t9 = S.op(eng, lambda e: e.tensor_scalar(
            out=w3, in0=xn, scalar1=-10.0, scalar2=2.0,
            op0=ALU.is_lt, op1=ALU.mult), deps=[t3])
        t10 = S.op(eng, lambda e: e.scalar_tensor_tensor(
            out=w3, in0=xn, scalar=-7.0, in1=w3, op0=ALU.is_lt, op1=ALU.add),
            deps=[t9])
        t11 = S.op(eng, lambda e: e.scalar_tensor_tensor(
            out=w3, in0=xn, scalar=-5.0, in1=w3, op0=ALU.is_lt, op1=ALU.add),
            deps=[t10])
        tq = S.op(eng, lambda e: e.tensor_tensor(out=w2, in0=w2, in1=w3,
                  op=ALU.subtract), deps=[t8, t11])
        tq2 = S.op(eng, lambda e: e.scalar_tensor_tensor(
            out=w2, in0=w2, scalar=2.0, in1=w1, op0=ALU.mult, op1=ALU.add),
            deps=[tq, t5])
        tf = S.op(eng, lambda e: e.tensor_tensor(
            out=dst.rearrange("p (a b) -> p a b", b=16),
            in0=w2.rearrange("p (a b) -> p a b", b=16),
            in1=am.unsqueeze(2).broadcast_to([p, nb, 16]), op=ALU.mult),
            deps=[tq2, t1])
        self._q_src = [t0, t3]
        return tf

    def ffn_weight_quant(self):
        S, io, pb = self.S, self.io, self.pb
        arB = self.arB
        self.set_qscr(IF // NCORE)
        gu_out = arB.bf16(128, 2 * IF)
        src_t = self.arF.f32(128, IF // NCORE)
        dq_t = arB.bf16(128, IF // NCORE)
        wdT_a = arB.bf16(128, D)
        wdT_b = arB.bf16(128, D)
        wdT_c = arB.bf16(96, D)
        ps = pb[7][:, 256:320].bitcast(BF16)
        toks1 = []
        prev_q = None
        ntile = IF // 128
        for which, wsrc in (("g", io["wgate_sh"]), ("u", io["wup_sh"])):
            for r in range(ntile):
                lt = S.dma("s", (lambda w_, r_: lambda e: e.dma_start(
                    out=src_t[:, 0:128],
                    in_=w_[128 * r_:128 * (r_ + 1), :]))(wsrc, r),
                    deps=[prev_q] if prev_q else [])
                qt = self.quant_seq(src_t[:, 0:128], dq_t[:, 0:128], [lt])
                off = (0 if which == "g" else IF) + 128 * r
                tt = S.op("t", (lambda: lambda e: e.transpose(
                    ps, dq_t[:, 0:128], self.ident_bf))(),
                    deps=[qt, self.t_idb, self.t_pbz[7]] +
                         ([prev_q] if prev_q else []))
                prev_q = S.op("v", (lambda off_: lambda e: e.tensor_copy(
                    out=gu_out[:, off_:off_ + 128], in_=ps))(off), deps=[tt])
                toks1.append(prev_q)
        st1 = S.dma("s", lambda e: e.dma_start(out=io["agw_in"], in_=gu_out),
                    deps=toks1)
        self.ag1 = S.cop(lambda e: e.collective_compute(
            "AllGather", ALU.bypass, replica_groups=[list(range(NCORE))],
            ins=[io["agw_in"].opt()], outs=[io["agw_out"].opt()]), deps=[st1])

        toks2 = []
        prev_q = None
        for r in range(NKC):
            lt = S.dma("s", (lambda r_: lambda e: e.dma_start(
                out=src_t, in_=io["wdown_sh"][128 * r_:128 * (r_ + 1), :]))(r),
                deps=[prev_q] if prev_q else [])
            qt = self.quant_seq(src_t, dq_t, [lt])
            for dst_t, c0, cw in ((wdT_a, 0, 128), (wdT_b, 128, 128),
                                  (wdT_c, 256, 96)):
                tt = S.op("t", (lambda c0_, cw_: lambda e: e.transpose(
                    ps[0:cw_, :], dq_t[:, c0_:c0_ + cw_],
                    self.ident_bf))(c0, cw),
                    deps=[qt, self.t_idb] + ([prev_q] if prev_q else []))
                prev_q = S.op("v", (lambda d_, r_, cw_: lambda e: e.tensor_copy(
                    out=d_[0:cw_, 128 * r_:128 * (r_ + 1)],
                    in_=ps[0:cw_, 0:128]))(dst_t, r, cw), deps=[tt])
                toks2.append(prev_q)
        st2 = [
            S.dma("s", lambda e: e.dma_start(out=io["agd_in"][0:128, :],
                  in_=wdT_a), deps=toks2),
            S.dma("s", lambda e: e.dma_start(out=io["agd_in"][128:256, :],
                  in_=wdT_b), deps=toks2),
            S.dma("s", lambda e: e.dma_start(out=io["agd_in"][256:352, :],
                  in_=wdT_c), deps=toks2),
        ]
        self.ag2 = S.cop(lambda e: e.collective_compute(
            "AllGather", ALU.bypass, replica_groups=[list(range(NCORE))],
            ins=[io["agd_in"].opt()], outs=[io["agd_out"].opt()]),
            deps=st2 + [self.ag1])

    def ln_mod(self, x_t, ca1, sa, out_t, deps):
        S = self.S
        m, rs, t1, t2, xs = (self.ln_m, self.ln_r, self.ln_t1, self.ln_t2,
                             self.ln_xs)
        ta = S.op("a", lambda e: e.activation(out=xs, in_=x_t, func=AF.Square),
                  deps=deps)
        t_m = S.op("v", lambda e: e.reduce_sum(out=m, in_=x_t, axis=AX.X),
                   deps=deps)
        t_q = S.op("v", lambda e: e.reduce_sum(out=t1, in_=xs, axis=AX.X),
                   deps=[ta])
        t_m2 = S.op("v", lambda e: e.tensor_scalar_mul(out=m, in0=m,
                    scalar1=1.0 / D), deps=[t_m])
        t_q2 = S.op("v", lambda e: e.tensor_scalar_mul(out=t1, in0=t1,
                    scalar1=1.0 / D), deps=[t_q])
        t_v = S.op("v", lambda e: e.tensor_mul(out=t2, in0=m, in1=m),
                   deps=[t_m2])
        t_v2 = S.op("v", lambda e: e.tensor_tensor(out=t1, in0=t1, in1=t2,
                    op=ALU.subtract), deps=[t_v, t_q2])
        t_v3 = S.op("v", lambda e: e.tensor_scalar_add(out=t1, in0=t1,
                    scalar1=1e-5), deps=[t_v2])
        t_sq = S.op("a", lambda e: e.activation(out=t2, in_=t1, func=AF.Sqrt),
                    deps=[t_v3])
        t_r = S.op("v", lambda e: e.reciprocal(out=rs, in_=t2), deps=[t_sq])
        t_n = S.op("v", lambda e: e.tensor_scalar(
            out=xs, in0=x_t, scalar1=m, scalar2=rs,
            op0=ALU.subtract, op1=ALU.mult), deps=[t_r, t_m2] + list(deps))
        t_h1 = S.op("v", lambda e: e.tensor_tensor(out=xs, in0=xs, in1=ca1,
                    op=ALU.mult), deps=[t_n, self.t_ca1, self.t_cb1])
        t_h2 = S.op("v", lambda e: e.tensor_tensor(out=out_t, in0=xs, in1=sa,
                    op=ALU.add), deps=[t_h1])
        return t_h2

    def hphase(self):
        S, io = self.S, self.io
        self.h_toks = []
        prev_store = None
        for i in range(NT):
            lx = S.dma("s", (lambda i_: lambda e: e.dma_start(
                out=self.x_t, in_=io["x_in"][128 * i_:128 * (i_ + 1), :]))(i),
                deps=[prev_store] if prev_store else [])
            th = self.ln_mod(self.x_t, self.rep["ca"], self.rep["sa"],
                             self.h_bf, [lx])
            prev_store = S.dma("s", (lambda i_: lambda e: e.dma_start(
                out=io["h_dram"][:, 128 * i_:128 * (i_ + 1), :]
                    .rearrange("c p k -> p c k"),
                in_=self.h_bf))(i), deps=[th])
            self.h_toks.append(prev_store)

    def qkvphase(self):
        S, io, pb = self.S, self.io, self.pb
        arB, arF = self.arB, self.arF
        wallT = [arB.bf16(128, 3104) for _ in range(NKC)]
        hT = [[arB.bf16(128, 512) for _ in range(NKC)] for _ in range(2)]
        var_bf = arB.bf16(128, D)
        q_raw = arF.f32(128, D)
        k_raw = arF.f32(128, D)
        sqs = arF.f32(128, D)
        ssq = arF.f32(128, H)
        ssk = arF.f32(128, H)
        zab = arF.f32(128, 2 * H)
        sab = arF.f32(128, 2 * H)
        la_t = arF.f32(128, H)
        laT = arF.f32(H, C)
        sc16 = arF.f32(128, H)
        wl = []
        for k in range(NKC):
            wl.append(S.dma("s", (lambda k_, d_: lambda e: e.dma_start(
                out=d_, in_=io["wall_dram"][k_], transpose=True))(k, wallT[k]),
                deps=self.w_wall))
        hl_tok = [[None] * NKC, [None] * NKC]
        store_toks = []
        evac_toks = []
        prev_lt = None
        prev_lt2 = None
        for i in range(NT):
            qtr, buf = i // 4, (i // 4) % 2
            if i % 4 == 0:
                for k in range(NKC):
                    hl_tok[buf][k] = S.dma("s", (lambda k_, d_, q_: lambda e:
                        e.dma_start(out=d_,
                                    in_=io["h_dram"][k_][512 * q_:512 * (q_ + 1), :],
                                    transpose=True))(k, hT[buf][k], qtr),
                        deps=[self.h_toks[4 * qtr + 3]] + store_toks +
                             ([hl_tok[buf][k]] if hl_tok[buf][k] else []))
            io_i = 128 * (i % 4)
            mm_ab = None
            mm_n = [None] * 6
            for k in range(NKC):
                for n in range(6):
                    deps = [wl[k], hl_tok[buf][k]]
                    if k == 0:
                        deps += evac_toks
                    if mm_n[n]:
                        deps.append(mm_n[n])
                    mm_n[n] = S.op("t", (lambda k_, n_, b2, o_: lambda e:
                        e.matmul(pb[n_], hT[b2][k_][:, o_:o_ + 128],
                                 wallT[k_][:, 512 * n_:512 * (n_ + 1)],
                                 start=(k_ == 0), stop=(k_ == NKC - 1)))(k, n, buf, io_i),
                        deps=deps)
                mm_ab = S.op("t", (lambda k_, b2, o_: lambda e: e.matmul(
                    pb[6][:, 0:32], hT[b2][k_][:, o_:o_ + 128],
                    wallT[k_][:, 3072:3104],
                    start=(k_ == 0), stop=(k_ == NKC - 1)))(k, buf, io_i),
                    deps=[wl[k], hl_tok[buf][k]] +
                         ([mm_ab] if mm_ab else evac_toks))
            tq0 = S.op("a", lambda e: e.copy(out=q_raw[:, 0:512],
                       in_=pb[0]), deps=[mm_n[0]])
            tq1 = S.op("a", lambda e: e.copy(out=q_raw[:, 512:1024],
                       in_=pb[1]), deps=[mm_n[1]])
            tk0 = S.op("a", lambda e: e.copy(out=k_raw[:, 0:512],
                       in_=pb[2]), deps=[mm_n[2]])
            tk1 = S.op("a", lambda e: e.copy(out=k_raw[:, 512:1024],
                       in_=pb[3]), deps=[mm_n[3]])
            tz = S.op("v", lambda e: e.tensor_tensor(
                out=zab, in0=self.bgbb_rep, in1=pb[6][:, 0:32], op=ALU.add),
                deps=[mm_ab, self.pt["bgr"], self.pt["bbr"]])
            ts_ = S.op("a", lambda e: e.activation(out=sab, in_=zab,
                       func=AF.Sigmoid), deps=[tz])
            tb = S.op("v", (lambda i_: lambda e: e.tensor_copy(
                out=self.b_sig[i_], in_=sab[:, H:2 * H]))(i), deps=[ts_])
            tcl = S.op("v", lambda e: e.tensor_scalar_max(
                out=la_t, in0=sab[:, 0:H], scalar1=1e-38), deps=[ts_])
            tla = S.op("a", lambda e: e.activation(out=la_t, in_=la_t,
                       func=AF.Ln), deps=[tcl])
            ttr = S.op("t", (lambda: lambda e: e.transpose(
                pb[7][0:H, 0:128], la_t, self.ident_sb))(),
                deps=[tla, self.pt["id"]] + ([prev_lt] if prev_lt else [])
                     + ([prev_lt2] if prev_lt2 else []))
            tev = S.op("v", lambda e: e.tensor_copy(out=laT,
                       in_=pb[7][0:H, 0:128]), deps=[ttr])
            tsc_ = S.op("v", (lambda i_: lambda e: e.tensor_tensor_scan(
                out=self.L_all[:, C * i_:C * (i_ + 1)], data0=laT,
                data1=self.zeroHC, initial=0.0, op0=ALU.add, op1=ALU.add))(i),
                deps=[tev, self.t_zero])
            tlt = S.op("t", (lambda i_: lambda e: e.transpose(
                pb[7][:, 128:128 + H], self.L_all[:, C * i_:C * (i_ + 1)],
                self.ident_sb[0:H, 0:H]))(i),
                deps=[tsc_] + ([prev_lt2] if prev_lt2 else []))
            tltv = S.op("v", (lambda i_: lambda e: e.tensor_copy(
                out=self.LT[i_], in_=pb[7][:, 128:128 + H]))(i), deps=[tlt])
            texp = S.op("a", (lambda i_: lambda e: e.activation(
                out=self.expL[i_], in_=self.LT[i_], func=AF.Exp))(i),
                deps=[tltv])
            prev_lt = tev
            prev_lt2 = tltv
            tsq = S.op("a", lambda e: e.activation(out=sqs, in_=q_raw,
                       func=AF.Square), deps=[tq0, tq1])
            trq = S.op("v", lambda e: e.tensor_reduce(
                out=ssq, in_=sqs.rearrange("p (h d) -> p h d", d=DH),
                axis=AX.X, op=ALU.add), deps=[tsq])
            tsk = S.op("a", lambda e: e.activation(out=sqs, in_=k_raw,
                       func=AF.Square), deps=[tk0, tk1, trq])
            trk = S.op("v", lambda e: e.tensor_reduce(
                out=ssk, in_=sqs.rearrange("p (h d) -> p h d", d=DH),
                axis=AX.X, op=ALU.add), deps=[tsk])
            tq2 = S.op("a", lambda e: e.activation(out=ssq, in_=ssq,
                       func=AF.Sqrt), deps=[trq])
            tq3 = S.op("v", lambda e: e.tensor_scalar_max(out=ssq, in0=ssq,
                       scalar1=1e-12), deps=[tq2])
            tq4 = S.op("v", (lambda i_: lambda e: e.reciprocal(
                out=self.rnq_t[i_], in_=ssq))(i), deps=[tq3])
            tk2 = S.op("a", lambda e: e.activation(out=ssk, in_=ssk,
                       func=AF.Sqrt), deps=[trk])
            tk3 = S.op("v", lambda e: e.tensor_scalar_max(out=ssk, in0=ssk,
                       scalar1=1e-12), deps=[tk2])
            tk4 = S.op("v", (lambda i_: lambda e: e.reciprocal(
                out=self.rnk_t[i_], in_=ssk))(i), deps=[tk3])

            def bc16(ap):
                return ap.unsqueeze(2).broadcast_to([128, H, DH])

            def vstore(dst, tok, i_=i):
                return S.dma("s", (lambda d_: lambda e: e.dma_start(
                    out=d_[:, 128 * i_:128 * (i_ + 1), :]
                        .rearrange("c p k -> p c k"),
                    in_=var_bf))(dst), deps=[tok])

            tv1 = S.op("v", (lambda i_: lambda e: e.tensor_tensor(
                out=var_bf.rearrange("p (h d) -> p h d", d=DH),
                in0=k_raw.rearrange("p (h d) -> p h d", d=DH),
                in1=bc16(self.rnk_t[i_]), op=ALU.mult))(i),
                deps=[tk4] + store_toks[4:5])
            s1 = vstore(io["kpl_dram"], tv1)
            tv2 = S.op("v", (lambda i_: lambda e: e.tensor_tensor(
                out=var_bf.rearrange("p (h d) -> p h d", d=DH),
                in0=var_bf.rearrange("p (h d) -> p h d", d=DH),
                in1=bc16(self.b_sig[i_]), op=ALU.mult))(i), deps=[tv1, tb, s1])
            s2 = vstore(io["kb_dram"], tv2)
            tv3 = S.op("v", (lambda i_: lambda e: e.tensor_tensor(
                out=var_bf.rearrange("p (h d) -> p h d", d=DH),
                in0=q_raw.rearrange("p (h d) -> p h d", d=DH),
                in1=bc16(self.rnq_t[i_]), op=ALU.mult))(i), deps=[tq4, s2])
            s3 = vstore(io["q_dram"], tv3)
            tg = S.op("v", (lambda i_: lambda e: e.tensor_tensor(
                out=sc16, in0=self.rnq_t[i_], in1=self.expL[i_],
                op=ALU.mult))(i), deps=[tq4, texp])
            tv4 = S.op("v", (lambda i_: lambda e: e.tensor_tensor(
                out=var_bf.rearrange("p (h d) -> p h d", d=DH),
                in0=q_raw.rearrange("p (h d) -> p h d", d=DH),
                in1=bc16(sc16), op=ALU.mult))(i), deps=[tg, s3])
            s4 = vstore(io["qg_dram"], tv4)
            tv5a = S.op("v", (lambda i_: lambda e: e.tensor_tensor(
                out=var_bf[:, 0:512].rearrange("p (h d) -> p h d", d=DH),
                in0=pb[4].rearrange("p (h d) -> p h d", d=DH),
                in1=self.b_sig[i_][:, 0:8].unsqueeze(2)
                    .broadcast_to([128, 8, DH]), op=ALU.mult))(i),
                deps=[mm_n[4], tb, s4])
            tv5b = S.op("v", (lambda i_: lambda e: e.tensor_tensor(
                out=var_bf[:, 512:1024].rearrange("p (h d) -> p h d", d=DH),
                in0=pb[5].rearrange("p (h d) -> p h d", d=DH),
                in1=self.b_sig[i_][:, 8:16].unsqueeze(2)
                    .broadcast_to([128, 8, DH]), op=ALU.mult))(i),
                deps=[mm_n[5], tb, tv5a])
            s5 = vstore(io["bv_dram"], tv5b)
            store_toks = [s1, s2, s3, s4, s5, tq0, tq1, tk0, tk1, tz,
                          tv5a, tv5b]
            evac_toks = [tq0, tq1, tk0, tk1, tz, tv5a, tv5b]
        self.qkv_last = list(store_toks)

        lst = []
        vtok = S.etok("v")
        for i in range(NT):
            lst.append(S.dma("s", (lambda i_: lambda e: e.dma_start(
                out=io["l_dram"][i_:i_ + 1, :]
                    .rearrange("x (h c) -> (x h) c", h=H),
                in_=self.L_all[:, C * i_:C * (i_ + 1)]))(i), deps=[vtok]))
        self.l_store = lst
        t1 = S.op("v", lambda e: e.tensor_copy(
            out=self.lcc, in_=self.L_all[:, C - 1::C]), deps=[vtok])
        t2 = S.op("t", lambda e: e.transpose(pb[7][0:NT, 256:256 + H],
                  self.lcc, self.ident_sb[0:H, 0:H]), deps=[t1, prev_lt2])
        t3 = S.op("v", lambda e: e.tensor_copy(out=self.lcct,
                  in_=pb[7][0:NT, 256:256 + H]), deps=[t2])
        t4 = S.dma("s", lambda e: e.dma_start(
            out=io["lc_dram"].rearrange("x (c h) -> (x c) h", h=H),
            in_=self.lcct), deps=[t3])
        t5 = S.dma("s", lambda e: e.dma_start(
            out=self.gcrep, in_=io["lc_dram"].partition_broadcast(128)),
            deps=[t4])
        self.t_gc = S.op("a", lambda e: e.activation(
            out=self.gcrep, in_=self.gcrep, func=AF.Exp), deps=[t5])
        gv = self.gcrep.rearrange("r (c p two) -> r c p two", p=8, two=2)
        gpv = self.gcp.rearrange("r (c p) -> r c p", p=8)
        tg1 = S.op("v", lambda e: e.tensor_copy(
            out=gpv[0:64], in_=gv[0:64, :, :, 0]), deps=[self.t_gc])
        self.t_gcp = S.op("v", lambda e: e.tensor_copy(
            out=gpv[64:128], in_=gv[64:128, :, :, 1]), deps=[tg1])

    def chunkphase(self):
        S, io, pb = self.S, self.io, self.pb
        arB, arF = self.arB, self.arF
        TQW = 256
        kT = [arB.bf16(128, TQW) for _ in range(NKC)]
        kbT = [arB.bf16(128, TQW) for _ in range(NKC)]
        qT = [arB.bf16(128, TQW) for _ in range(NKC)]
        qgT = [arB.bf16(128, TQW) for _ in range(NKC)]
        lrep = [arF.f32(128, H * C) for _ in range(2)]
        ein = arF.f32(128, H * C)
        e_ts = arB.bf16(128, H * C)
        e_sts = arB.bf16(128, H * C)
        t_full = arB.bf16(128, H * C)
        that_full = arB.bf16(128, H * C)
        rz_all = arB.bf16(128, H * 192)
        w_all = arB.bf16(128, H * 128)
        kpl_tm = arB.bf16(128, D)
        pw_ts = [arB.bf16(128, 1024) for _ in range(2)]
        pw_st = [arB.bf16(128, 1024) for _ in range(4)]
        g_all = arB.bf16(128, 1024)
        that_bd = arB.bf16(128, 1024)
        dg_scr = arB.bf16(128, 4 * C)
        z_all = rz_all
        qpt_sb = arB.bf16(128, 4 * C)
        tmp_odd = arB.bf16(128, 4 * C)
        bc_sb = arB.bf16(128, 4 * DH)
        bct_sb = arB.bf16(128, 4 * DH)
        mcs = arB.bf16(128, 4 * DH)
        self.chunk_tiles = dict(rz=rz_all, w=w_all, z=z_all, qpt=qpt_sb,
                                tmp=tmp_odd, bc=bc_sb, bct=bct_sb, mcs=mcs)

        rzv = rz_all.rearrange("p (h x) -> p h x", x=192)
        wav = w_all.rearrange("p (h x) -> p h x", x=128)

        zt = [self.t_i64b]
        for p in range(H // 2):
            zt.append(S.op("v", (lambda p_: lambda e: e.memset(
                self.stloc[p_], 0.0))(p)))
            zt.append(S.op("v", (lambda p_: lambda e: e.tensor_copy(
                out=self.prun[p_], in_=self.i64two_bf))(p), deps=[self.t_i64b]))
            zt.append(S.op("v", (lambda p_: lambda e: e.tensor_copy(
                out=self.prunT[p_], in_=self.i64two_bf))(p), deps=[self.t_i64b]))
        state_tok = zt

        ld_toks = []
        lr_tok = [None, None]
        prev_chunk_done = []
        prev_prods = []

        def emit_E(c, pre):
            pp = c % 2
            tl = S.dma("s", (lambda c_, p_: lambda e: e.dma_start(
                out=lrep[p_], in_=io["l_dram"][c_:c_ + 1, :]
                .partition_broadcast(128)))(c, pp),
                deps=[self.l_store[c]] + ([lr_tok[pp]] if lr_tok[pp] else []))
            emask = [(self.mask_ts_sb, e_ts, -1.0),
                     (self.mask_sts_sb, e_sts, 1.0)]
            ex_toks = []
            prev_ex = list(pre)
            last_ein = None
            for mask, edst, scl in emask:
                ein_toks = []
                for h in range(H):
                    ein_toks.append(S.op("v", (lambda c_, h_, p_, m_: lambda e:
                        e.scalar_tensor_tensor(
                            out=ein[:, C * h_:C * (h_ + 1)],
                            in0=lrep[p_][:, C * h_:C * (h_ + 1)],
                            scalar=self.LT[c_][:, h_:h_ + 1],
                            in1=m_, op0=ALU.subtract, op1=ALU.add))(c, h, pp, mask),
                        deps=[tl, self.pt["mts"], self.pt["msts"],
                              self.pt["msti"]] + prev_ex))
                ex = S.op("a", (lambda ed_, sc_: lambda e: e.activation(
                    out=ed_, in_=ein, func=AF.Exp, scale=sc_))(edst, scl),
                    deps=ein_toks)
                prev_ex = [ex]
                ex_toks.append(ex)
                last_ein = ein_toks[-1]
            lr_tok[pp] = last_ein
            return ex_toks

        ex_cache = {0: emit_E(0, [])}
        for c in range(NT):
            grp, cg = c // 2, c % 2
            if cg == 0:
                ld_toks = []
                for kk in range(NKC):
                    for dst, src in ((kT, "kpl_dram"), (kbT, "kb_dram"),
                                     (qT, "q_dram"), (qgT, "qg_dram")):
                        ld_toks.append(S.dma("s", (lambda kk_, d_, s_, g_:
                            lambda e: e.dma_start(
                                out=d_,
                                in_=io[s_][kk_][TQW * g_:TQW * (g_ + 1), :],
                                transpose=True))(kk, dst[kk], src, grp),
                            deps=self.qkv_last + prev_chunk_done))
            lkp = S.dma("s", (lambda c_: lambda e: e.dma_start(
                out=kpl_tm, in_=io["kpl_dram"][:, 128 * c_:128 * (c_ + 1), :]
                .rearrange("k p x -> p k x")))(c),
                deps=self.qkv_last + prev_chunk_done)
            wav4 = w_all.rearrange("p (hp two x) -> p hp two x", two=2,
                                   x=128)
            lbv = []
            for two in range(2):
                lbv.append(S.dma("s", (lambda c_, t2: lambda e: e.dma_start(
                    out=wav4[:, :, t2, 0:64],
                    in_=io["bv_dram"][:, 128 * c_:128 * (c_ + 1),
                                      64 * t2:64 * (t2 + 1)]
                    .rearrange("k p d -> p k d")))(c, two),
                    deps=self.qkv_last + prev_chunk_done))
            ex1, ex2 = ex_cache.pop(c)

            tfd = S.op("a", (lambda c_: lambda e: e.activation(
                out=self.fdt, in_=self.L_all[:, C * c_:C * (c_ + 1)],
                func=AF.Exp, scale=-1.0,
                bias=self.L_all[:, C * (c_ + 1) - 1:C * (c_ + 1)]))(c),
                deps=[self.l_store[c]] + prev_chunk_done)
            ttd = S.op("t", lambda e: e.transpose(
                pb[7][:, 256:256 + H], self.fdt, self.ident_sb[0:H, 0:H]),
                deps=[tfd] + prev_chunk_done)
            tfdv = S.op("v", lambda e: e.tensor_copy(out=self.sc16c,
                        in_=pb[7][:, 256:256 + H]), deps=[ttd])
            tkd = S.op("v", lambda e: e.tensor_tensor(
                out=rzv[:, :, 128:192],
                in0=kpl_tm.rearrange("p (h d) -> p h d", d=DH),
                in1=self.sc16c.unsqueeze(2).broadcast_to([128, H, DH]),
                op=ALU.mult), deps=[tfdv, lkp] + prev_chunk_done)
            tg2 = S.op("v", (lambda c_: lambda e: e.tensor_tensor(
                out=self.sc16c, in0=self.b_sig[c_], in1=self.expL[c_],
                op=ALU.mult))(c), deps=[tkd])
            tkbg = S.op("v", lambda e: e.tensor_tensor(
                out=wav[:, :, 64:128],
                in0=kpl_tm.rearrange("p (h d) -> p h d", d=DH),
                in1=self.sc16c.unsqueeze(2).broadcast_to([128, H, DH]),
                op=ALU.mult), deps=[tg2, lkp] + prev_chunk_done)

            prod_toks = []
            prev_fix = []
            ev4 = lambda ap: ap.rearrange("p (g two c) -> p g two c",
                                          two=2, c=C)
            rz4 = rz_all.rearrange("p (g two x) -> p g two x", two=2, x=192)
            pbv = [bk.rearrange("p (k one c) -> p k one c", one=1, c=C)
                   for bk in pb]
            for wv in range(2):
                mmt = []
                for hh in range(8):
                    h = 8 * wv + hh
                    r0 = 64 * (h % 2)
                    csl = slice(128 * cg, 128 * (cg + 1))
                    kt_sl = kT[h // 2][r0:r0 + 64, csl]
                    kbt_sl = kbT[h // 2][r0:r0 + 64, csl]
                    qt_sl = qT[h // 2][r0:r0 + 64, csl]
                    col = 128 * (hh // 2)
                    bA = pb[hh % 2]
                    bB = pb[2 + hh % 2]
                    bC = pb[4 + hh % 2]
                    dps = ld_toks + (list(prod_toks) if prod_toks else
                                     list(prev_chunk_done))
                    mmt.append(S.op("t", (lambda a, b_, o, co: lambda e:
                        e.matmul(o[:, co:co + 128], a, b_, start=True,
                                 stop=True))(kt_sl, kbt_sl, bA, col), deps=dps))
                    mmt.append(S.op("t", (lambda a, b_, o, co: lambda e:
                        e.matmul(o[:, co:co + 128], a, b_, start=True,
                                 stop=True))(kbt_sl, kt_sl, bB, col), deps=dps))
                    mmt.append(S.op("t", (lambda a, b_, o, co: lambda e:
                        e.matmul(o[:, co:co + 128], a, b_, start=True,
                                 stop=True))(kt_sl, qt_sl, bC, col), deps=dps))
                gsl = slice(4 * wv, 4 * wv + 4)
                for par in range(2):
                    psl = slice(par, par + 1)
                    pt1 = S.op("v", (lambda g_, p_, pa: lambda e:
                        e.tensor_tensor(
                            out=ev4(that_full)[:, g_, p_, :],
                            in0=ev4(e_sts)[:, g_, p_, :],
                            in1=pbv[pa][:, :, :, :],
                            op=ALU.mult))(gsl, psl, par),
                        deps=mmt + [ex2])
                    pt2 = S.op("v", (lambda g_, p_, pa: lambda e:
                        e.tensor_tensor(
                            out=ev4(t_full)[:, g_, p_, :],
                            in0=ev4(e_ts)[:, g_, p_, :],
                            in1=pbv[2 + pa][:, :, :, :],
                            op=ALU.mult))(gsl, psl, par),
                        deps=mmt + [ex1])
                    pt3 = S.op("v", (lambda g_, p_, pa: lambda e:
                        e.tensor_tensor(
                            out=rz4[:, g_, p_, 0:128],
                            in0=ev4(e_sts)[:, g_, p_, :],
                            in1=pbv[4 + pa][:, :, :, :],
                            op=ALU.mult))(gsl, psl, par),
                        deps=mmt + [ex2])
                    dg4 = dg_scr.rearrange("p (k one c) -> p k one c",
                                           one=1, c=C)
                    dg = S.op("v", (lambda pa: lambda e: e.tensor_tensor(
                        out=dg4, in0=pbv[4 + pa][:, :, :, :],
                        in1=self.ident_bf.unsqueeze(1).unsqueeze(1)
                            .broadcast_to([128, 4, 1, C]),
                        op=ALU.mult))(par),
                        deps=mmt + [self.t_idb] + prev_fix)
                    fx = S.op("v", (lambda g_, p_: lambda e: e.tensor_tensor(
                        out=rz4[:, g_, p_, 0:128],
                        in0=rz4[:, g_, p_, 0:128],
                        in1=dg4, op=ALU.add))(gsl, psl),
                        deps=[dg, pt3])
                    prev_fix = [fx]
                    prod_toks += [pt1, pt2, pt3, fx]
            if CHUNK_PARTS < 2:
                prev_chunk_done = [self.S.op("v", lambda e: e.engine_nop(),
                                   deps=prod_toks + [tkd, tkbg] + lbv)]
                state_tok = prev_chunk_done
                continue
            prev_prods = list(prod_toks)
            if c + 1 < NT:
                ex_cache[c + 1] = emit_E(c + 1, prev_prods)
            self._cur_c = c
            pre_solve = prod_toks + [tkd]
            td = self.dbgout("tdump", (NT, 128, H * C), BF16)
            rzd = self.dbgout("rzdump", (NT, 128, H * 192), BF16)
            if td is not None:
                pre_solve.append(S.dma("s", (lambda c_: lambda e: e.dma_start(
                    out=td[c_], in_=t_full))(c), deps=prod_toks))
                pre_solve.append(S.dma("s", (lambda c_: lambda e: e.dma_start(
                    out=rzd[c_], in_=rz_all))(c), deps=prod_toks + [tkd]))
            z_toks = self.solve_both(t_full, that_full, rz_all, pw_ts, pw_st,
                                     g_all, that_bd, z_all,
                                     pre_solve)
            zd = self.dbgout("zdump", (NT, 128, H * 192), BF16)
            if zd is not None:
                z_toks = z_toks + [S.dma("s", (lambda c_: lambda e:
                    e.dma_start(out=zd[c_], in_=z_all))(c), deps=z_toks)]
            if CHUNK_PARTS < 3:
                prev_chunk_done = [self.S.op("v", lambda e: e.engine_nop(),
                                   deps=z_toks + [tkbg] + lbv)]
                state_tok = prev_chunk_done
                continue
            prev_chunk_done = self.g2_state(
                c, z_all, w_all, qgT, qpt_sb, tmp_odd, bc_sb, bct_sb, mcs,
                z_toks, lbv + [tkbg], state_tok, cg)
            state_tok = prev_chunk_done

        sts = []
        for p in range(H // 2):
            for par in range(2):
                h = 2 * p + par
                sts.append(S.dma("s", (lambda p_, par_, h_: lambda e:
                    e.dma_start(
                        out=io["ags_in"][:, 2 * DH * h_:2 * DH * h_ + DH],
                        in_=self.prun[p_][64 * par_:64 * par_ + 64, :]))(p, par, h),
                    deps=prev_chunk_done))
                sts.append(S.dma("s", (lambda p_, par_, h_: lambda e:
                    e.dma_start(
                        out=io["ags_in"][:, 2 * DH * h_ + DH:2 * DH * (h_ + 1)],
                        in_=self.stloc[p_][64 * par_:64 * par_ + 64, :]))(p, par, h),
                    deps=prev_chunk_done))
        self.ag3 = S.cop(lambda e: e.collective_compute(
            "AllGather", ALU.bypass, replica_groups=[list(range(NCORE))],
            ins=[io["ags_in"].opt()], outs=[io["ags_out"].opt()]),
            deps=sts + [self.ag2])

    def solve_both(self, t_full, that_full, rz_all, pw_ts, pw_st,
                   g_all, that_bd, z_all, pre_toks):
        S, pb = self.S, self.pb
        rzv = rz_all.rearrange("p (h x) -> p h x", x=192)
        zv = z_all.rearrange("p (h x) -> p h x", x=192)
        wave_tok = []
        for wv in range(2):
            hs = [8 * wv + i for i in range(8)]
            tbd = S.op("v", (lambda w_: lambda e: e.tensor_tensor(
                out=that_bd.rearrange("p (g c) -> p g c", c=C),
                in0=that_full.rearrange("p (g c) -> p g c", c=C)
                    [:, 8 * w_:8 * w_ + 8, :],
                in1=self.bdm_bf.unsqueeze(1).broadcast_to([128, 8, C]),
                op=ALU.mult))(wv),
                deps=list(pre_toks) + [self.t_bdm] + wave_tok)
            sq_dep = list(pre_toks) + wave_tok
            for lv in range(5):
                mms_ts, mms_st = [], []
                for hh in range(8):
                    h = hs[hh]
                    bts = pb[6 + hh // 4]
                    bst = pb[4 + hh // 4]
                    co = 128 * (hh % 4)
                    if lv == 0:
                        for b_ in range(2):
                            r = slice(64 * b_, 64 * b_ + 64)
                            a_ts = t_full[r, C * h + 64 * b_:
                                          C * h + 64 * b_ + 64]
                            a_st = that_full[r, C * h + 64 * b_:
                                             C * h + 64 * b_ + 64]
                            mms_ts.append(S.op("t", (lambda a1, a2, b2, r_,
                                c_: lambda e: e.matmul(
                                    b2[r_, c_:c_ + 64], a1, a2, start=True,
                                    stop=True))(a_st, a_ts, bts, r,
                                                co + 64 * b_), deps=sq_dep))
                            mms_st.append(S.op("t", (lambda a1, a2, b2, r_,
                                c_: lambda e: e.matmul(
                                    b2[r_, c_:c_ + 64], a1, a2, start=True,
                                    stop=True))(a_ts, a_st, bst, r,
                                                co + 64 * b_), deps=sq_dep))
                    else:
                        src_ts = pw_ts[(lv - 1) % 2][:, 128 * hh:128 * hh + 128]
                        src_st = pw_st[lv - 1][:, 128 * hh:128 * hh + 128]
                        mms_ts.append(S.op("t", (lambda a1, a2, b2, c_:
                            lambda e: e.matmul(
                                b2[:, c_:c_ + 128], a1, a2, start=True,
                                stop=True))(src_st, src_ts, bts, co),
                            deps=sq_dep))
                        if lv < 4:
                            mms_st.append(S.op("t", (lambda a1, a2, b2, c_:
                                lambda e: e.matmul(
                                    b2[:, c_:c_ + 128], a1, a2, start=True,
                                    stop=True))(src_ts, src_st, bst, co),
                                deps=sq_dep))
                ev = []
                if lv == 0:
                    for half in range(2):
                        ev.append(S.op("v", (lambda hf: lambda e:
                            e.tensor_tensor(
                                out=pw_ts[0][:, 512 * hf:512 * hf + 512]
                                    .rearrange("p (k c) -> p k c", c=C),
                                in0=pb[6 + hf].rearrange(
                                    "p (k c) -> p k c", c=C),
                                in1=self.bdm_bf.unsqueeze(1)
                                    .broadcast_to([128, 4, C]),
                                op=ALU.mult))(half),
                            deps=mms_ts + [self.t_bdm]))
                        ev.append(S.op("v", (lambda hf: lambda e:
                            e.tensor_tensor(
                                out=pw_st[0][:, 512 * hf:512 * hf + 512]
                                    .rearrange("p (k c) -> p k c", c=C),
                                in0=pb[4 + hf].rearrange(
                                    "p (k c) -> p k c", c=C),
                                in1=self.bdm_bf.unsqueeze(1)
                                    .broadcast_to([128, 4, C]),
                                op=ALU.mult))(half),
                            deps=mms_st + [self.t_bdm]))
                else:
                    for half in range(2):
                        ev.append(S.op("a", (lambda hf, lv_: lambda e: e.copy(
                            out=pw_ts[lv_ % 2][:, 512 * hf:512 * hf + 512],
                            in_=pb[6 + hf]))(half, lv), deps=mms_ts))
                        if lv < 4:
                            ev.append(S.op("v", (lambda hf, lv_: lambda e:
                                e.tensor_copy(
                                    out=pw_st[lv_][:, 512 * hf:512 * hf + 512],
                                    in_=pb[4 + hf]))(half, lv), deps=mms_st))
                sq_dep = ev
            g0h = []
            for half in range(2):
                g0h.append(S.op("v", (lambda hf: lambda e: e.tensor_tensor(
                    out=g_all[:, 512 * hf:512 * hf + 512]
                        .rearrange("p (g c) -> p g c", c=C),
                    in0=pw_ts[0][:, 512 * hf:512 * hf + 512]
                        .rearrange("p (g c) -> p g c", c=C),
                    in1=self.ident_bf.unsqueeze(1).broadcast_to([128, 4, C]),
                    op=ALU.add))(half), deps=sq_dep + [self.t_idb]))
            gh = [[g0h[0]], [g0h[1]]]
            for lvk in (3, 2, 1, 0):
                for half in range(2):
                    mms = []
                    for hh in range(4 * half, 4 * half + 4):
                        co = 128 * (hh % 4)
                        mms.append(S.op("t", (lambda hh_, co_, lv_: lambda e:
                            e.matmul(pb[6 + hh_ // 4][:, co_:co_ + 128],
                                     pw_st[lv_][:, 128 * hh_:128 * hh_ + 128],
                                     g_all[:, 128 * hh_:128 * hh_ + 128],
                                     start=True, stop=True))(hh, co, lvk),
                            deps=gh[half]))
                    gh[half] = [S.op("v", (lambda hf: lambda e:
                        e.tensor_tensor(
                            out=g_all[:, 512 * hf:512 * hf + 512],
                            in0=g_all[:, 512 * hf:512 * hf + 512],
                            in1=pb[6 + hf], op=ALU.add))(half), deps=mms)]
            for half in range(2):
                mms = []
                for hh in range(4 * half, 4 * half + 4):
                    co = 128 * (hh % 4)
                    mms.append(S.op("t", (lambda hh_, co_: lambda e:
                        e.matmul(pb[6 + hh_ // 4][:, co_:co_ + 128],
                                 that_bd[:, 128 * hh_:128 * hh_ + 128],
                                 g_all[:, 128 * hh_:128 * hh_ + 128],
                                 start=True, stop=True))(hh, co),
                        deps=gh[half] + [tbd]))
                gh[half] = [S.op("v", (lambda hf: lambda e: e.tensor_tensor(
                    out=g_all[:, 512 * hf:512 * hf + 512],
                    in0=g_all[:, 512 * hf:512 * hf + 512],
                    in1=pb[6 + hf], op=ALU.subtract))(half), deps=mms)]
            gd = self.dbgout("gdump", (NT, 2, 128, 1024), BF16)
            if gd is not None:
                gtok = gtok + [S.dma("s", (lambda c_, w_: lambda e:
                    e.dma_start(out=gd[c_][w_], in_=g_all))(self._cur_c, wv),
                    deps=gtok)]
            bot = {}
            for hh in range(8):
                h = hs[hh]
                bank, co = 2 + hh // 2, 256 * (hh % 2)
                bot.setdefault(bank, []).append(S.op("t",
                    (lambda h_, bk, co_, hh_: lambda e: e.matmul(
                        self.pb[bk][64:128, co_:co_ + 192],
                        g_all[64:128, 128 * hh_ + 64:128 * hh_ + 128],
                        rzv[:, h_, :][64:128, :], start=True, stop=True))(
                        h, bank, co, hh),
                    deps=gh[hh // 4] + list(pre_toks)))
            evb = {}
            for bank in range(4):
                evb[2 + bank] = [S.op("a", (lambda bk, w_: lambda e: e.copy(
                    out=zv[:, 2 * bk + 8 * w_:2 * bk + 8 * w_ + 2, :][64:128],
                    in_=self.pb[2 + bk][64:128, :]
                        .rearrange("p (h x) -> p h x", x=256)[:, :, 0:192]))(
                        bank, wv), deps=bot[2 + bank])]
            off = {}
            for hh in range(8):
                h = hs[hh]
                bank, co = 2 + hh // 2, 256 * (hh % 2)
                off.setdefault(bank, []).append(S.op("t",
                    (lambda h_, bk, co_: lambda e: e.matmul(
                        self.pb[bk][0:64, co_:co_ + 192],
                        t_full[64:128, C * h_:C * h_ + 64],
                        zv[:, h_, :][64:128, :], start=True, stop=True))(
                        h, bank, co), deps=evb[bank]))
            evy = {}
            for bank in range(4):
                evy[2 + bank] = [S.op("v", (lambda bk, w_: lambda e:
                    e.tensor_tensor(
                    out=rzv[:, 2 * bk + 8 * w_:2 * bk + 8 * w_ + 2, :][0:64],
                    in0=rzv[:, 2 * bk + 8 * w_:2 * bk + 8 * w_ + 2, :][0:64],
                    in1=self.pb[2 + bk][0:64, :]
                        .rearrange("p (h x) -> p h x", x=256)[:, :, 0:192],
                    op=ALU.subtract))(bank, wv), deps=off[2 + bank])]
            top = {}
            for hh in range(8):
                h = hs[hh]
                bank, co = 2 + hh // 2, 256 * (hh % 2)
                top.setdefault(bank, []).append(S.op("t",
                    (lambda h_, bk, co_, hh_: lambda e: e.matmul(
                        self.pb[bk][0:64, co_:co_ + 192],
                        g_all[0:64, 128 * hh_:128 * hh_ + 64],
                        rzv[:, h_, :][0:64, :], start=True, stop=True))(
                        h, bank, co, hh), deps=evy[bank]))
            evt = []
            for bank in range(4):
                evt.append(S.op("a", (lambda bk, w_: lambda e: e.copy(
                    out=zv[:, 2 * bk + 8 * w_:2 * bk + 8 * w_ + 2, :][0:64],
                    in_=self.pb[2 + bk][0:64, :]
                        .rearrange("p (h x) -> p h x", x=256)[:, :, 0:192]))(
                        bank, wv), deps=top[2 + bank]))
            wave_tok = [S.op("v", lambda e: e.engine_nop(),
                            deps=evt + [t for v in evb.values() for t in v])]
        return wave_tok

    def g2_state(self, c, z_all, w_all, qgT, qpt_sb, tmp_odd, bc_sb, bct_sb,
                 mcs, z_toks, w_toks, state_tok, cg):
        S, pb, io = self.S, self.pb, self.io
        zv = z_all.rearrange("p (h x) -> p h x", x=192)
        wav = w_all.rearrange("p (h x) -> p h x", x=128)
        qptv = qpt_sb.rearrange("p (pr x) -> p pr x", x=C)
        mcsv = mcs.rearrange("p (j x) -> p j x", x=DH)
        bcv = bc_sb.rearrange("p (j x) -> p j x", x=DH)
        tmpv = tmp_odd.rearrange("p (j x) -> p j x", x=C)
        bctv = bct_sb.rearrange("p (j x) -> p j x", x=DH)
        wave_done = list(z_toks) + list(w_toks) + list(state_tok)
        all_done = []
        for W in range(2):
            prs = [4 * W + j for j in range(4)]
            s4 = []
            s1l = []
            for j, p in enumerate(prs):
                bA = pb[j]
                bB = pb[4 + j]
                s0 = []
                for par in range(2):
                    h = 2 * p + par
                    r = slice(64 * par, 64 * par + 64)
                    s0.append(S.op("t", (lambda h_, r_, b_: lambda e:
                        e.matmul(b_[r_, 0:192], wav[:, h_, 0:64],
                                 zv[:, h_, :], start=True, stop=True))(
                                 h, r, bA), deps=wave_done))
                    s0.append(S.op("t", (lambda h_, r_, b_: lambda e:
                        e.matmul(b_[r_, 192:384], wav[:, h_, 64:128],
                                 zv[:, h_, :], start=True, stop=True))(
                                 h, r, bA), deps=wave_done))
                s1 = S.op("v", (lambda p_, j_, b_, cg_: lambda e:
                    e.tensor_tensor(
                        out=qptv[:, j_, :],
                        in0=qgT[p_][:, 128 * cg_:128 * (cg_ + 1)],
                        in1=b_[:, 192:320], op=ALU.subtract))(
                        p, j, bA, cg), deps=s0)
                s1l.append(s1)
                s2 = S.dma("s", (lambda c_, p_, j_: lambda e: e.dma_start(
                    out=io["qpt_dram"][c_ * H + 2 * p_:c_ * H + 2 * p_ + 2, :]
                        .rearrange("two (d t) -> (two d) t", d=DH),
                    in_=qptv[:, j_, :]))(c, p, j), deps=[s1])
                s3 = []
                for par in range(2):
                    r = slice(64 * par, 64 * par + 64)
                    s3.append(S.op("t", (lambda p_, r_, j_, b_: lambda e:
                        e.matmul(b_[r_, 0:128],
                                 self.stloc[p_][r_, :], qptv[:, j_, :][r_, :],
                                 start=True, stop=True))(p, r, j, bB),
                        deps=[s1] + list(state_tok)))
                t1 = S.op("a", (lambda j_, b_: lambda e: e.copy(
                    out=tmpv[:, j_, :], in_=b_[:, 0:128]))(j, bB), deps=s3)
                tot = S.op("v", (lambda p_, j_, b_, c_: lambda e:
                    e.tensor_tensor(
                        out=self.Ot[p_][:, C * c_:C * (c_ + 1)],
                        in0=tmpv[:, j_, :], in1=b_[:, 0:128],
                        op=ALU.add))(p, j, bA, c), deps=[t1] + s0)
                ac = S.op("v", (lambda c_, p_, j_, b_: lambda e:
                    e.scalar_tensor_tensor(
                        out=mcsv[:, j_, :], in0=self.i64two_sb,
                        scalar=self.gcp[:, c_ * 8 + p_:c_ * 8 + p_ + 1],
                        in1=b_[:, 320:384], op0=ALU.mult,
                        op1=ALU.subtract))(c, p, j, bA),
                    deps=s0 + [s1, self.t_gcp])
                bcx = S.op("v", (lambda j_, b_: lambda e: e.tensor_copy(
                    out=bcv[:, j_, :], in_=b_[:, 128:192]))(
                    j, bA), deps=s0)
                spr = S.op("v", (lambda p_, c_: lambda e: e.tensor_copy(
                    out=self.prun_store[:, (c_ * (H // 2) + p_) * DH:
                                        (c_ * (H // 2) + p_ + 1) * DH],
                    in_=self.prun[p_]))(p, c), deps=list(state_tok))
                bcts = []
                for par in range(2):
                    r = slice(64 * par, 64 * par + 64)
                    bcts.append(S.op("t", (lambda r_, j_, b_: lambda e:
                        e.transpose(
                            b_[r_, 128:160].bitcast(BF16),
                            bcv[:, j_, :][r_, :], self.i64two_bf[r_, :]))(
                            r, j, bB), deps=[bcx, t1]))
                ebt = S.op("a", (lambda j_, b_: lambda e: e.copy(
                    out=bctv[:, j_, :], in_=b_[:, 128:160]
                    .bitcast(BF16)))(j, bB), deps=bcts)
                mstc = []
                for par in range(2):
                    r = slice(64 * par, 64 * par + 64)
                    mstc.append(S.op("t", (lambda p_, r_, j_, b_: lambda e:
                        e.matmul(b_[r_, 192:256],
                                 mcsv[:, j_, :][r_, :],
                                 self.stloc[p_][r_, :],
                                 start=True, stop=True))(p, r, j, bB),
                        deps=[ac, ebt] + list(state_tok) + s3))
                evst = S.op("v", (lambda p_, j_, b_: lambda e: e.tensor_tensor(
                    out=self.stloc[p_], in0=bctv[:, j_, :],
                    in1=b_[:, 192:256],
                    op=ALU.add))(p, j, bB), deps=mstc + [ebt])
                mprs = []
                for par in range(2):
                    r = slice(64 * par, 64 * par + 64)
                    mprs.append(S.op("t", (lambda p_, r_, j_, b_: lambda e:
                        e.matmul(b_[r_, 384:448],
                                 self.prunT[p_][r_, :], mcsv[:, j_, :][r_, :],
                                 start=True, stop=True))(p, r, j, bA),
                        deps=[ac, spr, tot, bcx]))
                    mprs.append(S.op("t", (lambda p_, r_, j_, b_: lambda e:
                        e.matmul(b_[r_, 448:512],
                                 mcsv[:, j_, :][r_, :], self.prunT[p_][r_, :],
                                 start=True, stop=True))(p, r, j, bA),
                        deps=[ac, spr, tot, bcx]))
                evpr = S.op("a", (lambda p_, j_, b_: lambda e: e.copy(
                    out=self.prun[p_], in_=b_[:, 384:448]))(
                    p, j, bA), deps=mprs + [spr])
                evprt = S.op("a", (lambda p_, j_, b_: lambda e: e.copy(
                    out=self.prunT[p_], in_=b_[:, 448:512]))(p, j, bA),
                    deps=mprs)
                s4 += [tot, evst, evpr, evprt, s2, ac, bcx, ebt]
            wave_done = s4 + s1l
            all_done += s4
        return [S.op("v", lambda e: e.engine_nop(), deps=all_done)]

    def corrphase(self):
        S, io, pb = self.S, self.io, self.pb
        arB, arF = self.arB, self.arF
        ragf = [arB.bf16(128, H * 2 * DH) for _ in range(2)]
        ragb = [arB.bf16(128, H * 2 * DH) for _ in range(2)]
        ipat_f = arF.f32(128, H * 2 * DH)
        sint = [arB.bf16(128, DH) for _ in range(H // 2)]
        mc = arB.bf16(128, DH)
        qptl = arB.bf16(128, 8 * C)
        qv = qptl.rearrange("p (pr t) -> p pr t", t=C)
        lip = S.dma("s", lambda e: e.dma_start(out=ipat_f, in_=io["ipat"]))
        zi = [S.op("v", (lambda p_: lambda e: e.memset(sint[p_], 0.0))(p))
              for p in range(H // 2)]
        ptoks = list(zi)
        rag_busy = [None, None]
        for j in range(NCORE):
            pp = j % 2
            lt = [S.dma("s", (lambda j_, p_: lambda e: e.dma_start(
                      out=ragf[p_][0:64, :],
                      in_=io["ags_out"][64 * j_:64 * (j_ + 1), :]))(j, pp),
                  deps=[self.ag3] + ([rag_busy[pp]] if rag_busy[pp] else [])),
                  S.dma("s", (lambda j_, p_: lambda e: e.dma_start(
                      out=ragf[p_][64:128, :],
                      in_=io["ags_out"][64 * j_:64 * (j_ + 1), :]))(j, pp),
                  deps=[self.ag3] + ([rag_busy[pp]] if rag_busy[pp] else []))]
            t1 = S.op("v", (lambda j_, p_: lambda e: e.tensor_scalar(
                out=ragf[p_], in0=ragf[p_],
                scalar1=self.hm_sb[:, j_:j_ + 1], scalar2=None,
                op0=ALU.mult))(j, pp), deps=lt + [self.pt["hm"]])
            t2 = S.op("v", (lambda j_, p_: lambda e: e.scalar_tensor_tensor(
                out=ragb[p_], in0=ipat_f,
                scalar=self.hm1_sb[:, j_:j_ + 1], in1=ragf[p_],
                op0=ALU.mult, op1=ALU.add))(j, pp),
                deps=[t1, lip, self.t_hm1])
            mms = []
            for p in range(H // 2):
                for par in range(2):
                    h = 2 * p + par
                    r = slice(64 * par, 64 * par + 64)
                    mms.append(S.op("t", (lambda p2, h_, p_, r_: lambda e:
                        e.matmul(
                            pb[0][r_, 64 * p_:64 * (p_ + 1)],
                            ragb[p2][r_, 2 * DH * h_:2 * DH * h_ + DH],
                            sint[p_][r_, :], start=True, stop=True))(pp, h, p, r),
                        deps=[t2] + ptoks))
            ntoks = []
            for p in range(H // 2):
                for par in range(2):
                    h = 2 * p + par
                    r = slice(64 * par, 64 * par + 64)
                    ntoks.append(S.op("v", (lambda p2, h_, p_, r_: lambda e:
                        e.tensor_tensor(
                            out=sint[p_][r_, :],
                            in0=ragb[p2][r_, 2 * DH * h_ + DH:2 * DH * (h_ + 1)],
                            in1=pb[0][r_, 64 * p_:64 * (p_ + 1)],
                            op=ALU.add))(pp, h, p, r),
                        deps=mms))
            rag_busy[pp] = ntoks[-1]
            ptoks = ntoks
        mcl = arB.bf16(128, 512)
        mclv = mcl.rearrange("p (pr x) -> p pr x", x=DH)
        prev = ptoks
        for c in range(NT):
            lq = S.dma("s", (lambda c_: lambda e: e.dma_start(
                out=qptl,
                in_=io["qpt_dram"][16 * c_:16 * (c_ + 1), :]
                .rearrange("(pr two) (d t) -> two d pr t", two=2, d=DH)))(c),
                deps=prev)
            mm1s = []
            for p in range(H // 2):
                for par in range(2):
                    r = slice(64 * par, 64 * par + 64)
                    mm1s.append(S.op("t", (lambda c_, p_, r_: lambda e:
                        e.matmul(
                            pb[1][r_, 64 * p_:64 * (p_ + 1)],
                            self.prun_store[r_, (c_ * (H // 2) + p_) * DH:
                                            (c_ * (H // 2) + p_ + 1) * DH],
                            sint[p_][r_, :], start=True, stop=True))(c, p, r),
                        deps=prev))
            evm = S.op("a", lambda e: e.copy(out=mcl, in_=pb[1]),
                       deps=mm1s)
            mm2s = []
            for p in range(H // 2):
                for par in range(2):
                    r = slice(64 * par, 64 * par + 64)
                    mm2s.append(S.op("t", (lambda p_, r_: lambda e: e.matmul(
                        pb[2 + p_ // 4][r_, 128 * (p_ % 4):128 * (p_ % 4) + 128],
                        mclv[:, p_, :][r_, :], qv[:, p_, :][r_, :],
                        start=True, stop=True))(p, r), deps=[evm, lq]))
            ntoks = []
            for p in range(H // 2):
                ntoks.append(S.op("v", (lambda c_, p_: lambda e:
                    e.tensor_tensor(
                        out=self.Ot[p_][:, C * c_:C * (c_ + 1)],
                        in0=self.Ot[p_][:, C * c_:C * (c_ + 1)],
                        in1=pb[2 + p_ // 4][:, 128 * (p_ % 4):
                                            128 * (p_ % 4) + 128],
                        op=ALU.add))(c, p), deps=mm2s))
            prev = ntoks
        self.corr_done = prev

    def woutphase(self):
        S, io, pb = self.S, self.io, self.pb
        arB = self.arB
        woutT = [arB.bf16(128, D) for _ in range(NKC)]
        wl = []
        for k in range(NKC):
            wl.append(S.dma("s", (lambda k_, d_: lambda e: e.dma_start(
                out=d_, in_=io["wout_dram"][k_], transpose=True))(k, woutT[k]),
                deps=self.w_wout))
        wf = []
        for k in range(NKC):
            wf.append(S.op("v", (lambda k_: lambda e: e.tensor_tensor(
                out=woutT[k_], in0=woutT[k_], in1=self.rep["ga"],
                op=ALU.mult))(k), deps=[wl[k], self.t_ga]))
        x_t = self.x_t
        x2_t = self.ln_xs
        x2_toks = []
        prev = list(self.corr_done)
        for i in range(NT):
            lx = S.dma("s", (lambda i_: lambda e: e.dma_start(
                out=x_t, in_=io["x_in"][128 * i_:128 * (i_ + 1), :]))(i),
                deps=prev)
            mms = [None, None]
            for k in range(NKC):
                for n in range(2):
                    dd = [wf[k]] + list(self.corr_done)
                    if mms[n]:
                        dd.append(mms[n])
                    if k == 0:
                        dd += x2_toks[-3:]
                    mms[n] = S.op("t", (lambda i_, k_, n_: lambda e: e.matmul(
                        pb[n_], self.Ot[k_][:, 128 * i_:128 * (i_ + 1)],
                        woutT[k_][:, 512 * n_:512 * (n_ + 1)],
                        start=(k_ == 0), stop=(k_ == NKC - 1)))(i, k, n),
                        deps=dd)
            ev = []
            for n in range(2):
                ev.append(S.op("v", (lambda n_: lambda e: e.tensor_tensor(
                    out=x2_t[:, 512 * n_:512 * (n_ + 1)],
                    in0=x_t[:, 512 * n_:512 * (n_ + 1)], in1=pb[n_],
                    op=ALU.add))(n), deps=[mms[n], lx]))
            st = S.dma("s", (lambda i_: lambda e: e.dma_start(
                out=io["x2_dram"][128 * i_:128 * (i_ + 1), :], in_=x2_t))(i),
                deps=ev)
            x2_toks = ev + [st]
            prev = [st]
        self.x2_done = prev

    def ffnphase(self):
        S, io, pb = self.S, self.io, self.pb
        arB, arF = self.arB, self.arF
        self.set_qscr(1408, dual=True)
        x_t = self.x_t
        silu_sb = arF.f32(128, 1408)
        h2f = silu_sb[:, 0:D]
        h2q = arB.bf16(128, D)
        qtoks = []
        prev = list(self.x2_done)
        for i in range(NT):
            lx = S.dma("s", (lambda i_: lambda e: e.dma_start(
                out=x_t, in_=io["x2_dram"][128 * i_:128 * (i_ + 1), :]))(i),
                deps=prev)
            th = self.ln_mod(x_t, self.rep["cb"], self.rep["sb2"], h2f, [lx])
            tq = self.quant_seq(h2f, h2q, [th] + prev, eng="v")
            st = S.dma("s", (lambda i_: lambda e: e.dma_start(
                out=io["h2q_dram"][:, 128 * i_:128 * (i_ + 1), :]
                    .rearrange("c p k -> p c k"), in_=h2q))(i), deps=[tq])
            qtoks.append(st)
            prev = [st, tq]
        h2qT_1 = [arB.bf16(128, 512) for _ in range(NKC)]
        h2qT = [h2qT_1, h2qT_1]
        wgu = [arB.bf16(128, 2816) for _ in range(NKC)]
        hm_f = arF.f32(128, 1408)
        hmq = arB.bf16(128, 1408)
        hql_tok = [[None] * NKC, [None] * NKC]
        gstore = []
        prev_readers = []
        prev_q = []
        prev_mult = []
        nsub = ((0, 512), (512, 512), (1024, 384))
        for G in range(2):
            wls = []
            for k in range(NKC):
                wls.append(S.dma("s", (lambda k_, G_, d_: lambda e: e.dma_start(
                    out=d_[:, 0:1408],
                    in_=io["agw_out"][128 * k_:128 * (k_ + 1),
                                      1408 * G_:1408 * (G_ + 1)]))(k, G, wgu[k]),
                    deps=[self.ag1] + gstore[-2:]))
                wls.append(S.dma("s", (lambda k_, G_, d_: lambda e: e.dma_start(
                    out=d_[:, 1408:2816],
                    in_=io["agw_out"][128 * k_:128 * (k_ + 1),
                                      IF + 1408 * G_:IF + 1408 * (G_ + 1)]))(k, G, wgu[k]),
                    deps=[self.ag1] + gstore[-2:]))
            for i in range(NT):
                qtr, buf = i // 4, (i // 4) % 2
                if i % 4 == 0:
                    for k in range(NKC):
                        hql_tok[buf][k] = S.dma("s", (lambda k_, d_, q_:
                            lambda e: e.dma_start(
                                out=d_,
                                in_=io["h2q_dram"][k_][512 * q_:512 * (q_ + 1), :],
                                transpose=True))(k, h2qT[buf][k], qtr),
                            deps=qtoks + gstore[-12:] +
                                 ([hql_tok[buf][k]] if hql_tok[buf][k] else []))
                io_i = 128 * (i % 4)
                mms = [None] * 6
                for k in range(NKC):
                    for n in range(6):
                        half_gu, n3 = divmod(n, 3)
                        src_c = 1408 * half_gu + nsub[n3][0]
                        w_ = nsub[n3][1]
                        dd = [wls[2 * k], wls[2 * k + 1], hql_tok[buf][k]]
                        if mms[n]:
                            dd.append(mms[n])
                        if k == 0:
                            dd += prev_readers
                        mms[n] = S.op("t", (lambda k_, n_, sc_, w2_, b2, o_:
                            lambda e: e.matmul(
                                pb[n_][:, 0:w2_],
                                h2qT[b2][k_][:, o_:o_ + 128],
                                wgu[k_][:, sc_:sc_ + w2_],
                                start=(k_ == 0), stop=(k_ == NKC - 1)))(
                                k, n, src_c, w_, buf, io_i), deps=dd)
                sil = []
                for n3, (c0, w_) in enumerate(nsub):
                    sil.append(S.op("a", (lambda n_, c0_, w2_: lambda e:
                        e.activation(out=silu_sb[:, c0_:c0_ + w2_],
                                     in_=pb[n_][:, 0:w2_], func=AF.Silu))(
                                     n3, c0, w_), deps=[mms[n3]] + prev_mult))
                mult = []
                for n3, (c0, w_) in enumerate(nsub):
                    mult.append(S.op("v", (lambda n_, c0_, w2_: lambda e:
                        e.tensor_tensor(
                            out=hm_f[:, c0_:c0_ + w2_],
                            in0=silu_sb[:, c0_:c0_ + w2_],
                            in1=pb[3 + n_][:, 0:w2_], op=ALU.mult))(n3, c0, w_),
                        deps=[sil[n3], mms[3 + n3]] + prev_q))
                qeng = "v"
                tq = self.quant_seq(hm_f, hmq, mult + prev_q, eng=qeng)
                prev_q = self._q_src
                st = S.dma("s", (lambda i_, G_: lambda e: e.dma_start(
                    out=io["hmid_dram"][11 * G_:11 * (G_ + 1),
                                        128 * i_:128 * (i_ + 1), :]
                        .rearrange("c p k -> p c k"), in_=hmq))(i, G),
                    deps=[tq])
                gstore.append(st)
                prev_q = prev_q + [st]
                prev_readers = sil + mult
                prev_mult = mult
        self.epoch()
        NIF = IF // 128
        wdnT = [arB.bf16(128, D) for _ in range(NIF)]
        wdl = []
        for k in range(NIF):
            wdl.append(S.dma("s", (lambda k_, d_: lambda e: e.dma_start(
                out=d_, in_=io["agd_out"][128 * k_:128 * (k_ + 1), :]))(k, wdnT[k]),
                deps=[self.ag2]))
        wdf = []
        for k in range(NIF):
            wdf.append(S.op("v", (lambda k_: lambda e: e.tensor_tensor(
                out=wdnT[k_], in0=wdnT[k_], in1=self.rep["gb"],
                op=ALU.mult))(k), deps=[wdl[k], self.t_gb]))
        hmT = [arB.bf16(128, 512) for _ in range(NIF)]
        out_toks = []
        prev_ev = []
        for qtr in range(NT // 4):
            htl = []
            for k in range(NIF):
                htl.append(S.dma("s", (lambda k_, d_, q_: lambda e: e.dma_start(
                    out=d_,
                    in_=io["hmid_dram"][k_][512 * q_:512 * (q_ + 1), :],
                    transpose=True))(k, hmT[k], qtr),
                    deps=gstore + out_toks[-3:]))
            for i2 in range(4):
                i = 4 * qtr + i2
                lx = S.dma("s", (lambda i_: lambda e: e.dma_start(
                    out=x_t, in_=io["x2_dram"][128 * i_:128 * (i_ + 1), :]))(i),
                    deps=out_toks[-1:] if out_toks else gstore[-1:])
                mms = [None, None]
                for k in range(NIF):
                    for n in range(2):
                        dd = [htl[k], wdf[k]]
                        if mms[n]:
                            dd.append(mms[n])
                        if k == 0:
                            dd += prev_ev
                        mms[n] = S.op("t", (lambda i2_, k_, n_: lambda e:
                            e.matmul(pb[6 + n_],
                                     hmT[k_][:, 128 * i2_:128 * (i2_ + 1)],
                                     wdnT[k_][:, 512 * n_:512 * (n_ + 1)],
                                     start=(k_ == 0), stop=(k_ == NIF - 1)))(
                                     i2, k, n), deps=dd)
                ev = []
                for n in range(2):
                    ev.append(S.op("v", (lambda n_: lambda e: e.tensor_tensor(
                        out=x_t[:, 512 * n_:512 * (n_ + 1)],
                        in0=x_t[:, 512 * n_:512 * (n_ + 1)], in1=pb[6 + n_],
                        op=ALU.add))(n), deps=[mms[n], lx]))
                st = S.dma("s", (lambda i_: lambda e: e.dma_start(
                    out=io["out_x"][128 * i_:128 * (i_ + 1), :], in_=x_t))(i),
                    deps=ev)
                out_toks.append(st)
                prev_ev = list(ev)
        self.out_toks = out_toks

    def finish(self):
        S = self.S
        S.op("v", lambda e: e.engine_nop(), deps=self.out_toks)
        S.op("s", lambda e: e.nop(), deps=self.out_toks)


_NC_CACHE = {}


def _get_nc(debug=()):
    key = tuple(sorted(debug))
    if key not in _NC_CACHE:
        _NC_CACHE[key] = build_nc(key)
    return _NC_CACHE[key]


def host_inputs(full):
    f32 = np.float32
    x = np.ascontiguousarray(full["L0_attn_x"], dtype=f32)
    s_idx = np.arange(128)[:, None]
    t_idx = np.arange(128)[None, :]
    mask_ts = np.where(s_idx > t_idx, 0.0, BIG).astype(f32)
    mask_sts = np.where(s_idx < t_idx, 0.0, -BIG).astype(f32)
    mask_sti = np.where(s_idx <= t_idx, 0.0, -BIG).astype(f32)
    i64 = np.eye(64, dtype=f32)
    i64two = np.concatenate([i64, i64], axis=0)
    i64rep8 = np.tile(i64two, (1, 8))
    ipat = np.zeros((128, H * 2 * DH), f32)
    for hh in range(H):
        ipat[0:64, 2 * DH * hh:2 * DH * hh + DH] = i64
        ipat[64:128, 2 * DH * hh:2 * DH * hh + DH] = i64
    shared = {
        "Wqkv": np.ascontiguousarray(full["L0_attn_Wqkv"], dtype=f32),
        "Wg": np.ascontiguousarray(full["L0_attn_Wg"], dtype=f32),
        "bg": np.ascontiguousarray(full["L0_attn_bg"], dtype=f32).reshape(1, H),
        "Wb": np.ascontiguousarray(full["L0_attn_Wb"], dtype=f32),
        "bb": np.ascontiguousarray(full["L0_attn_bb"], dtype=f32).reshape(1, H),
        "Wout": np.ascontiguousarray(full["L0_attn_Wout"], dtype=f32),
        "c_vec": np.ascontiguousarray(full["L0_c"], dtype=f32),
        "wmod": np.ascontiguousarray(full["L0_wmod"], dtype=f32),
        "ident128": np.eye(128, dtype=f32),
        "mask_ts": mask_ts, "mask_sts": mask_sts, "mask_sti": mask_sti,
        "i64two": i64two, "i64rep8": i64rep8, "ipat": ipat,
        "bdmask": ((s_idx < 64) == (t_idx < 64)).astype(f32),
    }
    in_maps = []
    for core in range(NCORE):
        b, q = divmod(core, 4)
        m = dict(shared)
        m["x_slice"] = np.ascontiguousarray(x[b, TQ * q:TQ * (q + 1)])
        m["wgate_sh"] = np.ascontiguousarray(
            full["L0_ffn_w_gate"][:, 128 * core:128 * (core + 1)], dtype=f32)
        m["wup_sh"] = np.ascontiguousarray(
            full["L0_ffn_w_up"][:, 128 * core:128 * (core + 1)], dtype=f32)
        m["wdown_sh"] = np.ascontiguousarray(
            full["L0_ffn_w_down"][:, 352 * core:352 * (core + 1)], dtype=f32)
        sel = np.zeros((B, 128), f32)
        sel[b, :] = 1.0
        m["sel128"] = sel
        hm = np.zeros((1, NCORE), f32)
        hm[0, 4 * b:4 * b + q] = 1.0
        m["hopmask"] = hm
        in_maps.append(m)
    return in_maps


def kernel(**inputs):
    nc, dbg = _get_nc()
    in_maps = host_inputs(inputs)
    res = run_bass_kernel_spmd(nc, in_maps, core_ids=list(range(NCORE)))
    out = np.zeros((B, T, D), np.float32)
    for core in range(NCORE):
        b, q = divmod(core, 4)
        out[b, TQ * q:TQ * (q + 1)] = res.results[core]["out_x"]
    return out


# version 13
# speedup vs baseline: 1.1401x; 1.0470x over previous
import numpy as np
import ml_dtypes
import concourse.bass as bass
import concourse.mybir as mybir
from concourse.bass_utils import run_bass_kernel_spmd

F32 = mybir.dt.float32
BF16 = mybir.dt.bfloat16
AF = mybir.ActivationFunctionType
ALU = mybir.AluOpType
AX = mybir.AxisListType

MAGIC = float(np.float32(1.5 * 2**23))
PHASE_LIMIT = 6
CHUNK_PARTS = 3
BIG = 1e30

B, T, D = 2, 8192, 1024
NCORE = 8
TQ = 2048
NT = TQ // 128
H, DH = 16, 64
C = 128
IF = 2816
NKC = D // 128

ENGINES = ("v", "a", "t", "g", "s")


class Tok:
    __slots__ = ("sem", "val")

    def __init__(self, sem, val):
        self.sem = sem
        self.val = val


class Sched:
    def __init__(self, nc, pool_sizes=None):
        self.nc = nc
        self._ctx = []
        self.sems = {}
        self.cnt = {}
        self.prog = {e: [] for e in ENGINES}
        self.last_waited = {e: {} for e in ENGINES}
        for e in ENGINES:
            self.sems[e] = self._mksem(f"sem_{e}")
            self.cnt[e] = 0
        self.sems["c"] = self._mksem("sem_c")
        self.cnt["c"] = 0
        ps = pool_sizes or {"s": 24, "a": 10, "g": 10}
        self.pools = {}
        for q, n in ps.items():
            self.pools[q] = {
                "sems": [self._mksem(f"sem_{q}dma{i}") for i in range(n)],
                "cnt": [0] * n,
                "next": 0,
            }

    def _mksem(self, name):
        cm = self.nc.semaphore(name)
        sem = cm.__enter__()
        self._ctx.append(cm)
        return sem

    def _emit_waits(self, eng_key, deps):
        waits = []
        lw = self.last_waited[eng_key]
        for d in deps:
            if d is None:
                continue
            key = id(d.sem)
            if lw.get(key, -1) >= d.val:
                continue
            lw[key] = d.val
            waits.append((d.sem, d.val))
        return waits

    def op(self, e, fn, deps=()):
        waits = self._emit_waits(e, deps)
        sem = self.sems[e]
        self.cnt[e] += 1
        self.prog[e].append((waits, fn, sem, 1))
        return Tok(sem, self.cnt[e])

    def cop(self, fn, deps=()):
        waits = self._emit_waits("g", deps)
        sem = self.sems["c"]
        self.cnt["c"] += 1
        self.prog["g"].append((waits, fn, sem, 1))
        return Tok(sem, self.cnt["c"])

    def dma(self, q, fn, deps=()):
        pool = self.pools[q]
        i = pool["next"]
        pool["next"] = (i + 1) % len(pool["sems"])
        sem = pool["sems"][i]
        prev = pool["cnt"][i]
        deps = list(deps)
        if prev:
            deps.append(Tok(sem, prev))
        waits = self._emit_waits(q, deps)
        pool["cnt"][i] = prev + 16
        self.prog[q].append((waits, fn, sem, 16))
        return Tok(sem, pool["cnt"][i])

    def gdma(self, fn, deps=()):
        return self.dma("g", fn, deps)

    def etok(self, e):
        return Tok(self.sems[e], self.cnt[e])

    def barrier(self):
        deps = [Tok(self.sems[e], self.cnt[e]) for e in ENGINES]
        if self.cnt["c"]:
            deps.append(Tok(self.sems["c"], self.cnt["c"]))
        for q, pool in self.pools.items():
            for i, cv in enumerate(pool["cnt"]):
                if cv:
                    deps.append(Tok(pool["sems"][i], cv))
        hub = self.op("v", lambda e: e.engine_nop(), deps=deps)
        for e in ENGINES:
            if e != "v":
                self.op(e, lambda en: en.engine_nop()
                        if hasattr(en, "engine_nop") else en.nop(),
                        deps=[hub])
            self.last_waited[e] = {}
        self.hub = hub
        return hub

    def finalize(self, block):
        def mk(e):
            def body(eng):
                for waits, fn, sem, inc in self.prog[e]:
                    for (s, vv) in waits:
                        eng.wait_ge(s, vv)
                    inst = fn(eng)
                    if inst is None:
                        raise RuntimeError(f"op fn on {e} returned None")
                    inst.then_inc(sem, inc)
            return body

        if self.prog["g"]:
            block.gpsimd(mk("g"))
        if self.prog["s"]:
            block.sync(mk("s"))
        if self.prog["t"]:
            block.tensor(mk("t"))
        if self.prog["v"]:
            block.vector(mk("v"))
        if self.prog["a"]:
            block.scalar(mk("a"))


class Arena:
    def __init__(self, nc, name, cols_f32):
        self.t = nc.alloc_sbuf_tensor(name, [128, cols_f32], F32).ap()
        self.cols = cols_f32
        self.off = 0

    def reset(self):
        self.off = 0

    def f32(self, p, n):
        a = self.t[:, self.off:self.off + n]
        self.off += n
        assert self.off <= self.cols, f"arena overflow {self.off}>{self.cols}"
        return a[0:p, :] if p != 128 else a

    def bf16(self, p, n):
        slots = (n + 1) // 2
        a = self.t[:, self.off:self.off + slots].bitcast(BF16)
        self.off += slots
        assert self.off <= self.cols, f"arena overflow {self.off}>{self.cols}"
        return a[0:p, 0:n] if p != 128 else a[:, 0:n]


def build_nc(debug=()):
    nc = bass.Bass("TRN2", target_bir_lowering=False)
    io = {}

    def inp(name, shape, dt=F32):
        return nc.dram_tensor(name, list(shape), dt, kind="ExternalInput").ap()

    io["x_in"] = inp("x_slice", (TQ, D))
    io["wqkv"] = inp("Wqkv", (3072, D))
    io["wg"] = inp("Wg", (H, D))
    io["bgv"] = inp("bg", (1, H))
    io["wb"] = inp("Wb", (H, D))
    io["bbv"] = inp("bb", (1, H))
    io["wout"] = inp("Wout", (D, D))
    io["wgate_sh"] = inp("wgate_sh", (IF, 128))
    io["wup_sh"] = inp("wup_sh", (IF, 128))
    io["wdown_sh"] = inp("wdown_sh", (D, IF // NCORE))
    io["c_in"] = inp("c_vec", (B, D))
    io["wmod"] = inp("wmod", (D, 6 * D))
    io["sel128"] = inp("sel128", (B, 128))
    io["hopmask"] = inp("hopmask", (1, NCORE))
    io["ident128"] = inp("ident128", (128, 128))
    io["mask_ts"] = inp("mask_ts", (128, 128))
    io["mask_sts"] = inp("mask_sts", (128, 128))
    io["mask_sti"] = inp("mask_sti", (128, 128))
    io["i64two"] = inp("i64two", (128, 64))
    io["i64rep8"] = inp("i64rep8", (128, 512))
    io["bdmask"] = inp("bdmask", (128, 128))
    io["ipat"] = inp("ipat", (128, H * 2 * DH))
    io["out_x"] = nc.dram_tensor("out_x", [TQ, D], F32, kind="ExternalOutput").ap()

    dbg = {}

    def dbgout(name, shape=None, dt=F32):
        if name in debug and name not in dbg:
            dbg[name] = nc.dram_tensor("dbg_" + name, list(shape), dt,
                                       kind="ExternalOutput").ap()
        return dbg.get(name)

    dr = lambda name, shape, dt: nc.dram_tensor(name, list(shape), dt).ap()
    io["h_dram"] = dr("h_dram", (NKC, TQ, 128), BF16)
    io["wall_dram"] = dr("wall_dram", (NKC, 3104, 128), BF16)
    io["wout_dram"] = dr("wout_dram", (NKC, D, 128), BF16)
    io["kpl_dram"] = dr("kpl_dram", (NKC, TQ, 128), BF16)
    io["kb_dram"] = dr("kb_dram", (NKC, TQ, 128), BF16)
    io["q_dram"] = dr("q_dram", (NKC, TQ, 128), BF16)
    io["qg_dram"] = dr("qg_dram", (NKC, TQ, 128), BF16)
    io["bv_dram"] = dr("bv_dram", (NKC, TQ, 128), BF16)
    io["l_dram"] = dr("l_dram", (NT, H * C), F32)
    io["lc_dram"] = dr("lc_dram", (1, NT * H), F32)
    io["qpt_dram"] = dr("qpt_dram", (NT * H, DH * C), BF16)
    io["x2_dram"] = dr("x2_dram", (TQ, D), F32)
    io["h2q_dram"] = dr("h2q_dram", (NKC, TQ, 128), BF16)
    io["hmid_dram"] = dr("hmid_dram", (IF // 128, TQ, 128), BF16)
    io["agw_in"] = dr("agw_in", (128, 2 * IF), BF16)
    io["agw_out"] = dr("agw_out", (NCORE * 128, 2 * IF), BF16)
    io["agd_in"] = dr("agd_in", (IF // NCORE, D), BF16)
    io["agd_out"] = dr("agd_out", (IF, D), BF16)
    io["ags_in"] = dr("ags_in", (DH, H * 2 * DH), BF16)
    io["ags_out"] = dr("ags_out", (NCORE * DH, H * 2 * DH), BF16)

    with nc.Block() as block:
        S = Sched(nc)
        K = Kern(nc, S, io, dbgout)
        K.emit()
        S.finalize(block)
    return nc, dbg


class Kern:
    def __init__(self, nc, S, io, dbgout):
        self.nc = nc
        self.S = S
        self.io = io
        self.dbgout = dbgout
        self.sb = lambda name, shape, dt=F32: nc.alloc_sbuf_tensor(
            name, list(shape), dt).ap()
        self.pb = [nc.alloc_psum_tensor(f"pb{i}", [128, 512], F32).ap()
                   for i in range(8)]
        self.arF = Arena(nc, "arF", 6200)
        self.arB = Arena(nc, "arB", 18800)

    def epoch(self):
        self.S.barrier()
        self.arF.reset()
        self.arB.reset()

    def emit(self):
        lim = PHASE_LIMIT
        self.persist()
        self.prologue()
        if lim >= 1:
            self.ffn_weight_quant()
            self.weights_attn()
        self.epoch()
        if lim >= 2:
            self.hphase()
        if lim >= 3:
            self.qkvphase()
        self.epoch()
        if lim >= 4:
            self.chunkphase()
        self.epoch()
        if lim >= 5:
            self.corrphase()
            self.woutphase()
        self.epoch()
        if lim >= 6:
            self.ffnphase()
            self.finish()
        else:
            self.dummy_out()

    def dummy_out(self):
        S, io = self.S, self.io
        z = S.op("v", lambda e: e.memset(self.x_t, 0.0),
                 deps=[S.hub])
        toks = []
        for i in range(NT):
            toks.append(S.dma("s", (lambda i_: lambda e: e.dma_start(
                out=io["out_x"][128 * i_:128 * (i_ + 1), :],
                in_=self.x_t))(i), deps=[z]))
        S.op("v", lambda e: e.engine_nop(), deps=toks)
        S.op("s", lambda e: e.nop(), deps=toks)

    def persist(self):
        S, io, sb = self.S, self.io, self.sb
        self.ident_sb = sb("ident_sb", (128, 128))
        self.mask_ts_sb = sb("mask_ts_sb", (128, 128))
        self.mask_sts_sb = sb("mask_sts_sb", (128, 128))
        self.mask_sti_sb = sb("mask_sti_sb", (128, 128))
        self.i64two_sb = sb("i64two_sb", (128, 64))
        self.i64two_bf = sb("i64two_bf", (128, 64), BF16)
        self.i64rep8f = sb("i64rep8f", (128, 512))
        self.i64rep8b = sb("i64rep8b", (128, 512), BF16)
        self.selF_sb = sb("selF_sb", (B, 128))
        self.hm_sb = sb("hm_sb", (128, NCORE))
        self.hm1_sb = sb("hm1_sb", (128, NCORE))
        self.bgbb_rep = sb("bgbb_rep", (128, 2 * H))
        t = {}
        t["id"] = S.dma("s", lambda e: e.dma_start(out=self.ident_sb, in_=io["ident128"]))
        t["mts"] = S.dma("s", lambda e: e.dma_start(out=self.mask_ts_sb, in_=io["mask_ts"]))
        t["msts"] = S.dma("s", lambda e: e.dma_start(out=self.mask_sts_sb, in_=io["mask_sts"]))
        t["msti"] = S.dma("s", lambda e: e.dma_start(out=self.mask_sti_sb, in_=io["mask_sti"]))
        t["i2"] = S.dma("s", lambda e: e.dma_start(out=self.i64two_sb, in_=io["i64two"]))
        t["i8"] = S.dma("s", lambda e: e.dma_start(out=self.i64rep8f, in_=io["i64rep8"]))
        self.bdm_f = sb("bdm_f", (128, 128))
        self.bdm_bf = sb("bdm_bf", (128, 128), BF16)
        self.odm_bf = sb("odm_bf", (128, 128), BF16)
        t["bdm"] = S.dma("s", lambda e: e.dma_start(out=self.bdm_f, in_=io["bdmask"]))
        t["sel"] = S.dma("s", lambda e: e.dma_start(out=self.selF_sb, in_=io["sel128"]))
        t["hm"] = S.dma("s", lambda e: e.dma_start(
            out=self.hm_sb, in_=io["hopmask"].partition_broadcast(128)))
        t["bgr"] = S.dma("s", lambda e: e.dma_start(
            out=self.bgbb_rep[:, 0:H], in_=io["bgv"].partition_broadcast(128)))
        t["bbr"] = S.dma("s", lambda e: e.dma_start(
            out=self.bgbb_rep[:, H:2 * H], in_=io["bbv"].partition_broadcast(128)))
        self.t_hm1 = S.op("v", lambda e: e.tensor_scalar(
            out=self.hm1_sb, in0=self.hm_sb, scalar1=-1.0, scalar2=1.0,
            op0=ALU.mult, op1=ALU.add), deps=[t["hm"]])
        self.ident_bf = self.sb("ident_bf", (128, 128), BF16)
        self.t_idb = S.op("v", lambda e: e.tensor_copy(
            out=self.ident_bf, in_=self.ident_sb), deps=[t["id"]])
        self.t_i64b = S.op("v", lambda e: e.tensor_copy(
            out=self.i64two_bf, in_=self.i64two_sb), deps=[t["i2"]])
        self.t_i8b = S.op("v", lambda e: e.tensor_copy(
            out=self.i64rep8b, in_=self.i64rep8f), deps=[t["i8"]])
        self.t_bdm = S.op("v", lambda e: e.tensor_copy(
            out=self.bdm_bf, in_=self.bdm_f), deps=[t["bdm"]])
        self.t_odm = S.op("v", lambda e: e.tensor_scalar(
            out=self.odm_bf, in0=self.ident_sb, scalar1=-1.0, scalar2=1.0,
            op0=ALU.mult, op1=ALU.add), deps=[t["id"]])
        self.t_pbz = [S.op("v", (lambda b_: lambda e: e.memset(
            self.pb[b_], 0.0))(b)) for b in range(8)]
        self.pt = t
        self.ln_m = sb("ln_m", (128, 1))
        self.ln_r = sb("ln_r", (128, 1))
        self.ln_t1 = sb("ln_t1", (128, 1))
        self.ln_t2 = sb("ln_t2", (128, 1))
        self.ln_xs = sb("ln_xs", (128, D))
        self.L_all = sb("L_all", (H, NT * C))
        self.LT = [sb(f"LT{i}", (128, H)) for i in range(NT)]
        self.expL = [sb(f"expL{i}", (128, H)) for i in range(NT)]
        self.b_sig = [sb(f"bsig{i}", (128, H)) for i in range(NT)]
        self.rnk_t = [sb(f"rnk{i}", (128, H)) for i in range(NT)]
        self.rnq_t = [sb(f"rnq{i}", (128, H)) for i in range(NT)]
        self.gcrep = sb("gcrep", (128, NT * H))
        self.gcp = sb("gcp", (128, NT * 8))
        self.zeroHC = sb("zeroHC", (H, C))
        self.t_zero = S.op("v", lambda e: e.memset(self.zeroHC, 0.0))
        self.Ot = [sb(f"Ot{p}", (128, TQ), BF16) for p in range(H // 2)]
        self.stloc = [sb(f"stloc{p}", (128, DH), BF16) for p in range(H // 2)]
        self.prun = [sb(f"prun{p}", (128, DH), BF16) for p in range(H // 2)]
        self.prunT = [sb(f"prunT{p}", (128, DH), BF16) for p in range(H // 2)]
        self.prun_store = sb("prun_store", (128, NT * (H // 2) * DH), BF16)
        self.rep = {nm: sb(f"rep_{nm}", (128, D))
                    for nm in ("sa", "ca", "ga", "sb2", "cb", "gb")}
        self.x_t = sb("x_t", (128, D))
        self.h_bf = sb("h_bf", (128, D), BF16)
        self.negLT = sb("negLT", (128, H))
        self.fdt = sb("fdt", (H, C))
        self.sc16c = sb("sc16c", (128, H))
        self.lcc = sb("lcc", (H, NT))
        self.lcct = sb("lcct", (NT, H))

    def set_qscr(self, width, dual=False):
        arF, arB = self.arF, self.arB

        def mk():
            return (arF.f32(128, width // 16), arF.f32(128, width // 16),
                    arF.f32(128, width), arB.bf16(128, width),
                    arB.bf16(128, width), arB.bf16(128, width))
        self.qscr_by_eng = {"v": mk()}
        self.qscr_by_eng["g"] = mk() if dual else self.qscr_by_eng["v"]

    def prologue(self):
        S, io, sb, pb = self.S, self.io, self.sb, self.pb
        arF, arB = self.arF, self.arB
        c_sb = arF.f32(B, D)
        mod_sb = self.arB.f32(B, 6 * D)
        sc_sb = arB.bf16(B, D)
        sct_sb = arB.bf16(128, 2 * NKC)
        wm_sb = [arB.bf16(128, 3 * D) for _ in range(2)]
        lc0 = S.dma("s", lambda e: e.dma_start(out=c_sb, in_=io["c_in"]))
        tsc = S.op("a", lambda e: e.activation(out=sc_sb, in_=c_sb,
                   func=AF.Silu), deps=[lc0])
        ps_t = pb[7][:, 0:8].bitcast(BF16)
        prev = None
        for k in range(NKC):
            tk = S.op("t", (lambda k_: lambda e: e.transpose(
                ps_t[:, 0:2], sc_sb[:, 128 * k_:128 * (k_ + 1)],
                self.ident_bf[0:2, 0:2]))(k),
                deps=[tsc, self.t_idb, self.t_pbz[7]] +
                     ([prev] if prev else []))
            prev = S.op("v", (lambda k_: lambda e: e.tensor_copy(
                out=sct_sb[:, 2 * k_:2 * k_ + 2], in_=ps_t[:, 0:2]))(k),
                deps=[tk])
        sct_tok = prev
        wm_tok = [None, None]
        mod_evacs = []
        for half in range(2):
            mm_last = [None] * 6
            prev_evacs = list(mod_evacs)
            for k in range(NKC):
                buf = k % 2
                lw = S.gdma((lambda k_, h_, b_: lambda e: e.dma_start(
                    out=wm_sb[b_],
                    in_=io["wmod"][128 * k_:128 * (k_ + 1),
                                   3072 * h_:3072 * (h_ + 1)]))(k, half, buf),
                    deps=[wm_tok[buf]] if wm_tok[buf] else [])
                for n in range(6):
                    deps = [lw, sct_tok] + (
                        prev_evacs + [self.t_pbz[n]] if k == 0 else [])
                    if mm_last[n]:
                        deps.append(mm_last[n])
                    mm_last[n] = S.op("t", (lambda k_, n_, b_: lambda e:
                        e.matmul(pb[n_][0:B, :],
                                 sct_sb[:, 2 * k_:2 * k_ + 2],
                                 wm_sb[b_][:, 512 * n_:512 * (n_ + 1)],
                                 start=(k_ == 0), stop=(k_ == NKC - 1)))(k, n, buf),
                        deps=deps)
                wm_tok[buf] = mm_last[5]
            mod_evacs = []
            for n in range(6):
                mod_evacs.append(S.op("v", (lambda h_, n_: lambda e:
                    e.tensor_copy(
                        out=mod_sb[:, 3072 * h_ + 512 * n_:
                                   3072 * h_ + 512 * (n_ + 1)],
                        in_=pb[n_][0:B, :]))(half, n), deps=[mm_last[n]]))
        names = ["sa", "ca", "ga", "sb2", "cb", "gb"]
        rep_toks = {}
        last = None
        for i, nm in enumerate(names):
            for n in range(2):
                mt = S.op("t", (lambda i_, n_: lambda e: e.matmul(
                    pb[6][:, 0:512], self.selF_sb,
                    mod_sb[:, D * i_ + 512 * n_:D * i_ + 512 * (n_ + 1)],
                    start=True, stop=True))(i, n),
                    deps=mod_evacs + [self.pt["sel"]] + ([last] if last else []))
                last = S.op("v", (lambda nm_, n_: lambda e: e.tensor_copy(
                    out=self.rep[nm_][:, 512 * n_:512 * (n_ + 1)],
                    in_=pb[6][:, 0:512]))(nm, n), deps=[mt])
            rep_toks[nm] = last
        self.t_ca1 = S.op("v", lambda e: e.tensor_scalar_add(
            out=self.rep["ca"], in0=self.rep["ca"], scalar1=1.0),
            deps=[rep_toks["ca"]])
        self.t_cb1 = S.op("v", lambda e: e.tensor_scalar_add(
            out=self.rep["cb"], in0=self.rep["cb"], scalar1=1.0),
            deps=[rep_toks["cb"]])
        sgn = arF.f32(128, D)
        gt = {}
        prev_g = None
        for nm in ("ga", "gb"):
            t1 = S.op("v", (lambda nm_: lambda e: e.tensor_scalar(
                out=sgn, in0=self.rep[nm_], scalar1=0.0, scalar2=2.0,
                op0=ALU.is_ge, op1=ALU.mult))(nm),
                deps=[rep_toks[nm]] + ([prev_g] if prev_g else []))
            t2 = S.op("v", lambda e: e.tensor_scalar_add(
                out=sgn, in0=sgn, scalar1=-1.0), deps=[t1])
            t3 = S.op("v", (lambda nm_: lambda e: e.tensor_mul(
                out=self.rep[nm_], in0=self.rep[nm_], in1=sgn))(nm),
                deps=[rep_toks[nm], t2])
            t4 = S.op("v", (lambda nm_: lambda e: e.tensor_scalar_max(
                out=self.rep[nm_], in0=self.rep[nm_], scalar1=0.25))(nm),
                deps=[t3])
            gt[nm] = S.op("v", (lambda nm_: lambda e: e.tensor_mul(
                out=self.rep[nm_], in0=self.rep[nm_], in1=sgn))(nm),
                deps=[t4])
            prev_g = gt[nm]
        self.t_ga = gt["ga"]
        self.t_gb = gt["gb"]

    def weights_attn(self):
        S, io = self.S, self.io
        wv = io["wall_dram"]
        self.w_wall = []
        for part in range(4):
            self.w_wall.append(S.gdma((lambda p_: lambda e: e.dma_start(
                out=wv[:, 768 * p_:768 * (p_ + 1), :]
                    .rearrange("c n k -> n c k"),
                in_=io["wqkv"][768 * p_:768 * (p_ + 1), :]))(part)))
        self.w_wall.append(S.gdma(lambda e: e.dma_start(
            out=wv[:, 3072:3088, :].rearrange("c n k -> n c k"),
            in_=io["wg"])))
        self.w_wall.append(S.gdma(lambda e: e.dma_start(
            out=wv[:, 3088:3104, :].rearrange("c n k -> n c k"),
            in_=io["wb"])))
        self.w_wout = []
        for part in range(2):
            self.w_wout.append(S.gdma((lambda p_: lambda e: e.dma_start(
                out=io["wout_dram"][:, 512 * p_:512 * (p_ + 1), :]
                    .rearrange("c n k -> n c k"),
                in_=io["wout"][512 * p_:512 * (p_ + 1), :]))(part)))

    def quant_seq(self, src, dst, toks, eng="v"):
        S = self.S
        p, n = src.shape[0], src.shape[1]
        nb = n // 16
        am0, r20, xn0, w10, w20, w30 = self.qscr_by_eng[eng]
        am, r2 = am0[0:p, 0:nb], r20[0:p, 0:nb]
        xn, w1, w2, w3 = (xn0[0:p, 0:n], w10[0:p, 0:n], w20[0:p, 0:n],
                          w30[0:p, 0:n])
        t0 = S.op("v", lambda e: e.tensor_reduce(
            out=am, in_=src.rearrange("p (a b) -> p a b", b=16),
            axis=AX.X, op=ALU.max, apply_absolute_value=True), deps=toks)
        t1 = S.op("v", lambda e: e.tensor_scalar(
            out=am, in0=am, scalar1=1.0 / 12.0, scalar2=0.5e-12,
            op0=ALU.mult, op1=ALU.max), deps=[t0])
        t2 = S.op("v", lambda e: e.reciprocal(out=r2, in_=am), deps=[t1])
        t3 = S.op(eng, lambda e: e.tensor_tensor(
            out=xn.rearrange("p (a b) -> p a b", b=16),
            in0=src.rearrange("p (a b) -> p a b", b=16),
            in1=r2.unsqueeze(2).broadcast_to([p, nb, 16]), op=ALU.mult),
            deps=[t2] + list(toks))
        t4 = S.op(eng, lambda e: e.tensor_scalar(
            out=w1, in0=xn, scalar1=MAGIC, scalar2=MAGIC,
            op0=ALU.add, op1=ALU.subtract), deps=[t3])
        t5 = S.op(eng, lambda e: e.tensor_scalar(
            out=w1, in0=w1, scalar1=4.0, scalar2=-4.0,
            op0=ALU.min, op1=ALU.max), deps=[t4])
        t6 = S.op(eng, lambda e: e.tensor_scalar(
            out=w2, in0=xn, scalar1=10.0, scalar2=2.0,
            op0=ALU.is_gt, op1=ALU.mult), deps=[t3])
        t7 = S.op(eng, lambda e: e.scalar_tensor_tensor(
            out=w2, in0=xn, scalar=7.0, in1=w2, op0=ALU.is_gt, op1=ALU.add),
            deps=[t6])
        t8 = S.op(eng, lambda e: e.scalar_tensor_tensor(
            out=w2, in0=xn, scalar=5.0, in1=w2, op0=ALU.is_gt, op1=ALU.add),
            deps=[t7])
        t9 = S.op(eng, lambda e: e.tensor_scalar(
            out=w3, in0=xn, scalar1=-10.0, scalar2=2.0,
            op0=ALU.is_lt, op1=ALU.mult), deps=[t3])
        t10 = S.op(eng, lambda e: e.scalar_tensor_tensor(
            out=w3, in0=xn, scalar=-7.0, in1=w3, op0=ALU.is_lt, op1=ALU.add),
            deps=[t9])
        t11 = S.op(eng, lambda e: e.scalar_tensor_tensor(
            out=w3, in0=xn, scalar=-5.0, in1=w3, op0=ALU.is_lt, op1=ALU.add),
            deps=[t10])
        tq = S.op(eng, lambda e: e.tensor_tensor(out=w2, in0=w2, in1=w3,
                  op=ALU.subtract), deps=[t8, t11])
        tq2 = S.op(eng, lambda e: e.scalar_tensor_tensor(
            out=w2, in0=w2, scalar=2.0, in1=w1, op0=ALU.mult, op1=ALU.add),
            deps=[tq, t5])
        tf = S.op(eng, lambda e: e.tensor_tensor(
            out=dst.rearrange("p (a b) -> p a b", b=16),
            in0=w2.rearrange("p (a b) -> p a b", b=16),
            in1=am.unsqueeze(2).broadcast_to([p, nb, 16]), op=ALU.mult),
            deps=[tq2, t1])
        self._q_src = [t0, t3]
        return tf

    def ffn_weight_quant(self):
        S, io, pb = self.S, self.io, self.pb
        arB = self.arB
        self.set_qscr(IF // NCORE)
        gu_out = arB.bf16(128, 2 * IF)
        src_t = self.arF.f32(128, IF // NCORE)
        dq_t = arB.bf16(128, IF // NCORE)
        wdT_a = arB.bf16(128, D)
        wdT_b = arB.bf16(128, D)
        wdT_c = arB.bf16(96, D)
        ps = pb[7][:, 256:320].bitcast(BF16)
        toks1 = []
        prev_q = None
        ntile = IF // 128
        for which, wsrc in (("g", io["wgate_sh"]), ("u", io["wup_sh"])):
            for r in range(ntile):
                lt = S.dma("s", (lambda w_, r_: lambda e: e.dma_start(
                    out=src_t[:, 0:128],
                    in_=w_[128 * r_:128 * (r_ + 1), :]))(wsrc, r),
                    deps=[prev_q] if prev_q else [])
                qt = self.quant_seq(src_t[:, 0:128], dq_t[:, 0:128], [lt])
                off = (0 if which == "g" else IF) + 128 * r
                tt = S.op("t", (lambda: lambda e: e.transpose(
                    ps, dq_t[:, 0:128], self.ident_bf))(),
                    deps=[qt, self.t_idb, self.t_pbz[7]] +
                         ([prev_q] if prev_q else []))
                prev_q = S.op("v", (lambda off_: lambda e: e.tensor_copy(
                    out=gu_out[:, off_:off_ + 128], in_=ps))(off), deps=[tt])
                toks1.append(prev_q)
        st1 = S.dma("s", lambda e: e.dma_start(out=io["agw_in"], in_=gu_out),
                    deps=toks1)
        self.ag1 = S.cop(lambda e: e.collective_compute(
            "AllGather", ALU.bypass, replica_groups=[list(range(NCORE))],
            ins=[io["agw_in"].opt()], outs=[io["agw_out"].opt()]), deps=[st1])

        toks2 = []
        prev_q = None
        for r in range(NKC):
            lt = S.dma("s", (lambda r_: lambda e: e.dma_start(
                out=src_t, in_=io["wdown_sh"][128 * r_:128 * (r_ + 1), :]))(r),
                deps=[prev_q] if prev_q else [])
            qt = self.quant_seq(src_t, dq_t, [lt])
            for dst_t, c0, cw in ((wdT_a, 0, 128), (wdT_b, 128, 128),
                                  (wdT_c, 256, 96)):
                tt = S.op("t", (lambda c0_, cw_: lambda e: e.transpose(
                    ps[0:cw_, :], dq_t[:, c0_:c0_ + cw_],
                    self.ident_bf))(c0, cw),
                    deps=[qt, self.t_idb] + ([prev_q] if prev_q else []))
                prev_q = S.op("v", (lambda d_, r_, cw_: lambda e: e.tensor_copy(
                    out=d_[0:cw_, 128 * r_:128 * (r_ + 1)],
                    in_=ps[0:cw_, 0:128]))(dst_t, r, cw), deps=[tt])
                toks2.append(prev_q)
        st2 = [
            S.dma("s", lambda e: e.dma_start(out=io["agd_in"][0:128, :],
                  in_=wdT_a), deps=toks2),
            S.dma("s", lambda e: e.dma_start(out=io["agd_in"][128:256, :],
                  in_=wdT_b), deps=toks2),
            S.dma("s", lambda e: e.dma_start(out=io["agd_in"][256:352, :],
                  in_=wdT_c), deps=toks2),
        ]
        self.ag2 = S.cop(lambda e: e.collective_compute(
            "AllGather", ALU.bypass, replica_groups=[list(range(NCORE))],
            ins=[io["agd_in"].opt()], outs=[io["agd_out"].opt()]),
            deps=st2 + [self.ag1])

    def ln_mod(self, x_t, ca1, sa, out_t, deps):
        S = self.S
        m, rs, t1, t2, xs = (self.ln_m, self.ln_r, self.ln_t1, self.ln_t2,
                             self.ln_xs)
        ta = S.op("a", lambda e: e.activation(out=xs, in_=x_t, func=AF.Square),
                  deps=deps)
        t_m = S.op("v", lambda e: e.reduce_sum(out=m, in_=x_t, axis=AX.X),
                   deps=deps)
        t_q = S.op("v", lambda e: e.reduce_sum(out=t1, in_=xs, axis=AX.X),
                   deps=[ta])
        t_m2 = S.op("v", lambda e: e.tensor_scalar_mul(out=m, in0=m,
                    scalar1=1.0 / D), deps=[t_m])
        t_q2 = S.op("v", lambda e: e.tensor_scalar_mul(out=t1, in0=t1,
                    scalar1=1.0 / D), deps=[t_q])
        t_v = S.op("v", lambda e: e.tensor_mul(out=t2, in0=m, in1=m),
                   deps=[t_m2])
        t_v2 = S.op("v", lambda e: e.tensor_tensor(out=t1, in0=t1, in1=t2,
                    op=ALU.subtract), deps=[t_v, t_q2])
        t_v3 = S.op("v", lambda e: e.tensor_scalar_add(out=t1, in0=t1,
                    scalar1=1e-5), deps=[t_v2])
        t_sq = S.op("a", lambda e: e.activation(out=t2, in_=t1, func=AF.Sqrt),
                    deps=[t_v3])
        t_r = S.op("v", lambda e: e.reciprocal(out=rs, in_=t2), deps=[t_sq])
        t_n = S.op("v", lambda e: e.tensor_scalar(
            out=xs, in0=x_t, scalar1=m, scalar2=rs,
            op0=ALU.subtract, op1=ALU.mult), deps=[t_r, t_m2] + list(deps))
        t_h1 = S.op("v", lambda e: e.tensor_tensor(out=xs, in0=xs, in1=ca1,
                    op=ALU.mult), deps=[t_n, self.t_ca1, self.t_cb1])
        t_h2 = S.op("v", lambda e: e.tensor_tensor(out=out_t, in0=xs, in1=sa,
                    op=ALU.add), deps=[t_h1])
        return t_h2

    def hphase(self):
        S, io = self.S, self.io
        self.h_toks = []
        prev_store = None
        for i in range(NT):
            lx = S.dma("s", (lambda i_: lambda e: e.dma_start(
                out=self.x_t, in_=io["x_in"][128 * i_:128 * (i_ + 1), :]))(i),
                deps=[prev_store] if prev_store else [])
            th = self.ln_mod(self.x_t, self.rep["ca"], self.rep["sa"],
                             self.h_bf, [lx])
            prev_store = S.dma("s", (lambda i_: lambda e: e.dma_start(
                out=io["h_dram"][:, 128 * i_:128 * (i_ + 1), :]
                    .rearrange("c p k -> p c k"),
                in_=self.h_bf))(i), deps=[th])
            self.h_toks.append(prev_store)

    def qkvphase(self):
        S, io, pb = self.S, self.io, self.pb
        arB, arF = self.arB, self.arF
        wallT = [arB.bf16(128, 3104) for _ in range(NKC)]
        hT = [[arB.bf16(128, 512) for _ in range(NKC)] for _ in range(2)]
        var_bf = arB.bf16(128, D)
        q_raw = arF.f32(128, D)
        k_raw = arF.f32(128, D)
        sqs = arF.f32(128, D)
        ssq = arF.f32(128, H)
        ssk = arF.f32(128, H)
        zab = arF.f32(128, 2 * H)
        sab = arF.f32(128, 2 * H)
        la_t = arF.f32(128, H)
        laT = arF.f32(H, C)
        sc16 = arF.f32(128, H)
        wl = []
        for k in range(NKC):
            wl.append(S.dma("s", (lambda k_, d_: lambda e: e.dma_start(
                out=d_, in_=io["wall_dram"][k_], transpose=True))(k, wallT[k]),
                deps=self.w_wall))
        hl_tok = [[None] * NKC, [None] * NKC]
        store_toks = []
        evac_toks = []
        prev_lt = None
        prev_lt2 = None
        for i in range(NT):
            qtr, buf = i // 4, (i // 4) % 2
            if i % 4 == 0:
                for k in range(NKC):
                    hl_tok[buf][k] = S.dma("s", (lambda k_, d_, q_: lambda e:
                        e.dma_start(out=d_,
                                    in_=io["h_dram"][k_][512 * q_:512 * (q_ + 1), :],
                                    transpose=True))(k, hT[buf][k], qtr),
                        deps=[self.h_toks[4 * qtr + 3]] + store_toks +
                             ([hl_tok[buf][k]] if hl_tok[buf][k] else []))
            io_i = 128 * (i % 4)
            mm_ab = None
            mm_n = [None] * 6
            for k in range(NKC):
                for n in range(6):
                    deps = [wl[k], hl_tok[buf][k]]
                    if k == 0:
                        deps += evac_toks
                    if mm_n[n]:
                        deps.append(mm_n[n])
                    mm_n[n] = S.op("t", (lambda k_, n_, b2, o_: lambda e:
                        e.matmul(pb[n_], hT[b2][k_][:, o_:o_ + 128],
                                 wallT[k_][:, 512 * n_:512 * (n_ + 1)],
                                 start=(k_ == 0), stop=(k_ == NKC - 1)))(k, n, buf, io_i),
                        deps=deps)
                mm_ab = S.op("t", (lambda k_, b2, o_: lambda e: e.matmul(
                    pb[6][:, 0:32], hT[b2][k_][:, o_:o_ + 128],
                    wallT[k_][:, 3072:3104],
                    start=(k_ == 0), stop=(k_ == NKC - 1)))(k, buf, io_i),
                    deps=[wl[k], hl_tok[buf][k]] +
                         ([mm_ab] if mm_ab else evac_toks))
            tq0 = S.op("a", lambda e: e.copy(out=q_raw[:, 0:512],
                       in_=pb[0]), deps=[mm_n[0]])
            tq1 = S.op("a", lambda e: e.copy(out=q_raw[:, 512:1024],
                       in_=pb[1]), deps=[mm_n[1]])
            tk0 = S.op("a", lambda e: e.copy(out=k_raw[:, 0:512],
                       in_=pb[2]), deps=[mm_n[2]])
            tk1 = S.op("a", lambda e: e.copy(out=k_raw[:, 512:1024],
                       in_=pb[3]), deps=[mm_n[3]])
            tz = S.op("v", lambda e: e.tensor_tensor(
                out=zab, in0=self.bgbb_rep, in1=pb[6][:, 0:32], op=ALU.add),
                deps=[mm_ab, self.pt["bgr"], self.pt["bbr"]])
            ts_ = S.op("a", lambda e: e.activation(out=sab, in_=zab,
                       func=AF.Sigmoid), deps=[tz])
            tb = S.op("v", (lambda i_: lambda e: e.tensor_copy(
                out=self.b_sig[i_], in_=sab[:, H:2 * H]))(i), deps=[ts_])
            tcl = S.op("v", lambda e: e.tensor_scalar_max(
                out=la_t, in0=sab[:, 0:H], scalar1=1e-38), deps=[ts_])
            tla = S.op("a", lambda e: e.activation(out=la_t, in_=la_t,
                       func=AF.Ln), deps=[tcl])
            ttr = S.op("t", (lambda: lambda e: e.transpose(
                pb[7][0:H, 0:128], la_t, self.ident_sb))(),
                deps=[tla, self.pt["id"]] + ([prev_lt] if prev_lt else [])
                     + ([prev_lt2] if prev_lt2 else []))
            tev = S.op("v", lambda e: e.tensor_copy(out=laT,
                       in_=pb[7][0:H, 0:128]), deps=[ttr])
            tsc_ = S.op("v", (lambda i_: lambda e: e.tensor_tensor_scan(
                out=self.L_all[:, C * i_:C * (i_ + 1)], data0=laT,
                data1=self.zeroHC, initial=0.0, op0=ALU.add, op1=ALU.add))(i),
                deps=[tev, self.t_zero])
            tlt = S.op("t", (lambda i_: lambda e: e.transpose(
                pb[7][:, 128:128 + H], self.L_all[:, C * i_:C * (i_ + 1)],
                self.ident_sb[0:H, 0:H]))(i),
                deps=[tsc_] + ([prev_lt2] if prev_lt2 else []))
            tltv = S.op("v", (lambda i_: lambda e: e.tensor_copy(
                out=self.LT[i_], in_=pb[7][:, 128:128 + H]))(i), deps=[tlt])
            texp = S.op("a", (lambda i_: lambda e: e.activation(
                out=self.expL[i_], in_=self.LT[i_], func=AF.Exp))(i),
                deps=[tltv])
            prev_lt = tev
            prev_lt2 = tltv
            tsq = S.op("a", lambda e: e.activation(out=sqs, in_=q_raw,
                       func=AF.Square), deps=[tq0, tq1])
            trq = S.op("v", lambda e: e.tensor_reduce(
                out=ssq, in_=sqs.rearrange("p (h d) -> p h d", d=DH),
                axis=AX.X, op=ALU.add), deps=[tsq])
            tsk = S.op("a", lambda e: e.activation(out=sqs, in_=k_raw,
                       func=AF.Square), deps=[tk0, tk1, trq])
            trk = S.op("v", lambda e: e.tensor_reduce(
                out=ssk, in_=sqs.rearrange("p (h d) -> p h d", d=DH),
                axis=AX.X, op=ALU.add), deps=[tsk])
            tq2 = S.op("a", lambda e: e.activation(out=ssq, in_=ssq,
                       func=AF.Sqrt), deps=[trq])
            tq3 = S.op("v", lambda e: e.tensor_scalar_max(out=ssq, in0=ssq,
                       scalar1=1e-12), deps=[tq2])
            tq4 = S.op("v", (lambda i_: lambda e: e.reciprocal(
                out=self.rnq_t[i_], in_=ssq))(i), deps=[tq3])
            tk2 = S.op("a", lambda e: e.activation(out=ssk, in_=ssk,
                       func=AF.Sqrt), deps=[trk])
            tk3 = S.op("v", lambda e: e.tensor_scalar_max(out=ssk, in0=ssk,
                       scalar1=1e-12), deps=[tk2])
            tk4 = S.op("v", (lambda i_: lambda e: e.reciprocal(
                out=self.rnk_t[i_], in_=ssk))(i), deps=[tk3])

            def bc16(ap):
                return ap.unsqueeze(2).broadcast_to([128, H, DH])

            def vstore(dst, tok, i_=i):
                return S.dma("s", (lambda d_: lambda e: e.dma_start(
                    out=d_[:, 128 * i_:128 * (i_ + 1), :]
                        .rearrange("c p k -> p c k"),
                    in_=var_bf))(dst), deps=[tok])

            tv1 = S.op("v", (lambda i_: lambda e: e.tensor_tensor(
                out=var_bf.rearrange("p (h d) -> p h d", d=DH),
                in0=k_raw.rearrange("p (h d) -> p h d", d=DH),
                in1=bc16(self.rnk_t[i_]), op=ALU.mult))(i),
                deps=[tk4] + store_toks[4:5])
            s1 = vstore(io["kpl_dram"], tv1)
            tv2 = S.op("v", (lambda i_: lambda e: e.tensor_tensor(
                out=var_bf.rearrange("p (h d) -> p h d", d=DH),
                in0=var_bf.rearrange("p (h d) -> p h d", d=DH),
                in1=bc16(self.b_sig[i_]), op=ALU.mult))(i), deps=[tv1, tb, s1])
            s2 = vstore(io["kb_dram"], tv2)
            tv3 = S.op("v", (lambda i_: lambda e: e.tensor_tensor(
                out=var_bf.rearrange("p (h d) -> p h d", d=DH),
                in0=q_raw.rearrange("p (h d) -> p h d", d=DH),
                in1=bc16(self.rnq_t[i_]), op=ALU.mult))(i), deps=[tq4, s2])
            s3 = vstore(io["q_dram"], tv3)
            tg = S.op("v", (lambda i_: lambda e: e.tensor_tensor(
                out=sc16, in0=self.rnq_t[i_], in1=self.expL[i_],
                op=ALU.mult))(i), deps=[tq4, texp])
            tv4 = S.op("v", (lambda i_: lambda e: e.tensor_tensor(
                out=var_bf.rearrange("p (h d) -> p h d", d=DH),
                in0=q_raw.rearrange("p (h d) -> p h d", d=DH),
                in1=bc16(sc16), op=ALU.mult))(i), deps=[tg, s3])
            s4 = vstore(io["qg_dram"], tv4)
            tv5a = S.op("v", (lambda i_: lambda e: e.tensor_tensor(
                out=var_bf[:, 0:512].rearrange("p (h d) -> p h d", d=DH),
                in0=pb[4].rearrange("p (h d) -> p h d", d=DH),
                in1=self.b_sig[i_][:, 0:8].unsqueeze(2)
                    .broadcast_to([128, 8, DH]), op=ALU.mult))(i),
                deps=[mm_n[4], tb, s4])
            tv5b = S.op("v", (lambda i_: lambda e: e.tensor_tensor(
                out=var_bf[:, 512:1024].rearrange("p (h d) -> p h d", d=DH),
                in0=pb[5].rearrange("p (h d) -> p h d", d=DH),
                in1=self.b_sig[i_][:, 8:16].unsqueeze(2)
                    .broadcast_to([128, 8, DH]), op=ALU.mult))(i),
                deps=[mm_n[5], tb, tv5a])
            s5 = vstore(io["bv_dram"], tv5b)
            store_toks = [s1, s2, s3, s4, s5, tq0, tq1, tk0, tk1, tz,
                          tv5a, tv5b]
            evac_toks = [tq0, tq1, tk0, tk1, tz, tv5a, tv5b]
        self.qkv_last = list(store_toks)

        lst = []
        vtok = S.etok("v")
        for i in range(NT):
            lst.append(S.dma("s", (lambda i_: lambda e: e.dma_start(
                out=io["l_dram"][i_:i_ + 1, :]
                    .rearrange("x (h c) -> (x h) c", h=H),
                in_=self.L_all[:, C * i_:C * (i_ + 1)]))(i), deps=[vtok]))
        self.l_store = lst
        t1 = S.op("v", lambda e: e.tensor_copy(
            out=self.lcc, in_=self.L_all[:, C - 1::C]), deps=[vtok])
        t2 = S.op("t", lambda e: e.transpose(pb[7][0:NT, 256:256 + H],
                  self.lcc, self.ident_sb[0:H, 0:H]), deps=[t1, prev_lt2])
        t3 = S.op("v", lambda e: e.tensor_copy(out=self.lcct,
                  in_=pb[7][0:NT, 256:256 + H]), deps=[t2])
        t4 = S.dma("s", lambda e: e.dma_start(
            out=io["lc_dram"].rearrange("x (c h) -> (x c) h", h=H),
            in_=self.lcct), deps=[t3])
        t5 = S.dma("s", lambda e: e.dma_start(
            out=self.gcrep, in_=io["lc_dram"].partition_broadcast(128)),
            deps=[t4])
        self.t_gc = S.op("a", lambda e: e.activation(
            out=self.gcrep, in_=self.gcrep, func=AF.Exp), deps=[t5])
        gv = self.gcrep.rearrange("r (c p two) -> r c p two", p=8, two=2)
        gpv = self.gcp.rearrange("r (c p) -> r c p", p=8)
        tg1 = S.op("v", lambda e: e.tensor_copy(
            out=gpv[0:64], in_=gv[0:64, :, :, 0]), deps=[self.t_gc])
        self.t_gcp = S.op("v", lambda e: e.tensor_copy(
            out=gpv[64:128], in_=gv[64:128, :, :, 1]), deps=[tg1])

    def chunkphase(self):
        S, io, pb = self.S, self.io, self.pb
        arB, arF = self.arB, self.arF
        TQW = 256
        kT = [arB.bf16(128, TQW) for _ in range(NKC)]
        kbT = [arB.bf16(128, TQW) for _ in range(NKC)]
        qT = [arB.bf16(128, TQW) for _ in range(NKC)]
        qgT = [arB.bf16(128, TQW) for _ in range(NKC)]
        lrep = [arF.f32(128, H * C) for _ in range(2)]
        ein = arF.f32(128, H * C)
        e_ts = arB.bf16(128, H * C)
        e_sts = arB.bf16(128, H * C)
        t_full = arB.bf16(128, H * C)
        that_full = arB.bf16(128, H * C)
        rz_all = arB.bf16(128, H * 192)
        w_all = arB.bf16(128, H * 128)
        kpl_tm = arB.bf16(128, D)
        pw_ts = [arB.bf16(128, 1024) for _ in range(2)]
        pw_st = [arB.bf16(128, 1024) for _ in range(4)]
        g_all = arB.bf16(128, 1024)
        that_bd = arB.bf16(128, 1024)
        dg_scr = arB.bf16(128, 4 * C)
        z_all = rz_all
        qpt_sb = arB.bf16(128, 4 * C)
        tmp_odd = arB.bf16(128, 4 * C)
        bc_sb = arB.bf16(128, 4 * DH)
        bct_sb = arB.bf16(128, 4 * DH)
        mcs = arB.bf16(128, 4 * DH)
        self.chunk_tiles = dict(rz=rz_all, w=w_all, z=z_all, qpt=qpt_sb,
                                tmp=tmp_odd, bc=bc_sb, bct=bct_sb, mcs=mcs)

        rzv = rz_all.rearrange("p (h x) -> p h x", x=192)
        wav = w_all.rearrange("p (h x) -> p h x", x=128)

        zt = [self.t_i64b]
        for p in range(H // 2):
            zt.append(S.op("v", (lambda p_: lambda e: e.memset(
                self.stloc[p_], 0.0))(p)))
            zt.append(S.op("v", (lambda p_: lambda e: e.tensor_copy(
                out=self.prun[p_], in_=self.i64two_bf))(p), deps=[self.t_i64b]))
            zt.append(S.op("v", (lambda p_: lambda e: e.tensor_copy(
                out=self.prunT[p_], in_=self.i64two_bf))(p), deps=[self.t_i64b]))
        state_tok = zt

        ld_toks = []
        lr_tok = [None, None]
        prev_chunk_done = []
        prev_prods = []

        def emit_E(c, pre):
            pp = c % 2
            tl = S.dma("s", (lambda c_, p_: lambda e: e.dma_start(
                out=lrep[p_], in_=io["l_dram"][c_:c_ + 1, :]
                .partition_broadcast(128)))(c, pp),
                deps=[self.l_store[c]] + ([lr_tok[pp]] if lr_tok[pp] else []))
            emask = [(self.mask_ts_sb, e_ts, -1.0),
                     (self.mask_sts_sb, e_sts, 1.0)]
            ex_toks = []
            prev_ex = list(pre)
            last_ein = None
            for mask, edst, scl in emask:
                ein_toks = []
                for h in range(H):
                    ein_toks.append(S.op("v", (lambda c_, h_, p_, m_: lambda e:
                        e.scalar_tensor_tensor(
                            out=ein[:, C * h_:C * (h_ + 1)],
                            in0=lrep[p_][:, C * h_:C * (h_ + 1)],
                            scalar=self.LT[c_][:, h_:h_ + 1],
                            in1=m_, op0=ALU.subtract, op1=ALU.add))(c, h, pp, mask),
                        deps=[tl, self.pt["mts"], self.pt["msts"],
                              self.pt["msti"]] + prev_ex))
                ex = S.op("a", (lambda ed_, sc_: lambda e: e.activation(
                    out=ed_, in_=ein, func=AF.Exp, scale=sc_))(edst, scl),
                    deps=ein_toks)
                prev_ex = [ex]
                ex_toks.append(ex)
                last_ein = ein_toks[-1]
            lr_tok[pp] = last_ein
            return ex_toks

        ex_cache = {0: emit_E(0, [])}
        for c in range(NT):
            grp, cg = c // 2, c % 2
            if cg == 0:
                ld_toks = []
                for kk in range(NKC):
                    for dst, src in ((kT, "kpl_dram"), (kbT, "kb_dram"),
                                     (qT, "q_dram"), (qgT, "qg_dram")):
                        ld_toks.append(S.dma("s", (lambda kk_, d_, s_, g_:
                            lambda e: e.dma_start(
                                out=d_,
                                in_=io[s_][kk_][TQW * g_:TQW * (g_ + 1), :],
                                transpose=True))(kk, dst[kk], src, grp),
                            deps=self.qkv_last + prev_chunk_done))
            lkp = S.dma("s", (lambda c_: lambda e: e.dma_start(
                out=kpl_tm, in_=io["kpl_dram"][:, 128 * c_:128 * (c_ + 1), :]
                .rearrange("k p x -> p k x")))(c),
                deps=self.qkv_last + prev_chunk_done)
            wav4 = w_all.rearrange("p (hp two x) -> p hp two x", two=2,
                                   x=128)
            lbv = []
            for two in range(2):
                lbv.append(S.dma("s", (lambda c_, t2: lambda e: e.dma_start(
                    out=wav4[:, :, t2, 0:64],
                    in_=io["bv_dram"][:, 128 * c_:128 * (c_ + 1),
                                      64 * t2:64 * (t2 + 1)]
                    .rearrange("k p d -> p k d")))(c, two),
                    deps=self.qkv_last + prev_chunk_done))
            ex1, ex2 = ex_cache.pop(c)

            tfd = S.op("a", (lambda c_: lambda e: e.activation(
                out=self.fdt, in_=self.L_all[:, C * c_:C * (c_ + 1)],
                func=AF.Exp, scale=-1.0,
                bias=self.L_all[:, C * (c_ + 1) - 1:C * (c_ + 1)]))(c),
                deps=[self.l_store[c]] + prev_chunk_done)
            ttd = S.op("t", lambda e: e.transpose(
                pb[7][:, 256:256 + H], self.fdt, self.ident_sb[0:H, 0:H]),
                deps=[tfd] + prev_chunk_done)
            tfdv = S.op("v", lambda e: e.tensor_copy(out=self.sc16c,
                        in_=pb[7][:, 256:256 + H]), deps=[ttd])
            tkd = S.op("v", lambda e: e.tensor_tensor(
                out=rzv[:, :, 128:192],
                in0=kpl_tm.rearrange("p (h d) -> p h d", d=DH),
                in1=self.sc16c.unsqueeze(2).broadcast_to([128, H, DH]),
                op=ALU.mult), deps=[tfdv, lkp] + prev_chunk_done)
            tg2 = S.op("v", (lambda c_: lambda e: e.tensor_tensor(
                out=self.sc16c, in0=self.b_sig[c_], in1=self.expL[c_],
                op=ALU.mult))(c), deps=[tkd])
            tkbg = S.op("v", lambda e: e.tensor_tensor(
                out=wav[:, :, 64:128],
                in0=kpl_tm.rearrange("p (h d) -> p h d", d=DH),
                in1=self.sc16c.unsqueeze(2).broadcast_to([128, H, DH]),
                op=ALU.mult), deps=[tg2, lkp] + prev_chunk_done)

            prod_toks = []
            prev_fix = []
            ev4 = lambda ap: ap.rearrange("p (g two c) -> p g two c",
                                          two=2, c=C)
            rz4 = rz_all.rearrange("p (g two x) -> p g two x", two=2, x=192)
            pbv = [bk.rearrange("p (k one c) -> p k one c", one=1, c=C)
                   for bk in pb]
            for wv in range(2):
                mmt_bank = {k: [] for k in range(6)}
                for hh in range(8):
                    h = 8 * wv + hh
                    r0 = 64 * (h % 2)
                    csl = slice(128 * cg, 128 * (cg + 1))
                    kt_sl = kT[h // 2][r0:r0 + 64, csl]
                    kbt_sl = kbT[h // 2][r0:r0 + 64, csl]
                    qt_sl = qT[h // 2][r0:r0 + 64, csl]
                    col = 128 * (hh // 2)
                    bA = pb[hh % 2]
                    bB = pb[2 + hh % 2]
                    bC = pb[4 + hh % 2]
                    dps = ld_toks + (list(prod_toks) if prod_toks else
                                     list(prev_chunk_done))
                    mmt_bank[hh % 2].append(S.op("t", (lambda a, b_, o, co:
                        lambda e: e.matmul(o[:, co:co + 128], a, b_,
                        start=True, stop=True))(kt_sl, kbt_sl, bA, col),
                        deps=dps))
                    mmt_bank[2 + hh % 2].append(S.op("t", (lambda a, b_, o, co:
                        lambda e: e.matmul(o[:, co:co + 128], a, b_,
                        start=True, stop=True))(kbt_sl, kt_sl, bB, col),
                        deps=dps))
                    mmt_bank[4 + hh % 2].append(S.op("t", (lambda a, b_, o, co:
                        lambda e: e.matmul(o[:, co:co + 128], a, b_,
                        start=True, stop=True))(kt_sl, qt_sl, bC, col),
                        deps=dps))
                mmt = [t for v in mmt_bank.values() for t in v]
                gsl = slice(4 * wv, 4 * wv + 4)
                for par in range(2):
                    psl = slice(par, par + 1)
                    pt1 = S.op("v", (lambda g_, p_, pa: lambda e:
                        e.tensor_tensor(
                            out=ev4(that_full)[:, g_, p_, :],
                            in0=ev4(e_sts)[:, g_, p_, :],
                            in1=pbv[pa][:, :, :, :],
                            op=ALU.mult))(gsl, psl, par),
                        deps=mmt_bank[par] + [ex2])
                    pt2 = S.op("v", (lambda g_, p_, pa: lambda e:
                        e.tensor_tensor(
                            out=ev4(t_full)[:, g_, p_, :],
                            in0=ev4(e_ts)[:, g_, p_, :],
                            in1=pbv[2 + pa][:, :, :, :],
                            op=ALU.mult))(gsl, psl, par),
                        deps=mmt_bank[2 + par] + [ex1])
                    pt3 = S.op("v", (lambda g_, p_, pa: lambda e:
                        e.tensor_tensor(
                            out=rz4[:, g_, p_, 0:128],
                            in0=ev4(e_sts)[:, g_, p_, :],
                            in1=pbv[4 + pa][:, :, :, :],
                            op=ALU.mult))(gsl, psl, par),
                        deps=mmt_bank[4 + par] + [ex2])
                    dg4 = dg_scr.rearrange("p (k one c) -> p k one c",
                                           one=1, c=C)
                    dg = S.op("v", (lambda pa: lambda e: e.tensor_tensor(
                        out=dg4, in0=pbv[4 + pa][:, :, :, :],
                        in1=self.ident_bf.unsqueeze(1).unsqueeze(1)
                            .broadcast_to([128, 4, 1, C]),
                        op=ALU.mult))(par),
                        deps=mmt_bank[4 + par] + [self.t_idb] + prev_fix)
                    fx = S.op("v", (lambda g_, p_: lambda e: e.tensor_tensor(
                        out=rz4[:, g_, p_, 0:128],
                        in0=rz4[:, g_, p_, 0:128],
                        in1=dg4, op=ALU.add))(gsl, psl),
                        deps=[dg, pt3])
                    prev_fix = [fx]
                    prod_toks += [pt1, pt2, pt3, fx]
            if CHUNK_PARTS < 2:
                prev_chunk_done = [self.S.op("v", lambda e: e.engine_nop(),
                                   deps=prod_toks + [tkd, tkbg] + lbv)]
                state_tok = prev_chunk_done
                continue
            prev_prods = list(prod_toks)
            if c + 1 < NT:
                ex_cache[c + 1] = emit_E(c + 1, prev_prods)
            self._cur_c = c
            pre_solve = prod_toks + [tkd]
            td = self.dbgout("tdump", (NT, 128, H * C), BF16)
            rzd = self.dbgout("rzdump", (NT, 128, H * 192), BF16)
            if td is not None:
                pre_solve.append(S.dma("s", (lambda c_: lambda e: e.dma_start(
                    out=td[c_], in_=t_full))(c), deps=prod_toks))
                pre_solve.append(S.dma("s", (lambda c_: lambda e: e.dma_start(
                    out=rzd[c_], in_=rz_all))(c), deps=prod_toks + [tkd]))
            z_toks = self.solve_both(t_full, that_full, rz_all, pw_ts, pw_st,
                                     g_all, that_bd, z_all,
                                     pre_solve)
            zd = self.dbgout("zdump", (NT, 128, H * 192), BF16)
            if zd is not None:
                z_toks = z_toks + [S.dma("s", (lambda c_: lambda e:
                    e.dma_start(out=zd[c_], in_=z_all))(c), deps=z_toks)]
            if CHUNK_PARTS < 3:
                prev_chunk_done = [self.S.op("v", lambda e: e.engine_nop(),
                                   deps=z_toks + [tkbg] + lbv)]
                state_tok = prev_chunk_done
                continue
            prev_chunk_done = self.g2_state(
                c, z_all, w_all, qgT, qpt_sb, tmp_odd, bc_sb, bct_sb, mcs,
                z_toks, lbv + [tkbg], state_tok, cg)
            state_tok = prev_chunk_done

        sts = []
        for p in range(H // 2):
            for par in range(2):
                h = 2 * p + par
                sts.append(S.dma("s", (lambda p_, par_, h_: lambda e:
                    e.dma_start(
                        out=io["ags_in"][:, 2 * DH * h_:2 * DH * h_ + DH],
                        in_=self.prun[p_][64 * par_:64 * par_ + 64, :]))(p, par, h),
                    deps=prev_chunk_done))
                sts.append(S.dma("s", (lambda p_, par_, h_: lambda e:
                    e.dma_start(
                        out=io["ags_in"][:, 2 * DH * h_ + DH:2 * DH * (h_ + 1)],
                        in_=self.stloc[p_][64 * par_:64 * par_ + 64, :]))(p, par, h),
                    deps=prev_chunk_done))
        self.ag3 = S.cop(lambda e: e.collective_compute(
            "AllGather", ALU.bypass, replica_groups=[list(range(NCORE))],
            ins=[io["ags_in"].opt()], outs=[io["ags_out"].opt()]),
            deps=sts + [self.ag2])

    def solve_both(self, t_full, that_full, rz_all, pw_ts, pw_st,
                   g_all, that_bd, z_all, pre_toks):
        S, pb = self.S, self.pb
        rzv = rz_all.rearrange("p (h x) -> p h x", x=192)
        zv = z_all.rearrange("p (h x) -> p h x", x=192)
        wave_tok = []
        for wv in range(2):
            hs = [8 * wv + i for i in range(8)]
            tbd = S.op("v", (lambda w_: lambda e: e.tensor_tensor(
                out=that_bd.rearrange("p (g c) -> p g c", c=C),
                in0=that_full.rearrange("p (g c) -> p g c", c=C)
                    [:, 8 * w_:8 * w_ + 8, :],
                in1=self.bdm_bf.unsqueeze(1).broadcast_to([128, 8, C]),
                op=ALU.mult))(wv),
                deps=list(pre_toks) + [self.t_bdm] + wave_tok)
            sq_dep = [list(pre_toks) + wave_tok, list(pre_toks) + wave_tok]
            for lv in range(5):
                mms_ts = [[], []]
                mms_st = [[], []]
                for hh in range(8):
                    h = hs[hh]
                    hf = hh // 4
                    bts = pb[6 + hf]
                    bst = pb[4 + hf]
                    co = 128 * (hh % 4)
                    if lv == 0:
                        for b_ in range(2):
                            r = slice(64 * b_, 64 * b_ + 64)
                            a_ts = t_full[r, C * h + 64 * b_:
                                          C * h + 64 * b_ + 64]
                            a_st = that_full[r, C * h + 64 * b_:
                                             C * h + 64 * b_ + 64]
                            mms_ts[hf].append(S.op("t", (lambda a1, a2, b2, r_,
                                c_: lambda e: e.matmul(
                                    b2[r_, c_:c_ + 64], a1, a2, start=True,
                                    stop=True))(a_st, a_ts, bts, r,
                                                co + 64 * b_),
                                deps=sq_dep[hf]))
                            mms_st[hf].append(S.op("t", (lambda a1, a2, b2, r_,
                                c_: lambda e: e.matmul(
                                    b2[r_, c_:c_ + 64], a1, a2, start=True,
                                    stop=True))(a_ts, a_st, bst, r,
                                                co + 64 * b_),
                                deps=sq_dep[hf]))
                    else:
                        src_ts = pw_ts[(lv - 1) % 2][:, 128 * hh:128 * hh + 128]
                        src_st = pw_st[lv - 1][:, 128 * hh:128 * hh + 128]
                        mms_ts[hf].append(S.op("t", (lambda a1, a2, b2, c_:
                            lambda e: e.matmul(
                                b2[:, c_:c_ + 128], a1, a2, start=True,
                                stop=True))(src_st, src_ts, bts, co),
                            deps=sq_dep[hf]))
                        if lv < 4:
                            mms_st[hf].append(S.op("t", (lambda a1, a2, b2, c_:
                                lambda e: e.matmul(
                                    b2[:, c_:c_ + 128], a1, a2, start=True,
                                    stop=True))(src_ts, src_st, bst, co),
                                deps=sq_dep[hf]))
                for half in range(2):
                    ev = []
                    if lv == 0:
                        ev.append(S.op("v", (lambda hf: lambda e:
                            e.tensor_tensor(
                                out=pw_ts[0][:, 512 * hf:512 * hf + 512]
                                    .rearrange("p (k c) -> p k c", c=C),
                                in0=pb[6 + hf].rearrange(
                                    "p (k c) -> p k c", c=C),
                                in1=self.bdm_bf.unsqueeze(1)
                                    .broadcast_to([128, 4, C]),
                                op=ALU.mult))(half),
                            deps=mms_ts[half] + [self.t_bdm]))
                        ev.append(S.op("v", (lambda hf: lambda e:
                            e.tensor_tensor(
                                out=pw_st[0][:, 512 * hf:512 * hf + 512]
                                    .rearrange("p (k c) -> p k c", c=C),
                                in0=pb[4 + hf].rearrange(
                                    "p (k c) -> p k c", c=C),
                                in1=self.bdm_bf.unsqueeze(1)
                                    .broadcast_to([128, 4, C]),
                                op=ALU.mult))(half),
                            deps=mms_st[half] + [self.t_bdm]))
                    else:
                        ev.append(S.op("a", (lambda hf, lv_: lambda e: e.copy(
                            out=pw_ts[lv_ % 2][:, 512 * hf:512 * hf + 512],
                            in_=pb[6 + hf]))(half, lv), deps=mms_ts[half]))
                        if lv < 4:
                            ev.append(S.op("v", (lambda hf, lv_: lambda e:
                                e.tensor_copy(
                                    out=pw_st[lv_][:, 512 * hf:512 * hf + 512],
                                    in_=pb[4 + hf]))(half, lv),
                                deps=mms_st[half]))
                    sq_dep[half] = ev
            g0h = []
            for half in range(2):
                g0h.append(S.op("v", (lambda hf: lambda e: e.tensor_tensor(
                    out=g_all[:, 512 * hf:512 * hf + 512]
                        .rearrange("p (g c) -> p g c", c=C),
                    in0=pw_ts[0][:, 512 * hf:512 * hf + 512]
                        .rearrange("p (g c) -> p g c", c=C),
                    in1=self.ident_bf.unsqueeze(1).broadcast_to([128, 4, C]),
                    op=ALU.add))(half), deps=sq_dep[half] + [self.t_idb]))
            gh = [[g0h[0]], [g0h[1]]]
            for lvk in (3, 2, 1, 0):
                for half in range(2):
                    mms = []
                    for hh in range(4 * half, 4 * half + 4):
                        co = 128 * (hh % 4)
                        mms.append(S.op("t", (lambda hh_, co_, lv_: lambda e:
                            e.matmul(pb[6 + hh_ // 4][:, co_:co_ + 128],
                                     pw_st[lv_][:, 128 * hh_:128 * hh_ + 128],
                                     g_all[:, 128 * hh_:128 * hh_ + 128],
                                     start=True, stop=True))(hh, co, lvk),
                            deps=gh[half]))
                    gh[half] = [S.op("v", (lambda hf: lambda e:
                        e.tensor_tensor(
                            out=g_all[:, 512 * hf:512 * hf + 512],
                            in0=g_all[:, 512 * hf:512 * hf + 512],
                            in1=pb[6 + hf], op=ALU.add))(half), deps=mms)]
            for half in range(2):
                mms = []
                for hh in range(4 * half, 4 * half + 4):
                    co = 128 * (hh % 4)
                    mms.append(S.op("t", (lambda hh_, co_: lambda e:
                        e.matmul(pb[6 + hh_ // 4][:, co_:co_ + 128],
                                 that_bd[:, 128 * hh_:128 * hh_ + 128],
                                 g_all[:, 128 * hh_:128 * hh_ + 128],
                                 start=True, stop=True))(hh, co),
                        deps=gh[half] + [tbd]))
                gh[half] = [S.op("v", (lambda hf: lambda e: e.tensor_tensor(
                    out=g_all[:, 512 * hf:512 * hf + 512],
                    in0=g_all[:, 512 * hf:512 * hf + 512],
                    in1=pb[6 + hf], op=ALU.subtract))(half), deps=mms)]
            gd = self.dbgout("gdump", (NT, 2, 128, 1024), BF16)
            if gd is not None:
                gtok = gtok + [S.dma("s", (lambda c_, w_: lambda e:
                    e.dma_start(out=gd[c_][w_], in_=g_all))(self._cur_c, wv),
                    deps=gtok)]
            bot = {}
            for hh in range(8):
                h = hs[hh]
                bank, co = 2 + hh // 2, 256 * (hh % 2)
                bot.setdefault(bank, []).append(S.op("t",
                    (lambda h_, bk, co_, hh_: lambda e: e.matmul(
                        self.pb[bk][64:128, co_:co_ + 192],
                        g_all[64:128, 128 * hh_ + 64:128 * hh_ + 128],
                        rzv[:, h_, :][64:128, :], start=True, stop=True))(
                        h, bank, co, hh),
                    deps=gh[hh // 4] + list(pre_toks)))
            evb = {}
            for bank in range(4):
                evb[2 + bank] = [S.op("a", (lambda bk, w_: lambda e: e.copy(
                    out=zv[:, 2 * bk + 8 * w_:2 * bk + 8 * w_ + 2, :][64:128],
                    in_=self.pb[2 + bk][64:128, :]
                        .rearrange("p (h x) -> p h x", x=256)[:, :, 0:192]))(
                        bank, wv), deps=bot[2 + bank])]
            off = {}
            for hh in range(8):
                h = hs[hh]
                bank, co = 2 + hh // 2, 256 * (hh % 2)
                off.setdefault(bank, []).append(S.op("t",
                    (lambda h_, bk, co_: lambda e: e.matmul(
                        self.pb[bk][0:64, co_:co_ + 192],
                        t_full[64:128, C * h_:C * h_ + 64],
                        zv[:, h_, :][64:128, :], start=True, stop=True))(
                        h, bank, co), deps=evb[bank]))
            evy = {}
            for bank in range(4):
                evy[2 + bank] = [S.op("v", (lambda bk, w_: lambda e:
                    e.tensor_tensor(
                    out=rzv[:, 2 * bk + 8 * w_:2 * bk + 8 * w_ + 2, :][0:64],
                    in0=rzv[:, 2 * bk + 8 * w_:2 * bk + 8 * w_ + 2, :][0:64],
                    in1=self.pb[2 + bk][0:64, :]
                        .rearrange("p (h x) -> p h x", x=256)[:, :, 0:192],
                    op=ALU.subtract))(bank, wv), deps=off[2 + bank])]
            top = {}
            for hh in range(8):
                h = hs[hh]
                bank, co = 2 + hh // 2, 256 * (hh % 2)
                top.setdefault(bank, []).append(S.op("t",
                    (lambda h_, bk, co_, hh_: lambda e: e.matmul(
                        self.pb[bk][0:64, co_:co_ + 192],
                        g_all[0:64, 128 * hh_:128 * hh_ + 64],
                        rzv[:, h_, :][0:64, :], start=True, stop=True))(
                        h, bank, co, hh), deps=evy[bank]))
            evt = []
            for bank in range(4):
                evt.append(S.op("a", (lambda bk, w_: lambda e: e.copy(
                    out=zv[:, 2 * bk + 8 * w_:2 * bk + 8 * w_ + 2, :][0:64],
                    in_=self.pb[2 + bk][0:64, :]
                        .rearrange("p (h x) -> p h x", x=256)[:, :, 0:192]))(
                        bank, wv), deps=top[2 + bank]))
            wave_tok = [S.op("v", lambda e: e.engine_nop(),
                            deps=evt + [t for v in evb.values() for t in v])]
        return wave_tok

    def g2_state(self, c, z_all, w_all, qgT, qpt_sb, tmp_odd, bc_sb, bct_sb,
                 mcs, z_toks, w_toks, state_tok, cg):
        S, pb, io = self.S, self.pb, self.io
        zv = z_all.rearrange("p (h x) -> p h x", x=192)
        wav = w_all.rearrange("p (h x) -> p h x", x=128)
        qptv = qpt_sb.rearrange("p (pr x) -> p pr x", x=C)
        mcsv = mcs.rearrange("p (j x) -> p j x", x=DH)
        bcv = bc_sb.rearrange("p (j x) -> p j x", x=DH)
        tmpv = tmp_odd.rearrange("p (j x) -> p j x", x=C)
        bctv = bct_sb.rearrange("p (j x) -> p j x", x=DH)
        wave_done = list(z_toks) + list(w_toks) + list(state_tok)
        all_done = []
        for W in range(2):
            prs = [4 * W + j for j in range(4)]
            s4 = []
            s1l = []
            for j, p in enumerate(prs):
                bA = pb[j]
                bB = pb[4 + j]
                s0 = []
                for par in range(2):
                    h = 2 * p + par
                    r = slice(64 * par, 64 * par + 64)
                    s0.append(S.op("t", (lambda h_, r_, b_: lambda e:
                        e.matmul(b_[r_, 0:192], wav[:, h_, 0:64],
                                 zv[:, h_, :], start=True, stop=True))(
                                 h, r, bA), deps=wave_done))
                    s0.append(S.op("t", (lambda h_, r_, b_: lambda e:
                        e.matmul(b_[r_, 192:384], wav[:, h_, 64:128],
                                 zv[:, h_, :], start=True, stop=True))(
                                 h, r, bA), deps=wave_done))
                s1 = S.op("v", (lambda p_, j_, b_, cg_: lambda e:
                    e.tensor_tensor(
                        out=qptv[:, j_, :],
                        in0=qgT[p_][:, 128 * cg_:128 * (cg_ + 1)],
                        in1=b_[:, 192:320], op=ALU.subtract))(
                        p, j, bA, cg), deps=s0)
                s1l.append(s1)
                s2 = S.dma("s", (lambda c_, p_, j_: lambda e: e.dma_start(
                    out=io["qpt_dram"][c_ * H + 2 * p_:c_ * H + 2 * p_ + 2, :]
                        .rearrange("two (d t) -> (two d) t", d=DH),
                    in_=qptv[:, j_, :]))(c, p, j), deps=[s1])
                s3 = []
                for par in range(2):
                    r = slice(64 * par, 64 * par + 64)
                    s3.append(S.op("t", (lambda p_, r_, j_, b_: lambda e:
                        e.matmul(b_[r_, 0:128],
                                 self.stloc[p_][r_, :], qptv[:, j_, :][r_, :],
                                 start=True, stop=True))(p, r, j, bB),
                        deps=[s1] + list(state_tok)))
                t1 = S.op("a", (lambda j_, b_: lambda e: e.copy(
                    out=tmpv[:, j_, :], in_=b_[:, 0:128]))(j, bB), deps=s3)
                tot = S.op("v", (lambda p_, j_, b_, c_: lambda e:
                    e.tensor_tensor(
                        out=self.Ot[p_][:, C * c_:C * (c_ + 1)],
                        in0=tmpv[:, j_, :], in1=b_[:, 0:128],
                        op=ALU.add))(p, j, bA, c), deps=[t1] + s0)
                ac = S.op("v", (lambda c_, p_, j_, b_: lambda e:
                    e.scalar_tensor_tensor(
                        out=mcsv[:, j_, :], in0=self.i64two_sb,
                        scalar=self.gcp[:, c_ * 8 + p_:c_ * 8 + p_ + 1],
                        in1=b_[:, 320:384], op0=ALU.mult,
                        op1=ALU.subtract))(c, p, j, bA),
                    deps=s0 + [s1, self.t_gcp])
                bcx = S.op("v", (lambda j_, b_: lambda e: e.tensor_copy(
                    out=bcv[:, j_, :], in_=b_[:, 128:192]))(
                    j, bA), deps=s0)
                spr = S.op("v", (lambda p_, c_: lambda e: e.tensor_copy(
                    out=self.prun_store[:, (c_ * (H // 2) + p_) * DH:
                                        (c_ * (H // 2) + p_ + 1) * DH],
                    in_=self.prun[p_]))(p, c), deps=list(state_tok))
                bcts = []
                for par in range(2):
                    r = slice(64 * par, 64 * par + 64)
                    bcts.append(S.op("t", (lambda r_, j_, b_: lambda e:
                        e.transpose(
                            b_[r_, 128:160].bitcast(BF16),
                            bcv[:, j_, :][r_, :], self.i64two_bf[r_, :]))(
                            r, j, bB), deps=[bcx, t1]))
                ebt = S.op("a", (lambda j_, b_: lambda e: e.copy(
                    out=bctv[:, j_, :], in_=b_[:, 128:160]
                    .bitcast(BF16)))(j, bB), deps=bcts)
                mstc = []
                for par in range(2):
                    r = slice(64 * par, 64 * par + 64)
                    mstc.append(S.op("t", (lambda p_, r_, j_, b_: lambda e:
                        e.matmul(b_[r_, 192:256],
                                 mcsv[:, j_, :][r_, :],
                                 self.stloc[p_][r_, :],
                                 start=True, stop=True))(p, r, j, bB),
                        deps=[ac, ebt] + list(state_tok) + s3))
                evst = S.op("v", (lambda p_, j_, b_: lambda e: e.tensor_tensor(
                    out=self.stloc[p_], in0=bctv[:, j_, :],
                    in1=b_[:, 192:256],
                    op=ALU.add))(p, j, bB), deps=mstc + [ebt])
                mprs = []
                for par in range(2):
                    r = slice(64 * par, 64 * par + 64)
                    mprs.append(S.op("t", (lambda p_, r_, j_, b_: lambda e:
                        e.matmul(b_[r_, 384:448],
                                 self.prunT[p_][r_, :], mcsv[:, j_, :][r_, :],
                                 start=True, stop=True))(p, r, j, bA),
                        deps=[ac, spr, tot, bcx]))
                    mprs.append(S.op("t", (lambda p_, r_, j_, b_: lambda e:
                        e.matmul(b_[r_, 448:512],
                                 mcsv[:, j_, :][r_, :], self.prunT[p_][r_, :],
                                 start=True, stop=True))(p, r, j, bA),
                        deps=[ac, spr, tot, bcx]))
                evpr = S.op("a", (lambda p_, j_, b_: lambda e: e.copy(
                    out=self.prun[p_], in_=b_[:, 384:448]))(
                    p, j, bA), deps=mprs + [spr])
                evprt = S.op("a", (lambda p_, j_, b_: lambda e: e.copy(
                    out=self.prunT[p_], in_=b_[:, 448:512]))(p, j, bA),
                    deps=mprs)
                s4 += [tot, evst, evpr, evprt, s2, ac, bcx, ebt]
            wave_done = s4 + s1l
            all_done += s4
        return [S.op("v", lambda e: e.engine_nop(), deps=all_done)]

    def corrphase(self):
        S, io, pb = self.S, self.io, self.pb
        arB, arF = self.arB, self.arF
        ragf = [arB.bf16(128, H * 2 * DH) for _ in range(2)]
        ragb = [arB.bf16(128, H * 2 * DH) for _ in range(2)]
        ipat_f = arF.f32(128, H * 2 * DH)
        sint = [arB.bf16(128, DH) for _ in range(H // 2)]
        mc = arB.bf16(128, DH)
        qptl = arB.bf16(128, 8 * C)
        qv = qptl.rearrange("p (pr t) -> p pr t", t=C)
        lip = S.dma("s", lambda e: e.dma_start(out=ipat_f, in_=io["ipat"]))
        zi = [S.op("v", (lambda p_: lambda e: e.memset(sint[p_], 0.0))(p))
              for p in range(H // 2)]
        ptoks = list(zi)
        rag_busy = [None, None]
        for j in range(NCORE):
            pp = j % 2
            lt = [S.dma("s", (lambda j_, p_: lambda e: e.dma_start(
                      out=ragf[p_][0:64, :],
                      in_=io["ags_out"][64 * j_:64 * (j_ + 1), :]))(j, pp),
                  deps=[self.ag3] + ([rag_busy[pp]] if rag_busy[pp] else [])),
                  S.dma("s", (lambda j_, p_: lambda e: e.dma_start(
                      out=ragf[p_][64:128, :],
                      in_=io["ags_out"][64 * j_:64 * (j_ + 1), :]))(j, pp),
                  deps=[self.ag3] + ([rag_busy[pp]] if rag_busy[pp] else []))]
            t1 = S.op("v", (lambda j_, p_: lambda e: e.tensor_scalar(
                out=ragf[p_], in0=ragf[p_],
                scalar1=self.hm_sb[:, j_:j_ + 1], scalar2=None,
                op0=ALU.mult))(j, pp), deps=lt + [self.pt["hm"]])
            t2 = S.op("v", (lambda j_, p_: lambda e: e.scalar_tensor_tensor(
                out=ragb[p_], in0=ipat_f,
                scalar=self.hm1_sb[:, j_:j_ + 1], in1=ragf[p_],
                op0=ALU.mult, op1=ALU.add))(j, pp),
                deps=[t1, lip, self.t_hm1])
            mms = []
            for p in range(H // 2):
                for par in range(2):
                    h = 2 * p + par
                    r = slice(64 * par, 64 * par + 64)
                    mms.append(S.op("t", (lambda p2, h_, p_, r_: lambda e:
                        e.matmul(
                            pb[0][r_, 64 * p_:64 * (p_ + 1)],
                            ragb[p2][r_, 2 * DH * h_:2 * DH * h_ + DH],
                            sint[p_][r_, :], start=True, stop=True))(pp, h, p, r),
                        deps=[t2] + ptoks))
            ntoks = []
            for p in range(H // 2):
                for par in range(2):
                    h = 2 * p + par
                    r = slice(64 * par, 64 * par + 64)
                    ntoks.append(S.op("v", (lambda p2, h_, p_, r_: lambda e:
                        e.tensor_tensor(
                            out=sint[p_][r_, :],
                            in0=ragb[p2][r_, 2 * DH * h_ + DH:2 * DH * (h_ + 1)],
                            in1=pb[0][r_, 64 * p_:64 * (p_ + 1)],
                            op=ALU.add))(pp, h, p, r),
                        deps=mms))
            rag_busy[pp] = ntoks[-1]
            ptoks = ntoks
        mcl = arB.bf16(128, 512)
        mclv = mcl.rearrange("p (pr x) -> p pr x", x=DH)
        prev = ptoks
        for c in range(NT):
            lq = S.dma("s", (lambda c_: lambda e: e.dma_start(
                out=qptl,
                in_=io["qpt_dram"][16 * c_:16 * (c_ + 1), :]
                .rearrange("(pr two) (d t) -> two d pr t", two=2, d=DH)))(c),
                deps=prev)
            mm1s = []
            for p in range(H // 2):
                for par in range(2):
                    r = slice(64 * par, 64 * par + 64)
                    mm1s.append(S.op("t", (lambda c_, p_, r_: lambda e:
                        e.matmul(
                            pb[1][r_, 64 * p_:64 * (p_ + 1)],
                            self.prun_store[r_, (c_ * (H // 2) + p_) * DH:
                                            (c_ * (H // 2) + p_ + 1) * DH],
                            sint[p_][r_, :], start=True, stop=True))(c, p, r),
                        deps=prev))
            evm = S.op("a", lambda e: e.copy(out=mcl, in_=pb[1]),
                       deps=mm1s)
            mm2s = []
            for p in range(H // 2):
                for par in range(2):
                    r = slice(64 * par, 64 * par + 64)
                    mm2s.append(S.op("t", (lambda p_, r_: lambda e: e.matmul(
                        pb[2 + p_ // 4][r_, 128 * (p_ % 4):128 * (p_ % 4) + 128],
                        mclv[:, p_, :][r_, :], qv[:, p_, :][r_, :],
                        start=True, stop=True))(p, r), deps=[evm, lq]))
            ntoks = []
            for p in range(H // 2):
                ntoks.append(S.op("v", (lambda c_, p_: lambda e:
                    e.tensor_tensor(
                        out=self.Ot[p_][:, C * c_:C * (c_ + 1)],
                        in0=self.Ot[p_][:, C * c_:C * (c_ + 1)],
                        in1=pb[2 + p_ // 4][:, 128 * (p_ % 4):
                                            128 * (p_ % 4) + 128],
                        op=ALU.add))(c, p), deps=mm2s))
            prev = ntoks
        self.corr_done = prev

    def woutphase(self):
        S, io, pb = self.S, self.io, self.pb
        arB = self.arB
        woutT = [arB.bf16(128, D) for _ in range(NKC)]
        wl = []
        for k in range(NKC):
            wl.append(S.dma("s", (lambda k_, d_: lambda e: e.dma_start(
                out=d_, in_=io["wout_dram"][k_], transpose=True))(k, woutT[k]),
                deps=self.w_wout))
        wf = []
        for k in range(NKC):
            wf.append(S.op("v", (lambda k_: lambda e: e.tensor_tensor(
                out=woutT[k_], in0=woutT[k_], in1=self.rep["ga"],
                op=ALU.mult))(k), deps=[wl[k], self.t_ga]))
        x_t = self.x_t
        x2_t = self.ln_xs
        x2_toks = []
        prev = list(self.corr_done)
        for i in range(NT):
            lx = S.dma("s", (lambda i_: lambda e: e.dma_start(
                out=x_t, in_=io["x_in"][128 * i_:128 * (i_ + 1), :]))(i),
                deps=prev)
            mms = [None, None]
            for k in range(NKC):
                for n in range(2):
                    dd = [wf[k]] + list(self.corr_done)
                    if mms[n]:
                        dd.append(mms[n])
                    if k == 0:
                        dd += x2_toks[-3:]
                    mms[n] = S.op("t", (lambda i_, k_, n_: lambda e: e.matmul(
                        pb[n_], self.Ot[k_][:, 128 * i_:128 * (i_ + 1)],
                        woutT[k_][:, 512 * n_:512 * (n_ + 1)],
                        start=(k_ == 0), stop=(k_ == NKC - 1)))(i, k, n),
                        deps=dd)
            ev = []
            for n in range(2):
                ev.append(S.op("v", (lambda n_: lambda e: e.tensor_tensor(
                    out=x2_t[:, 512 * n_:512 * (n_ + 1)],
                    in0=x_t[:, 512 * n_:512 * (n_ + 1)], in1=pb[n_],
                    op=ALU.add))(n), deps=[mms[n], lx]))
            st = S.dma("s", (lambda i_: lambda e: e.dma_start(
                out=io["x2_dram"][128 * i_:128 * (i_ + 1), :], in_=x2_t))(i),
                deps=ev)
            x2_toks = ev + [st]
            prev = [st]
        self.x2_done = prev

    def ffnphase(self):
        S, io, pb = self.S, self.io, self.pb
        arB, arF = self.arB, self.arF
        self.set_qscr(1408, dual=True)
        x_t = self.x_t
        silu_sb = arF.f32(128, 1408)
        h2f = silu_sb[:, 0:D]
        h2q = arB.bf16(128, D)
        qtoks = []
        prev = list(self.x2_done)
        for i in range(NT):
            lx = S.dma("s", (lambda i_: lambda e: e.dma_start(
                out=x_t, in_=io["x2_dram"][128 * i_:128 * (i_ + 1), :]))(i),
                deps=prev)
            th = self.ln_mod(x_t, self.rep["cb"], self.rep["sb2"], h2f, [lx])
            tq = self.quant_seq(h2f, h2q, [th] + prev, eng="v")
            st = S.dma("s", (lambda i_: lambda e: e.dma_start(
                out=io["h2q_dram"][:, 128 * i_:128 * (i_ + 1), :]
                    .rearrange("c p k -> p c k"), in_=h2q))(i), deps=[tq])
            qtoks.append(st)
            prev = [st, tq]
        h2qT_1 = [arB.bf16(128, 512) for _ in range(NKC)]
        h2qT = [h2qT_1, h2qT_1]
        wgu = [arB.bf16(128, 2816) for _ in range(NKC)]
        hm_f = arF.f32(128, 1408)
        hmq = arB.bf16(128, 1408)
        hql_tok = [[None] * NKC, [None] * NKC]
        gstore = []
        prev_readers = []
        prev_q = []
        prev_mult = []
        nsub = ((0, 512), (512, 512), (1024, 384))
        for G in range(2):
            wls = []
            for k in range(NKC):
                wls.append(S.dma("s", (lambda k_, G_, d_: lambda e: e.dma_start(
                    out=d_[:, 0:1408],
                    in_=io["agw_out"][128 * k_:128 * (k_ + 1),
                                      1408 * G_:1408 * (G_ + 1)]))(k, G, wgu[k]),
                    deps=[self.ag1] + gstore[-2:]))
                wls.append(S.dma("s", (lambda k_, G_, d_: lambda e: e.dma_start(
                    out=d_[:, 1408:2816],
                    in_=io["agw_out"][128 * k_:128 * (k_ + 1),
                                      IF + 1408 * G_:IF + 1408 * (G_ + 1)]))(k, G, wgu[k]),
                    deps=[self.ag1] + gstore[-2:]))
            for i in range(NT):
                qtr, buf = i // 4, (i // 4) % 2
                if i % 4 == 0:
                    for k in range(NKC):
                        hql_tok[buf][k] = S.dma("s", (lambda k_, d_, q_:
                            lambda e: e.dma_start(
                                out=d_,
                                in_=io["h2q_dram"][k_][512 * q_:512 * (q_ + 1), :],
                                transpose=True))(k, h2qT[buf][k], qtr),
                            deps=qtoks + gstore[-12:] +
                                 ([hql_tok[buf][k]] if hql_tok[buf][k] else []))
                io_i = 128 * (i % 4)
                mms = [None] * 6
                for k in range(NKC):
                    for n in range(6):
                        half_gu, n3 = divmod(n, 3)
                        src_c = 1408 * half_gu + nsub[n3][0]
                        w_ = nsub[n3][1]
                        dd = [wls[2 * k], wls[2 * k + 1], hql_tok[buf][k]]
                        if mms[n]:
                            dd.append(mms[n])
                        if k == 0:
                            dd += prev_readers
                        mms[n] = S.op("t", (lambda k_, n_, sc_, w2_, b2, o_:
                            lambda e: e.matmul(
                                pb[n_][:, 0:w2_],
                                h2qT[b2][k_][:, o_:o_ + 128],
                                wgu[k_][:, sc_:sc_ + w2_],
                                start=(k_ == 0), stop=(k_ == NKC - 1)))(
                                k, n, src_c, w_, buf, io_i), deps=dd)
                sil = []
                for n3, (c0, w_) in enumerate(nsub):
                    sil.append(S.op("a", (lambda n_, c0_, w2_: lambda e:
                        e.activation(out=silu_sb[:, c0_:c0_ + w2_],
                                     in_=pb[n_][:, 0:w2_], func=AF.Silu))(
                                     n3, c0, w_), deps=[mms[n3]] + prev_mult))
                mult = []
                for n3, (c0, w_) in enumerate(nsub):
                    mult.append(S.op("v", (lambda n_, c0_, w2_: lambda e:
                        e.tensor_tensor(
                            out=hm_f[:, c0_:c0_ + w2_],
                            in0=silu_sb[:, c0_:c0_ + w2_],
                            in1=pb[3 + n_][:, 0:w2_], op=ALU.mult))(n3, c0, w_),
                        deps=[sil[n3], mms[3 + n3]] + prev_q))
                qeng = "v"
                tq = self.quant_seq(hm_f, hmq, mult + prev_q, eng=qeng)
                prev_q = self._q_src
                st = S.dma("s", (lambda i_, G_: lambda e: e.dma_start(
                    out=io["hmid_dram"][11 * G_:11 * (G_ + 1),
                                        128 * i_:128 * (i_ + 1), :]
                        .rearrange("c p k -> p c k"), in_=hmq))(i, G),
                    deps=[tq])
                gstore.append(st)
                prev_q = prev_q + [st]
                prev_readers = sil + mult
                prev_mult = mult
        self.epoch()
        NIF = IF // 128
        wdnT = [arB.bf16(128, D) for _ in range(NIF)]
        wdl = []
        for k in range(NIF):
            wdl.append(S.dma("s", (lambda k_, d_: lambda e: e.dma_start(
                out=d_, in_=io["agd_out"][128 * k_:128 * (k_ + 1), :]))(k, wdnT[k]),
                deps=[self.ag2]))
        wdf = []
        for k in range(NIF):
            wdf.append(S.op("v", (lambda k_: lambda e: e.tensor_tensor(
                out=wdnT[k_], in0=wdnT[k_], in1=self.rep["gb"],
                op=ALU.mult))(k), deps=[wdl[k], self.t_gb]))
        hmT = [arB.bf16(128, 512) for _ in range(NIF)]
        out_toks = []
        prev_ev = []
        for qtr in range(NT // 4):
            htl = []
            for k in range(NIF):
                htl.append(S.dma("s", (lambda k_, d_, q_: lambda e: e.dma_start(
                    out=d_,
                    in_=io["hmid_dram"][k_][512 * q_:512 * (q_ + 1), :],
                    transpose=True))(k, hmT[k], qtr),
                    deps=gstore + out_toks[-3:]))
            for i2 in range(4):
                i = 4 * qtr + i2
                lx = S.dma("s", (lambda i_: lambda e: e.dma_start(
                    out=x_t, in_=io["x2_dram"][128 * i_:128 * (i_ + 1), :]))(i),
                    deps=out_toks[-1:] if out_toks else gstore[-1:])
                mms = [None, None]
                for k in range(NIF):
                    for n in range(2):
                        dd = [htl[k], wdf[k]]
                        if mms[n]:
                            dd.append(mms[n])
                        if k == 0:
                            dd += prev_ev
                        mms[n] = S.op("t", (lambda i2_, k_, n_: lambda e:
                            e.matmul(pb[6 + n_],
                                     hmT[k_][:, 128 * i2_:128 * (i2_ + 1)],
                                     wdnT[k_][:, 512 * n_:512 * (n_ + 1)],
                                     start=(k_ == 0), stop=(k_ == NIF - 1)))(
                                     i2, k, n), deps=dd)
                ev = []
                for n in range(2):
                    ev.append(S.op("v", (lambda n_: lambda e: e.tensor_tensor(
                        out=x_t[:, 512 * n_:512 * (n_ + 1)],
                        in0=x_t[:, 512 * n_:512 * (n_ + 1)], in1=pb[6 + n_],
                        op=ALU.add))(n), deps=[mms[n], lx]))
                st = S.dma("s", (lambda i_: lambda e: e.dma_start(
                    out=io["out_x"][128 * i_:128 * (i_ + 1), :], in_=x_t))(i),
                    deps=ev)
                out_toks.append(st)
                prev_ev = list(ev)
        self.out_toks = out_toks

    def finish(self):
        S = self.S
        S.op("v", lambda e: e.engine_nop(), deps=self.out_toks)
        S.op("s", lambda e: e.nop(), deps=self.out_toks)


_NC_CACHE = {}


def _get_nc(debug=()):
    key = tuple(sorted(debug))
    if key not in _NC_CACHE:
        _NC_CACHE[key] = build_nc(key)
    return _NC_CACHE[key]


def host_inputs(full):
    f32 = np.float32
    x = np.ascontiguousarray(full["L0_attn_x"], dtype=f32)
    s_idx = np.arange(128)[:, None]
    t_idx = np.arange(128)[None, :]
    mask_ts = np.where(s_idx > t_idx, 0.0, BIG).astype(f32)
    mask_sts = np.where(s_idx < t_idx, 0.0, -BIG).astype(f32)
    mask_sti = np.where(s_idx <= t_idx, 0.0, -BIG).astype(f32)
    i64 = np.eye(64, dtype=f32)
    i64two = np.concatenate([i64, i64], axis=0)
    i64rep8 = np.tile(i64two, (1, 8))
    ipat = np.zeros((128, H * 2 * DH), f32)
    for hh in range(H):
        ipat[0:64, 2 * DH * hh:2 * DH * hh + DH] = i64
        ipat[64:128, 2 * DH * hh:2 * DH * hh + DH] = i64
    shared = {
        "Wqkv": np.ascontiguousarray(full["L0_attn_Wqkv"], dtype=f32),
        "Wg": np.ascontiguousarray(full["L0_attn_Wg"], dtype=f32),
        "bg": np.ascontiguousarray(full["L0_attn_bg"], dtype=f32).reshape(1, H),
        "Wb": np.ascontiguousarray(full["L0_attn_Wb"], dtype=f32),
        "bb": np.ascontiguousarray(full["L0_attn_bb"], dtype=f32).reshape(1, H),
        "Wout": np.ascontiguousarray(full["L0_attn_Wout"], dtype=f32),
        "c_vec": np.ascontiguousarray(full["L0_c"], dtype=f32),
        "wmod": np.ascontiguousarray(full["L0_wmod"], dtype=f32),
        "ident128": np.eye(128, dtype=f32),
        "mask_ts": mask_ts, "mask_sts": mask_sts, "mask_sti": mask_sti,
        "i64two": i64two, "i64rep8": i64rep8, "ipat": ipat,
        "bdmask": ((s_idx < 64) == (t_idx < 64)).astype(f32),
    }
    in_maps = []
    for core in range(NCORE):
        b, q = divmod(core, 4)
        m = dict(shared)
        m["x_slice"] = np.ascontiguousarray(x[b, TQ * q:TQ * (q + 1)])
        m["wgate_sh"] = np.ascontiguousarray(
            full["L0_ffn_w_gate"][:, 128 * core:128 * (core + 1)], dtype=f32)
        m["wup_sh"] = np.ascontiguousarray(
            full["L0_ffn_w_up"][:, 128 * core:128 * (core + 1)], dtype=f32)
        m["wdown_sh"] = np.ascontiguousarray(
            full["L0_ffn_w_down"][:, 352 * core:352 * (core + 1)], dtype=f32)
        sel = np.zeros((B, 128), f32)
        sel[b, :] = 1.0
        m["sel128"] = sel
        hm = np.zeros((1, NCORE), f32)
        hm[0, 4 * b:4 * b + q] = 1.0
        m["hopmask"] = hm
        in_maps.append(m)
    return in_maps


def kernel(**inputs):
    nc, dbg = _get_nc()
    in_maps = host_inputs(inputs)
    res = run_bass_kernel_spmd(nc, in_maps, core_ids=list(range(NCORE)))
    out = np.zeros((B, T, D), np.float32)
    for core in range(NCORE):
        b, q = divmod(core, 4)
        out[b, TQ * q:TQ * (q + 1)] = res.results[core]["out_x"]
    return out


# version 14
# speedup vs baseline: 1.1674x; 1.0239x over previous
import numpy as np
import ml_dtypes
import concourse.bass as bass
import concourse.mybir as mybir
from concourse.bass_utils import run_bass_kernel_spmd

F32 = mybir.dt.float32
BF16 = mybir.dt.bfloat16
AF = mybir.ActivationFunctionType
ALU = mybir.AluOpType
AX = mybir.AxisListType

MAGIC = float(np.float32(1.5 * 2**23))
PHASE_LIMIT = 6
CHUNK_PARTS = 3
BIG = 1e30

B, T, D = 2, 8192, 1024
NCORE = 8
TQ = 2048
NT = TQ // 128
H, DH = 16, 64
C = 128
IF = 2816
NKC = D // 128

ENGINES = ("v", "a", "t", "g", "s")


class Tok:
    __slots__ = ("sem", "val")

    def __init__(self, sem, val):
        self.sem = sem
        self.val = val


class Sched:
    def __init__(self, nc, pool_sizes=None):
        self.nc = nc
        self._ctx = []
        self.sems = {}
        self.cnt = {}
        self.prog = {e: [] for e in ENGINES}
        self.last_waited = {e: {} for e in ENGINES}
        for e in ENGINES:
            self.sems[e] = self._mksem(f"sem_{e}")
            self.cnt[e] = 0
        self.sems["c"] = self._mksem("sem_c")
        self.cnt["c"] = 0
        ps = pool_sizes or {"s": 24, "a": 10, "g": 10}
        self.pools = {}
        for q, n in ps.items():
            self.pools[q] = {
                "sems": [self._mksem(f"sem_{q}dma{i}") for i in range(n)],
                "cnt": [0] * n,
                "next": 0,
            }

    def _mksem(self, name):
        cm = self.nc.semaphore(name)
        sem = cm.__enter__()
        self._ctx.append(cm)
        return sem

    def _emit_waits(self, eng_key, deps):
        waits = []
        lw = self.last_waited[eng_key]
        for d in deps:
            if d is None:
                continue
            key = id(d.sem)
            if lw.get(key, -1) >= d.val:
                continue
            lw[key] = d.val
            waits.append((d.sem, d.val))
        return waits

    def op(self, e, fn, deps=()):
        waits = self._emit_waits(e, deps)
        sem = self.sems[e]
        self.cnt[e] += 1
        self.prog[e].append((waits, fn, sem, 1))
        return Tok(sem, self.cnt[e])

    def cop(self, fn, deps=()):
        waits = self._emit_waits("g", deps)
        sem = self.sems["c"]
        self.cnt["c"] += 1
        self.prog["g"].append((waits, fn, sem, 1))
        return Tok(sem, self.cnt["c"])

    def dma(self, q, fn, deps=()):
        pool = self.pools[q]
        i = pool["next"]
        pool["next"] = (i + 1) % len(pool["sems"])
        sem = pool["sems"][i]
        prev = pool["cnt"][i]
        deps = list(deps)
        if prev:
            deps.append(Tok(sem, prev))
        waits = self._emit_waits(q, deps)
        pool["cnt"][i] = prev + 16
        self.prog[q].append((waits, fn, sem, 16))
        return Tok(sem, pool["cnt"][i])

    def gdma(self, fn, deps=()):
        return self.dma("g", fn, deps)

    def etok(self, e):
        return Tok(self.sems[e], self.cnt[e])

    def barrier(self):
        deps = [Tok(self.sems[e], self.cnt[e]) for e in ENGINES]
        if self.cnt["c"]:
            deps.append(Tok(self.sems["c"], self.cnt["c"]))
        for q, pool in self.pools.items():
            for i, cv in enumerate(pool["cnt"]):
                if cv:
                    deps.append(Tok(pool["sems"][i], cv))
        hub = self.op("v", lambda e: e.engine_nop(), deps=deps)
        for e in ENGINES:
            if e != "v":
                self.op(e, lambda en: en.engine_nop()
                        if hasattr(en, "engine_nop") else en.nop(),
                        deps=[hub])
            self.last_waited[e] = {}
        self.hub = hub
        return hub

    def finalize(self, block):
        def mk(e):
            def body(eng):
                for waits, fn, sem, inc in self.prog[e]:
                    for (s, vv) in waits:
                        eng.wait_ge(s, vv)
                    inst = fn(eng)
                    if inst is None:
                        raise RuntimeError(f"op fn on {e} returned None")
                    inst.then_inc(sem, inc)
            return body

        if self.prog["g"]:
            block.gpsimd(mk("g"))
        if self.prog["s"]:
            block.sync(mk("s"))
        if self.prog["t"]:
            block.tensor(mk("t"))
        if self.prog["v"]:
            block.vector(mk("v"))
        if self.prog["a"]:
            block.scalar(mk("a"))


class Arena:
    def __init__(self, nc, name, cols_f32):
        self.t = nc.alloc_sbuf_tensor(name, [128, cols_f32], F32).ap()
        self.cols = cols_f32
        self.off = 0

    def reset(self):
        self.off = 0

    def f32(self, p, n):
        a = self.t[:, self.off:self.off + n]
        self.off += n
        assert self.off <= self.cols, f"arena overflow {self.off}>{self.cols}"
        return a[0:p, :] if p != 128 else a

    def bf16(self, p, n):
        slots = (n + 1) // 2
        a = self.t[:, self.off:self.off + slots].bitcast(BF16)
        self.off += slots
        assert self.off <= self.cols, f"arena overflow {self.off}>{self.cols}"
        return a[0:p, 0:n] if p != 128 else a[:, 0:n]


def build_nc(debug=()):
    nc = bass.Bass("TRN2", target_bir_lowering=False)
    io = {}

    def inp(name, shape, dt=F32):
        return nc.dram_tensor(name, list(shape), dt, kind="ExternalInput").ap()

    io["x_in"] = inp("x_slice", (TQ, D))
    io["wqkv"] = inp("Wqkv", (3072, D))
    io["wg"] = inp("Wg", (H, D))
    io["bgv"] = inp("bg", (1, H))
    io["wb"] = inp("Wb", (H, D))
    io["bbv"] = inp("bb", (1, H))
    io["wout"] = inp("Wout", (D, D))
    io["wgate_sh"] = inp("wgate_sh", (IF, 128))
    io["wup_sh"] = inp("wup_sh", (IF, 128))
    io["wdown_sh"] = inp("wdown_sh", (D, IF // NCORE))
    io["c_in"] = inp("c_vec", (B, D))
    io["wmod"] = inp("wmod", (D, 6 * D))
    io["sel128"] = inp("sel128", (B, 128))
    io["hopmask"] = inp("hopmask", (1, NCORE))
    io["ident128"] = inp("ident128", (128, 128))
    io["mask_ts"] = inp("mask_ts", (128, 128))
    io["mask_sts"] = inp("mask_sts", (128, 128))
    io["mask_sti"] = inp("mask_sti", (128, 128))
    io["i64two"] = inp("i64two", (128, 64))
    io["i64rep8"] = inp("i64rep8", (128, 512))
    io["bdmask"] = inp("bdmask", (128, 128))
    io["ipat"] = inp("ipat", (128, H * 2 * DH))
    io["out_x"] = nc.dram_tensor("out_x", [TQ, D], F32, kind="ExternalOutput").ap()

    dbg = {}

    def dbgout(name, shape=None, dt=F32):
        if name in debug and name not in dbg:
            dbg[name] = nc.dram_tensor("dbg_" + name, list(shape), dt,
                                       kind="ExternalOutput").ap()
        return dbg.get(name)

    dr = lambda name, shape, dt: nc.dram_tensor(name, list(shape), dt).ap()
    io["h_dram"] = dr("h_dram", (NKC, TQ, 128), BF16)
    io["wall_dram"] = dr("wall_dram", (NKC, 3104, 128), BF16)
    io["wout_dram"] = dr("wout_dram", (NKC, D, 128), BF16)
    io["kpl_dram"] = dr("kpl_dram", (NKC, TQ, 128), BF16)
    io["kb_dram"] = dr("kb_dram", (NKC, TQ, 128), BF16)
    io["q_dram"] = dr("q_dram", (NKC, TQ, 128), BF16)
    io["qg_dram"] = dr("qg_dram", (NKC, TQ, 128), BF16)
    io["bv_dram"] = dr("bv_dram", (NKC, TQ, 128), BF16)
    io["l_dram"] = dr("l_dram", (NT, H * C), F32)
    io["lc_dram"] = dr("lc_dram", (1, NT * H), F32)
    io["qpt_dram"] = dr("qpt_dram", (NT * H, DH * C), BF16)
    io["x2_dram"] = dr("x2_dram", (TQ, D), F32)
    io["h2q_dram"] = dr("h2q_dram", (NKC, TQ, 128), BF16)
    io["hmid_dram"] = dr("hmid_dram", (IF // 128, TQ, 128), BF16)
    io["agw_in"] = dr("agw_in", (128, 2 * IF), BF16)
    io["agw_out"] = dr("agw_out", (NCORE * 128, 2 * IF), BF16)
    io["agd_in"] = dr("agd_in", (IF // NCORE, D), BF16)
    io["agd_out"] = dr("agd_out", (IF, D), BF16)
    io["ags_in"] = dr("ags_in", (DH, H * 2 * DH), BF16)
    io["ags_out"] = dr("ags_out", (NCORE * DH, H * 2 * DH), BF16)

    with nc.Block() as block:
        S = Sched(nc)
        K = Kern(nc, S, io, dbgout)
        K.emit()
        S.finalize(block)
    return nc, dbg


class Kern:
    def __init__(self, nc, S, io, dbgout):
        self.nc = nc
        self.S = S
        self.io = io
        self.dbgout = dbgout
        self.sb = lambda name, shape, dt=F32: nc.alloc_sbuf_tensor(
            name, list(shape), dt).ap()
        self.pb = [nc.alloc_psum_tensor(f"pb{i}", [128, 512], F32).ap()
                   for i in range(8)]
        self.arF = Arena(nc, "arF", 6200)
        self.arB = Arena(nc, "arB", 18800)

    def epoch(self):
        self.S.barrier()
        self.arF.reset()
        self.arB.reset()

    def emit(self):
        lim = PHASE_LIMIT
        self.persist()
        self.prologue()
        if lim >= 1:
            self.ffn_weight_quant()
            self.weights_attn()
        self.epoch()
        if lim >= 2:
            self.hphase()
        if lim >= 3:
            self.qkvphase()
        self.epoch()
        if lim >= 4:
            self.chunkphase()
        self.epoch()
        if lim >= 5:
            self.corrphase()
            self.woutphase()
        self.epoch()
        if lim >= 6:
            self.ffnphase()
            self.finish()
        else:
            self.dummy_out()

    def dummy_out(self):
        S, io = self.S, self.io
        z = S.op("v", lambda e: e.memset(self.x_t, 0.0),
                 deps=[S.hub])
        toks = []
        for i in range(NT):
            toks.append(S.dma("s", (lambda i_: lambda e: e.dma_start(
                out=io["out_x"][128 * i_:128 * (i_ + 1), :],
                in_=self.x_t))(i), deps=[z]))
        S.op("v", lambda e: e.engine_nop(), deps=toks)
        S.op("s", lambda e: e.nop(), deps=toks)

    def persist(self):
        S, io, sb = self.S, self.io, self.sb
        self.ident_sb = sb("ident_sb", (128, 128))
        self.mask_ts_sb = sb("mask_ts_sb", (128, 128))
        self.mask_sts_sb = sb("mask_sts_sb", (128, 128))
        self.mask_sti_sb = sb("mask_sti_sb", (128, 128))
        self.i64two_sb = sb("i64two_sb", (128, 64))
        self.i64two_bf = sb("i64two_bf", (128, 64), BF16)
        self.i64rep8f = sb("i64rep8f", (128, 512))
        self.i64rep8b = sb("i64rep8b", (128, 512), BF16)
        self.selF_sb = sb("selF_sb", (B, 128))
        self.hm_sb = sb("hm_sb", (128, NCORE))
        self.hm1_sb = sb("hm1_sb", (128, NCORE))
        self.bgbb_rep = sb("bgbb_rep", (128, 2 * H))
        t = {}
        t["id"] = S.dma("s", lambda e: e.dma_start(out=self.ident_sb, in_=io["ident128"]))
        t["mts"] = S.dma("s", lambda e: e.dma_start(out=self.mask_ts_sb, in_=io["mask_ts"]))
        t["msts"] = S.dma("s", lambda e: e.dma_start(out=self.mask_sts_sb, in_=io["mask_sts"]))
        t["msti"] = S.dma("s", lambda e: e.dma_start(out=self.mask_sti_sb, in_=io["mask_sti"]))
        t["i2"] = S.dma("s", lambda e: e.dma_start(out=self.i64two_sb, in_=io["i64two"]))
        t["i8"] = S.dma("s", lambda e: e.dma_start(out=self.i64rep8f, in_=io["i64rep8"]))
        self.bdm_f = sb("bdm_f", (128, 128))
        self.bdm_bf = sb("bdm_bf", (128, 128), BF16)
        self.odm_bf = sb("odm_bf", (128, 128), BF16)
        t["bdm"] = S.dma("s", lambda e: e.dma_start(out=self.bdm_f, in_=io["bdmask"]))
        t["sel"] = S.dma("s", lambda e: e.dma_start(out=self.selF_sb, in_=io["sel128"]))
        t["hm"] = S.dma("s", lambda e: e.dma_start(
            out=self.hm_sb, in_=io["hopmask"].partition_broadcast(128)))
        t["bgr"] = S.dma("s", lambda e: e.dma_start(
            out=self.bgbb_rep[:, 0:H], in_=io["bgv"].partition_broadcast(128)))
        t["bbr"] = S.dma("s", lambda e: e.dma_start(
            out=self.bgbb_rep[:, H:2 * H], in_=io["bbv"].partition_broadcast(128)))
        self.t_hm1 = S.op("v", lambda e: e.tensor_scalar(
            out=self.hm1_sb, in0=self.hm_sb, scalar1=-1.0, scalar2=1.0,
            op0=ALU.mult, op1=ALU.add), deps=[t["hm"]])
        self.ident_bf = self.sb("ident_bf", (128, 128), BF16)
        self.t_idb = S.op("v", lambda e: e.tensor_copy(
            out=self.ident_bf, in_=self.ident_sb), deps=[t["id"]])
        self.t_i64b = S.op("v", lambda e: e.tensor_copy(
            out=self.i64two_bf, in_=self.i64two_sb), deps=[t["i2"]])
        self.t_i8b = S.op("v", lambda e: e.tensor_copy(
            out=self.i64rep8b, in_=self.i64rep8f), deps=[t["i8"]])
        self.t_bdm = S.op("v", lambda e: e.tensor_copy(
            out=self.bdm_bf, in_=self.bdm_f), deps=[t["bdm"]])
        self.t_odm = S.op("v", lambda e: e.tensor_scalar(
            out=self.odm_bf, in0=self.ident_sb, scalar1=-1.0, scalar2=1.0,
            op0=ALU.mult, op1=ALU.add), deps=[t["id"]])
        self.t_pbz = [S.op("v", (lambda b_: lambda e: e.memset(
            self.pb[b_], 0.0))(b)) for b in range(8)]
        self.pt = t
        self.ln_m = sb("ln_m", (128, 1))
        self.ln_r = sb("ln_r", (128, 1))
        self.ln_t1 = sb("ln_t1", (128, 1))
        self.ln_t2 = sb("ln_t2", (128, 1))
        self.ln_xs = sb("ln_xs", (128, D))
        self.L_all = sb("L_all", (H, NT * C))
        self.LT = [sb(f"LT{i}", (128, H)) for i in range(NT)]
        self.expL = [sb(f"expL{i}", (128, H)) for i in range(NT)]
        self.b_sig = [sb(f"bsig{i}", (128, H)) for i in range(NT)]
        self.rnk_t = [sb(f"rnk{i}", (128, H)) for i in range(NT)]
        self.rnq_t = [sb(f"rnq{i}", (128, H)) for i in range(NT)]
        self.gcrep = sb("gcrep", (128, NT * H))
        self.gcp = sb("gcp", (128, NT * 8))
        self.zeroHC = sb("zeroHC", (H, C))
        self.t_zero = S.op("v", lambda e: e.memset(self.zeroHC, 0.0))
        self.Ot = [sb(f"Ot{p}", (128, TQ), BF16) for p in range(H // 2)]
        self.stloc = [sb(f"stloc{p}", (128, DH), BF16) for p in range(H // 2)]
        self.prun = [sb(f"prun{p}", (128, DH), BF16) for p in range(H // 2)]
        self.prunT = [sb(f"prunT{p}", (128, DH), BF16) for p in range(H // 2)]
        self.prun_store = sb("prun_store", (128, NT * (H // 2) * DH), BF16)
        self.rep = {nm: sb(f"rep_{nm}", (128, D))
                    for nm in ("sa", "ca", "ga", "sb2", "cb", "gb")}
        self.x_t = sb("x_t", (128, D))
        self.h_bf = sb("h_bf", (128, D), BF16)
        self.negLT = sb("negLT", (128, H))
        self.fdt = sb("fdt", (H, C))
        self.sc16c = sb("sc16c", (128, H))
        self.lcc = sb("lcc", (H, NT))
        self.lcct = sb("lcct", (NT, H))

    def set_qscr(self, width, dual=False):
        arF, arB = self.arF, self.arB

        def mk():
            return (arF.f32(128, width // 16), arF.f32(128, width // 16),
                    arF.f32(128, width), arB.bf16(128, width),
                    arB.bf16(128, width), arB.bf16(128, width))
        self.qscr_by_eng = {"v": mk()}
        self.qscr_by_eng["g"] = mk() if dual else self.qscr_by_eng["v"]

    def prologue(self):
        S, io, sb, pb = self.S, self.io, self.sb, self.pb
        arF, arB = self.arF, self.arB
        c_sb = arF.f32(B, D)
        mod_sb = self.arB.f32(B, 6 * D)
        sc_sb = arB.bf16(B, D)
        sct_sb = arB.bf16(128, 2 * NKC)
        wm_sb = [arB.bf16(128, 3 * D) for _ in range(2)]
        lc0 = S.dma("s", lambda e: e.dma_start(out=c_sb, in_=io["c_in"]))
        tsc = S.op("a", lambda e: e.activation(out=sc_sb, in_=c_sb,
                   func=AF.Silu), deps=[lc0])
        ps_t = pb[7][:, 0:8].bitcast(BF16)
        prev = None
        for k in range(NKC):
            tk = S.op("t", (lambda k_: lambda e: e.transpose(
                ps_t[:, 0:2], sc_sb[:, 128 * k_:128 * (k_ + 1)],
                self.ident_bf[0:2, 0:2]))(k),
                deps=[tsc, self.t_idb, self.t_pbz[7]] +
                     ([prev] if prev else []))
            prev = S.op("v", (lambda k_: lambda e: e.tensor_copy(
                out=sct_sb[:, 2 * k_:2 * k_ + 2], in_=ps_t[:, 0:2]))(k),
                deps=[tk])
        sct_tok = prev
        wm_tok = [None, None]
        mod_evacs = []
        for half in range(2):
            mm_last = [None] * 6
            prev_evacs = list(mod_evacs)
            for k in range(NKC):
                buf = k % 2
                lw = S.gdma((lambda k_, h_, b_: lambda e: e.dma_start(
                    out=wm_sb[b_],
                    in_=io["wmod"][128 * k_:128 * (k_ + 1),
                                   3072 * h_:3072 * (h_ + 1)]))(k, half, buf),
                    deps=[wm_tok[buf]] if wm_tok[buf] else [])
                for n in range(6):
                    deps = [lw, sct_tok] + (
                        prev_evacs + [self.t_pbz[n]] if k == 0 else [])
                    if mm_last[n]:
                        deps.append(mm_last[n])
                    mm_last[n] = S.op("t", (lambda k_, n_, b_: lambda e:
                        e.matmul(pb[n_][0:B, :],
                                 sct_sb[:, 2 * k_:2 * k_ + 2],
                                 wm_sb[b_][:, 512 * n_:512 * (n_ + 1)],
                                 start=(k_ == 0), stop=(k_ == NKC - 1)))(k, n, buf),
                        deps=deps)
                wm_tok[buf] = mm_last[5]
            mod_evacs = []
            for n in range(6):
                mod_evacs.append(S.op("v", (lambda h_, n_: lambda e:
                    e.tensor_copy(
                        out=mod_sb[:, 3072 * h_ + 512 * n_:
                                   3072 * h_ + 512 * (n_ + 1)],
                        in_=pb[n_][0:B, :]))(half, n), deps=[mm_last[n]]))
        names = ["sa", "ca", "ga", "sb2", "cb", "gb"]
        rep_toks = {}
        last = None
        for i, nm in enumerate(names):
            for n in range(2):
                mt = S.op("t", (lambda i_, n_: lambda e: e.matmul(
                    pb[6][:, 0:512], self.selF_sb,
                    mod_sb[:, D * i_ + 512 * n_:D * i_ + 512 * (n_ + 1)],
                    start=True, stop=True))(i, n),
                    deps=mod_evacs + [self.pt["sel"]] + ([last] if last else []))
                last = S.op("v", (lambda nm_, n_: lambda e: e.tensor_copy(
                    out=self.rep[nm_][:, 512 * n_:512 * (n_ + 1)],
                    in_=pb[6][:, 0:512]))(nm, n), deps=[mt])
            rep_toks[nm] = last
        self.t_ca1 = S.op("v", lambda e: e.tensor_scalar_add(
            out=self.rep["ca"], in0=self.rep["ca"], scalar1=1.0),
            deps=[rep_toks["ca"]])
        self.t_cb1 = S.op("v", lambda e: e.tensor_scalar_add(
            out=self.rep["cb"], in0=self.rep["cb"], scalar1=1.0),
            deps=[rep_toks["cb"]])
        sgn = arF.f32(128, D)
        gt = {}
        prev_g = None
        for nm in ("ga", "gb"):
            t1 = S.op("v", (lambda nm_: lambda e: e.tensor_scalar(
                out=sgn, in0=self.rep[nm_], scalar1=0.0, scalar2=2.0,
                op0=ALU.is_ge, op1=ALU.mult))(nm),
                deps=[rep_toks[nm]] + ([prev_g] if prev_g else []))
            t2 = S.op("v", lambda e: e.tensor_scalar_add(
                out=sgn, in0=sgn, scalar1=-1.0), deps=[t1])
            t3 = S.op("v", (lambda nm_: lambda e: e.tensor_mul(
                out=self.rep[nm_], in0=self.rep[nm_], in1=sgn))(nm),
                deps=[rep_toks[nm], t2])
            t4 = S.op("v", (lambda nm_: lambda e: e.tensor_scalar_max(
                out=self.rep[nm_], in0=self.rep[nm_], scalar1=0.25))(nm),
                deps=[t3])
            gt[nm] = S.op("v", (lambda nm_: lambda e: e.tensor_mul(
                out=self.rep[nm_], in0=self.rep[nm_], in1=sgn))(nm),
                deps=[t4])
            prev_g = gt[nm]
        self.t_ga = gt["ga"]
        self.t_gb = gt["gb"]

    def weights_attn(self):
        S, io = self.S, self.io
        wv = io["wall_dram"]
        self.w_wall = []
        for part in range(4):
            self.w_wall.append(S.gdma((lambda p_: lambda e: e.dma_start(
                out=wv[:, 768 * p_:768 * (p_ + 1), :]
                    .rearrange("c n k -> n c k"),
                in_=io["wqkv"][768 * p_:768 * (p_ + 1), :]))(part)))
        self.w_wall.append(S.gdma(lambda e: e.dma_start(
            out=wv[:, 3072:3088, :].rearrange("c n k -> n c k"),
            in_=io["wg"])))
        self.w_wall.append(S.gdma(lambda e: e.dma_start(
            out=wv[:, 3088:3104, :].rearrange("c n k -> n c k"),
            in_=io["wb"])))
        self.w_wout = []
        for part in range(2):
            self.w_wout.append(S.gdma((lambda p_: lambda e: e.dma_start(
                out=io["wout_dram"][:, 512 * p_:512 * (p_ + 1), :]
                    .rearrange("c n k -> n c k"),
                in_=io["wout"][512 * p_:512 * (p_ + 1), :]))(part)))

    def quant_seq(self, src, dst, toks, eng="v"):
        S = self.S
        p, n = src.shape[0], src.shape[1]
        nb = n // 16
        am0, r20, xn0, w10, w20, w30 = self.qscr_by_eng[eng]
        am, r2 = am0[0:p, 0:nb], r20[0:p, 0:nb]
        xn, w1, w2, w3 = (xn0[0:p, 0:n], w10[0:p, 0:n], w20[0:p, 0:n],
                          w30[0:p, 0:n])
        t0 = S.op("v", lambda e: e.tensor_reduce(
            out=am, in_=src.rearrange("p (a b) -> p a b", b=16),
            axis=AX.X, op=ALU.max, apply_absolute_value=True), deps=toks)
        t1 = S.op("v", lambda e: e.tensor_scalar(
            out=am, in0=am, scalar1=1.0 / 12.0, scalar2=0.5e-12,
            op0=ALU.mult, op1=ALU.max), deps=[t0])
        t2 = S.op("v", lambda e: e.reciprocal(out=r2, in_=am), deps=[t1])
        t3 = S.op(eng, lambda e: e.tensor_tensor(
            out=xn.rearrange("p (a b) -> p a b", b=16),
            in0=src.rearrange("p (a b) -> p a b", b=16),
            in1=r2.unsqueeze(2).broadcast_to([p, nb, 16]), op=ALU.mult),
            deps=[t2] + list(toks))
        t4 = S.op(eng, lambda e: e.tensor_scalar(
            out=w1, in0=xn, scalar1=MAGIC, scalar2=MAGIC,
            op0=ALU.add, op1=ALU.subtract), deps=[t3])
        t5 = S.op(eng, lambda e: e.tensor_scalar(
            out=w1, in0=w1, scalar1=4.0, scalar2=-4.0,
            op0=ALU.min, op1=ALU.max), deps=[t4])
        t6 = S.op(eng, lambda e: e.tensor_scalar(
            out=w2, in0=xn, scalar1=10.0, scalar2=2.0,
            op0=ALU.is_gt, op1=ALU.mult), deps=[t3])
        t7 = S.op(eng, lambda e: e.scalar_tensor_tensor(
            out=w2, in0=xn, scalar=7.0, in1=w2, op0=ALU.is_gt, op1=ALU.add),
            deps=[t6])
        t8 = S.op(eng, lambda e: e.scalar_tensor_tensor(
            out=w2, in0=xn, scalar=5.0, in1=w2, op0=ALU.is_gt, op1=ALU.add),
            deps=[t7])
        t9 = S.op(eng, lambda e: e.tensor_scalar(
            out=w3, in0=xn, scalar1=-10.0, scalar2=2.0,
            op0=ALU.is_lt, op1=ALU.mult), deps=[t3])
        t10 = S.op(eng, lambda e: e.scalar_tensor_tensor(
            out=w3, in0=xn, scalar=-7.0, in1=w3, op0=ALU.is_lt, op1=ALU.add),
            deps=[t9])
        t11 = S.op(eng, lambda e: e.scalar_tensor_tensor(
            out=w3, in0=xn, scalar=-5.0, in1=w3, op0=ALU.is_lt, op1=ALU.add),
            deps=[t10])
        tq = S.op(eng, lambda e: e.tensor_tensor(out=w2, in0=w2, in1=w3,
                  op=ALU.subtract), deps=[t8, t11])
        tq2 = S.op(eng, lambda e: e.scalar_tensor_tensor(
            out=w2, in0=w2, scalar=2.0, in1=w1, op0=ALU.mult, op1=ALU.add),
            deps=[tq, t5])
        tf = S.op(eng, lambda e: e.tensor_tensor(
            out=dst.rearrange("p (a b) -> p a b", b=16),
            in0=w2.rearrange("p (a b) -> p a b", b=16),
            in1=am.unsqueeze(2).broadcast_to([p, nb, 16]), op=ALU.mult),
            deps=[tq2, t1])
        self._q_src = [t0, t3]
        return tf

    def ffn_weight_quant(self):
        S, io, pb = self.S, self.io, self.pb
        arB = self.arB
        self.set_qscr(IF // NCORE)
        gu_out = arB.bf16(128, 2 * IF)
        src_t = self.arF.f32(128, IF // NCORE)
        dq_t = arB.bf16(128, IF // NCORE)
        wdT_a = arB.bf16(128, D)
        wdT_b = arB.bf16(128, D)
        wdT_c = arB.bf16(96, D)
        ps = pb[7][:, 256:320].bitcast(BF16)
        toks1 = []
        prev_q = None
        ntile = IF // 128
        for which, wsrc in (("g", io["wgate_sh"]), ("u", io["wup_sh"])):
            for r in range(ntile):
                lt = S.dma("s", (lambda w_, r_: lambda e: e.dma_start(
                    out=src_t[:, 0:128],
                    in_=w_[128 * r_:128 * (r_ + 1), :]))(wsrc, r),
                    deps=[prev_q] if prev_q else [])
                qt = self.quant_seq(src_t[:, 0:128], dq_t[:, 0:128], [lt])
                off = (0 if which == "g" else IF) + 128 * r
                tt = S.op("t", (lambda: lambda e: e.transpose(
                    ps, dq_t[:, 0:128], self.ident_bf))(),
                    deps=[qt, self.t_idb, self.t_pbz[7]] +
                         ([prev_q] if prev_q else []))
                prev_q = S.op("v", (lambda off_: lambda e: e.tensor_copy(
                    out=gu_out[:, off_:off_ + 128], in_=ps))(off), deps=[tt])
                toks1.append(prev_q)
        st1 = S.dma("s", lambda e: e.dma_start(out=io["agw_in"], in_=gu_out),
                    deps=toks1)
        self.ag1 = S.cop(lambda e: e.collective_compute(
            "AllGather", ALU.bypass, replica_groups=[list(range(NCORE))],
            ins=[io["agw_in"].opt()], outs=[io["agw_out"].opt()]), deps=[st1])

        toks2 = []
        prev_q = None
        for r in range(NKC):
            lt = S.dma("s", (lambda r_: lambda e: e.dma_start(
                out=src_t, in_=io["wdown_sh"][128 * r_:128 * (r_ + 1), :]))(r),
                deps=[prev_q] if prev_q else [])
            qt = self.quant_seq(src_t, dq_t, [lt])
            for dst_t, c0, cw in ((wdT_a, 0, 128), (wdT_b, 128, 128),
                                  (wdT_c, 256, 96)):
                tt = S.op("t", (lambda c0_, cw_: lambda e: e.transpose(
                    ps[0:cw_, :], dq_t[:, c0_:c0_ + cw_],
                    self.ident_bf))(c0, cw),
                    deps=[qt, self.t_idb] + ([prev_q] if prev_q else []))
                prev_q = S.op("v", (lambda d_, r_, cw_: lambda e: e.tensor_copy(
                    out=d_[0:cw_, 128 * r_:128 * (r_ + 1)],
                    in_=ps[0:cw_, 0:128]))(dst_t, r, cw), deps=[tt])
                toks2.append(prev_q)
        st2 = [
            S.dma("s", lambda e: e.dma_start(out=io["agd_in"][0:128, :],
                  in_=wdT_a), deps=toks2),
            S.dma("s", lambda e: e.dma_start(out=io["agd_in"][128:256, :],
                  in_=wdT_b), deps=toks2),
            S.dma("s", lambda e: e.dma_start(out=io["agd_in"][256:352, :],
                  in_=wdT_c), deps=toks2),
        ]
        self.ag2 = S.cop(lambda e: e.collective_compute(
            "AllGather", ALU.bypass, replica_groups=[list(range(NCORE))],
            ins=[io["agd_in"].opt()], outs=[io["agd_out"].opt()]),
            deps=st2 + [self.ag1])

    def ln_mod(self, x_t, ca1, sa, out_t, deps):
        S = self.S
        m, rs, t1, t2, xs = (self.ln_m, self.ln_r, self.ln_t1, self.ln_t2,
                             self.ln_xs)
        ta = S.op("a", lambda e: e.activation(out=xs, in_=x_t, func=AF.Square),
                  deps=deps)
        t_m = S.op("v", lambda e: e.reduce_sum(out=m, in_=x_t, axis=AX.X),
                   deps=deps)
        t_q = S.op("v", lambda e: e.reduce_sum(out=t1, in_=xs, axis=AX.X),
                   deps=[ta])
        t_m2 = S.op("v", lambda e: e.tensor_scalar_mul(out=m, in0=m,
                    scalar1=1.0 / D), deps=[t_m])
        t_q2 = S.op("v", lambda e: e.tensor_scalar_mul(out=t1, in0=t1,
                    scalar1=1.0 / D), deps=[t_q])
        t_v = S.op("v", lambda e: e.tensor_mul(out=t2, in0=m, in1=m),
                   deps=[t_m2])
        t_v2 = S.op("v", lambda e: e.tensor_tensor(out=t1, in0=t1, in1=t2,
                    op=ALU.subtract), deps=[t_v, t_q2])
        t_v3 = S.op("v", lambda e: e.tensor_scalar_add(out=t1, in0=t1,
                    scalar1=1e-5), deps=[t_v2])
        t_sq = S.op("a", lambda e: e.activation(out=t2, in_=t1, func=AF.Sqrt),
                    deps=[t_v3])
        t_r = S.op("v", lambda e: e.reciprocal(out=rs, in_=t2), deps=[t_sq])
        t_n = S.op("v", lambda e: e.tensor_scalar(
            out=xs, in0=x_t, scalar1=m, scalar2=rs,
            op0=ALU.subtract, op1=ALU.mult), deps=[t_r, t_m2] + list(deps))
        t_h1 = S.op("v", lambda e: e.tensor_tensor(out=xs, in0=xs, in1=ca1,
                    op=ALU.mult), deps=[t_n, self.t_ca1, self.t_cb1])
        t_h2 = S.op("v", lambda e: e.tensor_tensor(out=out_t, in0=xs, in1=sa,
                    op=ALU.add), deps=[t_h1])
        return t_h2

    def hphase(self):
        S, io = self.S, self.io
        self.h_toks = []
        prev_store = None
        for i in range(NT):
            lx = S.dma("s", (lambda i_: lambda e: e.dma_start(
                out=self.x_t, in_=io["x_in"][128 * i_:128 * (i_ + 1), :]))(i),
                deps=[prev_store] if prev_store else [])
            th = self.ln_mod(self.x_t, self.rep["ca"], self.rep["sa"],
                             self.h_bf, [lx])
            prev_store = S.dma("s", (lambda i_: lambda e: e.dma_start(
                out=io["h_dram"][:, 128 * i_:128 * (i_ + 1), :]
                    .rearrange("c p k -> p c k"),
                in_=self.h_bf))(i), deps=[th])
            self.h_toks.append(prev_store)

    def qkvphase(self):
        S, io, pb = self.S, self.io, self.pb
        arB, arF = self.arB, self.arF
        wallT = [arB.bf16(128, 3104) for _ in range(NKC)]
        hT = [[arB.bf16(128, 512) for _ in range(NKC)] for _ in range(2)]
        var_bf = arB.bf16(128, D)
        q_raw = arF.f32(128, D)
        k_raw = arF.f32(128, D)
        sqs = arF.f32(128, D)
        ssq = arF.f32(128, H)
        ssk = arF.f32(128, H)
        zab = arF.f32(128, 2 * H)
        sab = arF.f32(128, 2 * H)
        la_t = arF.f32(128, H)
        laT = arF.f32(H, C)
        sc16 = arF.f32(128, H)
        wl = []
        for k in range(NKC):
            wl.append(S.dma("s", (lambda k_, d_: lambda e: e.dma_start(
                out=d_, in_=io["wall_dram"][k_], transpose=True))(k, wallT[k]),
                deps=self.w_wall))
        hl_tok = [[None] * NKC, [None] * NKC]
        store_toks = []
        evac_toks = []
        prev_lt = None
        prev_lt2 = None
        for i in range(NT):
            qtr, buf = i // 4, (i // 4) % 2
            if i % 4 == 0:
                for k in range(NKC):
                    hl_tok[buf][k] = S.dma("s", (lambda k_, d_, q_: lambda e:
                        e.dma_start(out=d_,
                                    in_=io["h_dram"][k_][512 * q_:512 * (q_ + 1), :],
                                    transpose=True))(k, hT[buf][k], qtr),
                        deps=[self.h_toks[4 * qtr + 3]] + store_toks +
                             ([hl_tok[buf][k]] if hl_tok[buf][k] else []))
            io_i = 128 * (i % 4)
            mm_ab = None
            mm_n = [None] * 6
            for k in range(NKC):
                for n in range(6):
                    deps = [wl[k], hl_tok[buf][k]]
                    if k == 0:
                        deps += evac_toks
                    if mm_n[n]:
                        deps.append(mm_n[n])
                    mm_n[n] = S.op("t", (lambda k_, n_, b2, o_: lambda e:
                        e.matmul(pb[n_], hT[b2][k_][:, o_:o_ + 128],
                                 wallT[k_][:, 512 * n_:512 * (n_ + 1)],
                                 start=(k_ == 0), stop=(k_ == NKC - 1)))(k, n, buf, io_i),
                        deps=deps)
                mm_ab = S.op("t", (lambda k_, b2, o_: lambda e: e.matmul(
                    pb[6][:, 0:32], hT[b2][k_][:, o_:o_ + 128],
                    wallT[k_][:, 3072:3104],
                    start=(k_ == 0), stop=(k_ == NKC - 1)))(k, buf, io_i),
                    deps=[wl[k], hl_tok[buf][k]] +
                         ([mm_ab] if mm_ab else evac_toks))
            tq0 = S.op("a", lambda e: e.copy(out=q_raw[:, 0:512],
                       in_=pb[0]), deps=[mm_n[0]])
            tq1 = S.op("a", lambda e: e.copy(out=q_raw[:, 512:1024],
                       in_=pb[1]), deps=[mm_n[1]])
            tk0 = S.op("a", lambda e: e.copy(out=k_raw[:, 0:512],
                       in_=pb[2]), deps=[mm_n[2]])
            tk1 = S.op("a", lambda e: e.copy(out=k_raw[:, 512:1024],
                       in_=pb[3]), deps=[mm_n[3]])
            tz = S.op("v", lambda e: e.tensor_tensor(
                out=zab, in0=self.bgbb_rep, in1=pb[6][:, 0:32], op=ALU.add),
                deps=[mm_ab, self.pt["bgr"], self.pt["bbr"]])
            ts_ = S.op("a", lambda e: e.activation(out=sab, in_=zab,
                       func=AF.Sigmoid), deps=[tz])
            tb = S.op("v", (lambda i_: lambda e: e.tensor_copy(
                out=self.b_sig[i_], in_=sab[:, H:2 * H]))(i), deps=[ts_])
            tcl = S.op("v", lambda e: e.tensor_scalar_max(
                out=la_t, in0=sab[:, 0:H], scalar1=1e-38), deps=[ts_])
            tla = S.op("a", lambda e: e.activation(out=la_t, in_=la_t,
                       func=AF.Ln), deps=[tcl])
            ttr = S.op("t", (lambda: lambda e: e.transpose(
                pb[7][0:H, 0:128], la_t, self.ident_sb))(),
                deps=[tla, self.pt["id"]] + ([prev_lt] if prev_lt else [])
                     + ([prev_lt2] if prev_lt2 else []))
            tev = S.op("v", lambda e: e.tensor_copy(out=laT,
                       in_=pb[7][0:H, 0:128]), deps=[ttr])
            tsc_ = S.op("v", (lambda i_: lambda e: e.tensor_tensor_scan(
                out=self.L_all[:, C * i_:C * (i_ + 1)], data0=laT,
                data1=self.zeroHC, initial=0.0, op0=ALU.add, op1=ALU.add))(i),
                deps=[tev, self.t_zero])
            tlt = S.op("t", (lambda i_: lambda e: e.transpose(
                pb[7][:, 128:128 + H], self.L_all[:, C * i_:C * (i_ + 1)],
                self.ident_sb[0:H, 0:H]))(i),
                deps=[tsc_] + ([prev_lt2] if prev_lt2 else []))
            tltv = S.op("v", (lambda i_: lambda e: e.tensor_copy(
                out=self.LT[i_], in_=pb[7][:, 128:128 + H]))(i), deps=[tlt])
            texp = S.op("a", (lambda i_: lambda e: e.activation(
                out=self.expL[i_], in_=self.LT[i_], func=AF.Exp))(i),
                deps=[tltv])
            prev_lt = tev
            prev_lt2 = tltv
            tsq = S.op("a", lambda e: e.activation(out=sqs, in_=q_raw,
                       func=AF.Square), deps=[tq0, tq1])
            trq = S.op("v", lambda e: e.tensor_reduce(
                out=ssq, in_=sqs.rearrange("p (h d) -> p h d", d=DH),
                axis=AX.X, op=ALU.add), deps=[tsq])
            tsk = S.op("a", lambda e: e.activation(out=sqs, in_=k_raw,
                       func=AF.Square), deps=[tk0, tk1, trq])
            trk = S.op("v", lambda e: e.tensor_reduce(
                out=ssk, in_=sqs.rearrange("p (h d) -> p h d", d=DH),
                axis=AX.X, op=ALU.add), deps=[tsk])
            tq2 = S.op("a", lambda e: e.activation(out=ssq, in_=ssq,
                       func=AF.Sqrt), deps=[trq])
            tq3 = S.op("v", lambda e: e.tensor_scalar_max(out=ssq, in0=ssq,
                       scalar1=1e-12), deps=[tq2])
            tq4 = S.op("v", (lambda i_: lambda e: e.reciprocal(
                out=self.rnq_t[i_], in_=ssq))(i), deps=[tq3])
            tk2 = S.op("a", lambda e: e.activation(out=ssk, in_=ssk,
                       func=AF.Sqrt), deps=[trk])
            tk3 = S.op("v", lambda e: e.tensor_scalar_max(out=ssk, in0=ssk,
                       scalar1=1e-12), deps=[tk2])
            tk4 = S.op("v", (lambda i_: lambda e: e.reciprocal(
                out=self.rnk_t[i_], in_=ssk))(i), deps=[tk3])

            def bc16(ap):
                return ap.unsqueeze(2).broadcast_to([128, H, DH])

            def vstore(dst, tok, i_=i):
                return S.dma("s", (lambda d_: lambda e: e.dma_start(
                    out=d_[:, 128 * i_:128 * (i_ + 1), :]
                        .rearrange("c p k -> p c k"),
                    in_=var_bf))(dst), deps=[tok])

            tv1 = S.op("v", (lambda i_: lambda e: e.tensor_tensor(
                out=var_bf.rearrange("p (h d) -> p h d", d=DH),
                in0=k_raw.rearrange("p (h d) -> p h d", d=DH),
                in1=bc16(self.rnk_t[i_]), op=ALU.mult))(i),
                deps=[tk4] + store_toks[4:5])
            s1 = vstore(io["kpl_dram"], tv1)
            tv2 = S.op("v", (lambda i_: lambda e: e.tensor_tensor(
                out=var_bf.rearrange("p (h d) -> p h d", d=DH),
                in0=var_bf.rearrange("p (h d) -> p h d", d=DH),
                in1=bc16(self.b_sig[i_]), op=ALU.mult))(i), deps=[tv1, tb, s1])
            s2 = vstore(io["kb_dram"], tv2)
            tv3 = S.op("v", (lambda i_: lambda e: e.tensor_tensor(
                out=var_bf.rearrange("p (h d) -> p h d", d=DH),
                in0=q_raw.rearrange("p (h d) -> p h d", d=DH),
                in1=bc16(self.rnq_t[i_]), op=ALU.mult))(i), deps=[tq4, s2])
            s3 = vstore(io["q_dram"], tv3)
            tg = S.op("v", (lambda i_: lambda e: e.tensor_tensor(
                out=sc16, in0=self.rnq_t[i_], in1=self.expL[i_],
                op=ALU.mult))(i), deps=[tq4, texp])
            tv4 = S.op("v", (lambda i_: lambda e: e.tensor_tensor(
                out=var_bf.rearrange("p (h d) -> p h d", d=DH),
                in0=q_raw.rearrange("p (h d) -> p h d", d=DH),
                in1=bc16(sc16), op=ALU.mult))(i), deps=[tg, s3])
            s4 = vstore(io["qg_dram"], tv4)
            tv5a = S.op("v", (lambda i_: lambda e: e.tensor_tensor(
                out=var_bf[:, 0:512].rearrange("p (h d) -> p h d", d=DH),
                in0=pb[4].rearrange("p (h d) -> p h d", d=DH),
                in1=self.b_sig[i_][:, 0:8].unsqueeze(2)
                    .broadcast_to([128, 8, DH]), op=ALU.mult))(i),
                deps=[mm_n[4], tb, s4])
            tv5b = S.op("v", (lambda i_: lambda e: e.tensor_tensor(
                out=var_bf[:, 512:1024].rearrange("p (h d) -> p h d", d=DH),
                in0=pb[5].rearrange("p (h d) -> p h d", d=DH),
                in1=self.b_sig[i_][:, 8:16].unsqueeze(2)
                    .broadcast_to([128, 8, DH]), op=ALU.mult))(i),
                deps=[mm_n[5], tb, tv5a])
            s5 = vstore(io["bv_dram"], tv5b)
            store_toks = [s1, s2, s3, s4, s5, tq0, tq1, tk0, tk1, tz,
                          tv5a, tv5b]
            evac_toks = [tq0, tq1, tk0, tk1, tz, tv5a, tv5b]
        self.qkv_last = list(store_toks)

        lst = []
        vtok = S.etok("v")
        for i in range(NT):
            lst.append(S.dma("s", (lambda i_: lambda e: e.dma_start(
                out=io["l_dram"][i_:i_ + 1, :]
                    .rearrange("x (h c) -> (x h) c", h=H),
                in_=self.L_all[:, C * i_:C * (i_ + 1)]))(i), deps=[vtok]))
        self.l_store = lst
        t1 = S.op("v", lambda e: e.tensor_copy(
            out=self.lcc, in_=self.L_all[:, C - 1::C]), deps=[vtok])
        t2 = S.op("t", lambda e: e.transpose(pb[7][0:NT, 256:256 + H],
                  self.lcc, self.ident_sb[0:H, 0:H]), deps=[t1, prev_lt2])
        t3 = S.op("v", lambda e: e.tensor_copy(out=self.lcct,
                  in_=pb[7][0:NT, 256:256 + H]), deps=[t2])
        t4 = S.dma("s", lambda e: e.dma_start(
            out=io["lc_dram"].rearrange("x (c h) -> (x c) h", h=H),
            in_=self.lcct), deps=[t3])
        t5 = S.dma("s", lambda e: e.dma_start(
            out=self.gcrep, in_=io["lc_dram"].partition_broadcast(128)),
            deps=[t4])
        self.t_gc = S.op("a", lambda e: e.activation(
            out=self.gcrep, in_=self.gcrep, func=AF.Exp), deps=[t5])
        gv = self.gcrep.rearrange("r (c p two) -> r c p two", p=8, two=2)
        gpv = self.gcp.rearrange("r (c p) -> r c p", p=8)
        tg1 = S.op("v", lambda e: e.tensor_copy(
            out=gpv[0:64], in_=gv[0:64, :, :, 0]), deps=[self.t_gc])
        self.t_gcp = S.op("v", lambda e: e.tensor_copy(
            out=gpv[64:128], in_=gv[64:128, :, :, 1]), deps=[tg1])

    def chunkphase(self):
        S, io, pb = self.S, self.io, self.pb
        arB, arF = self.arB, self.arF
        TQW = 256
        kT = [arB.bf16(128, TQW) for _ in range(NKC)]
        kbT = [arB.bf16(128, TQW) for _ in range(NKC)]
        qT = [arB.bf16(128, TQW) for _ in range(NKC)]
        qgT = [arB.bf16(128, TQW) for _ in range(NKC)]
        lrep = [arF.f32(128, H * C) for _ in range(2)]
        ein = arF.f32(128, H * C)
        e_ts = arB.bf16(128, H * C)
        e_sts = arB.bf16(128, H * C)
        t_full = arB.bf16(128, H * C)
        that_full = arB.bf16(128, H * C)
        rz_all = arB.bf16(128, H * 192)
        w_all = arB.bf16(128, H * 128)
        kpl_tm = arB.bf16(128, D)
        pw_ts = [arB.bf16(128, 1024) for _ in range(2)]
        pw_st = [arB.bf16(128, 1024) for _ in range(4)]
        g_all = arB.bf16(128, 1024)
        that_bd = arB.bf16(128, 1024)
        dg_scr = arB.bf16(128, 4 * C)
        z_all = rz_all
        qpt_sb = arB.bf16(128, 4 * C)
        tmp_odd = arB.bf16(128, 4 * C)
        bc_sb = arB.bf16(128, 4 * DH)
        bct_sb = arB.bf16(128, 4 * DH)
        mcs = arB.bf16(128, 4 * DH)
        self.chunk_tiles = dict(rz=rz_all, w=w_all, z=z_all, qpt=qpt_sb,
                                tmp=tmp_odd, bc=bc_sb, bct=bct_sb, mcs=mcs)

        rzv = rz_all.rearrange("p (h x) -> p h x", x=192)
        wav = w_all.rearrange("p (h x) -> p h x", x=128)

        zt = [self.t_i64b]
        for p in range(H // 2):
            zt.append(S.op("v", (lambda p_: lambda e: e.memset(
                self.stloc[p_], 0.0))(p)))
            zt.append(S.op("v", (lambda p_: lambda e: e.tensor_copy(
                out=self.prun[p_], in_=self.i64two_bf))(p), deps=[self.t_i64b]))
            zt.append(S.op("v", (lambda p_: lambda e: e.tensor_copy(
                out=self.prunT[p_], in_=self.i64two_bf))(p), deps=[self.t_i64b]))
        state_tok = zt

        ld_toks = []
        lr_tok = [None, None]
        prev_chunk_done = []
        prev_prods = []

        def emit_E(c, pre):
            pp = c % 2
            tl = S.dma("s", (lambda c_, p_: lambda e: e.dma_start(
                out=lrep[p_], in_=io["l_dram"][c_:c_ + 1, :]
                .partition_broadcast(128)))(c, pp),
                deps=[self.l_store[c]] + ([lr_tok[pp]] if lr_tok[pp] else []))
            emask = [(self.mask_ts_sb, e_ts, -1.0),
                     (self.mask_sts_sb, e_sts, 1.0)]
            ex_toks = []
            prev_ex = list(pre)
            last_ein = None
            for mask, edst, scl in emask:
                ein_toks = []
                for h in range(H):
                    ein_toks.append(S.op("v", (lambda c_, h_, p_, m_: lambda e:
                        e.scalar_tensor_tensor(
                            out=ein[:, C * h_:C * (h_ + 1)],
                            in0=lrep[p_][:, C * h_:C * (h_ + 1)],
                            scalar=self.LT[c_][:, h_:h_ + 1],
                            in1=m_, op0=ALU.subtract, op1=ALU.add))(c, h, pp, mask),
                        deps=[tl, self.pt["mts"], self.pt["msts"],
                              self.pt["msti"]] + prev_ex))
                ex = S.op("a", (lambda ed_, sc_: lambda e: e.activation(
                    out=ed_, in_=ein, func=AF.Exp, scale=sc_))(edst, scl),
                    deps=ein_toks)
                prev_ex = [ex]
                ex_toks.append(ex)
                last_ein = ein_toks[-1]
            lr_tok[pp] = last_ein
            return ex_toks

        ex_cache = {0: emit_E(0, [])}
        for c in range(NT):
            grp, cg = c // 2, c % 2
            if cg == 0:
                ld_toks = []
                for kk in range(NKC):
                    for qi, (dst, src) in enumerate(
                            ((kT, "kpl_dram"), (kbT, "kb_dram"),
                             (qT, "q_dram"), (qgT, "qg_dram"))):
                        ld_toks.append(S.dma("s" if (kk + qi) % 2 else "a",
                            (lambda kk_, d_, s_, g_:
                            lambda e: e.dma_start(
                                out=d_,
                                in_=io[s_][kk_][TQW * g_:TQW * (g_ + 1), :],
                                transpose=True))(kk, dst[kk], src, grp),
                            deps=self.qkv_last + prev_chunk_done))
            lkp = S.dma("s", (lambda c_: lambda e: e.dma_start(
                out=kpl_tm, in_=io["kpl_dram"][:, 128 * c_:128 * (c_ + 1), :]
                .rearrange("k p x -> p k x")))(c),
                deps=self.qkv_last + prev_chunk_done)
            wav4 = w_all.rearrange("p (hp two x) -> p hp two x", two=2,
                                   x=128)
            lbv = []
            for two in range(2):
                lbv.append(S.dma("s", (lambda c_, t2: lambda e: e.dma_start(
                    out=wav4[:, :, t2, 0:64],
                    in_=io["bv_dram"][:, 128 * c_:128 * (c_ + 1),
                                      64 * t2:64 * (t2 + 1)]
                    .rearrange("k p d -> p k d")))(c, two),
                    deps=self.qkv_last + prev_chunk_done))
            ex1, ex2 = ex_cache.pop(c)

            tfd = S.op("a", (lambda c_: lambda e: e.activation(
                out=self.fdt, in_=self.L_all[:, C * c_:C * (c_ + 1)],
                func=AF.Exp, scale=-1.0,
                bias=self.L_all[:, C * (c_ + 1) - 1:C * (c_ + 1)]))(c),
                deps=[self.l_store[c]] + prev_chunk_done)
            ttd = S.op("t", lambda e: e.transpose(
                pb[7][:, 256:256 + H], self.fdt, self.ident_sb[0:H, 0:H]),
                deps=[tfd] + prev_chunk_done)
            tfdv = S.op("v", lambda e: e.tensor_copy(out=self.sc16c,
                        in_=pb[7][:, 256:256 + H]), deps=[ttd])
            tkd = S.op("v", lambda e: e.tensor_tensor(
                out=rzv[:, :, 128:192],
                in0=kpl_tm.rearrange("p (h d) -> p h d", d=DH),
                in1=self.sc16c.unsqueeze(2).broadcast_to([128, H, DH]),
                op=ALU.mult), deps=[tfdv, lkp] + prev_chunk_done)
            tg2 = S.op("v", (lambda c_: lambda e: e.tensor_tensor(
                out=self.sc16c, in0=self.b_sig[c_], in1=self.expL[c_],
                op=ALU.mult))(c), deps=[tkd])
            tkbg = S.op("v", lambda e: e.tensor_tensor(
                out=wav[:, :, 64:128],
                in0=kpl_tm.rearrange("p (h d) -> p h d", d=DH),
                in1=self.sc16c.unsqueeze(2).broadcast_to([128, H, DH]),
                op=ALU.mult), deps=[tg2, lkp] + prev_chunk_done)

            prod_toks = []
            prev_fix = []
            ev4 = lambda ap: ap.rearrange("p (g two c) -> p g two c",
                                          two=2, c=C)
            rz4 = rz_all.rearrange("p (g two x) -> p g two x", two=2, x=192)
            pbv = [bk.rearrange("p (k one c) -> p k one c", one=1, c=C)
                   for bk in pb]
            for wv in range(2):
                mmt_bank = {k: [] for k in range(6)}
                for hh in range(8):
                    h = 8 * wv + hh
                    r0 = 64 * (h % 2)
                    csl = slice(128 * cg, 128 * (cg + 1))
                    kt_sl = kT[h // 2][r0:r0 + 64, csl]
                    kbt_sl = kbT[h // 2][r0:r0 + 64, csl]
                    qt_sl = qT[h // 2][r0:r0 + 64, csl]
                    col = 128 * (hh // 2)
                    bA = pb[hh % 2]
                    bB = pb[2 + hh % 2]
                    bC = pb[4 + hh % 2]
                    dps = ld_toks + (list(prod_toks) if prod_toks else
                                     list(prev_chunk_done))
                    mmt_bank[hh % 2].append(S.op("t", (lambda a, b_, o, co:
                        lambda e: e.matmul(o[:, co:co + 128], a, b_,
                        start=True, stop=True))(kt_sl, kbt_sl, bA, col),
                        deps=dps))
                    mmt_bank[2 + hh % 2].append(S.op("t", (lambda a, b_, o, co:
                        lambda e: e.matmul(o[:, co:co + 128], a, b_,
                        start=True, stop=True))(kbt_sl, kt_sl, bB, col),
                        deps=dps))
                    mmt_bank[4 + hh % 2].append(S.op("t", (lambda a, b_, o, co:
                        lambda e: e.matmul(o[:, co:co + 128], a, b_,
                        start=True, stop=True))(kt_sl, qt_sl, bC, col),
                        deps=dps))
                mmt = [t for v in mmt_bank.values() for t in v]
                gsl = slice(4 * wv, 4 * wv + 4)
                for par in range(2):
                    psl = slice(par, par + 1)
                    pt1 = S.op("v", (lambda g_, p_, pa: lambda e:
                        e.tensor_tensor(
                            out=ev4(that_full)[:, g_, p_, :],
                            in0=ev4(e_sts)[:, g_, p_, :],
                            in1=pbv[pa][:, :, :, :],
                            op=ALU.mult))(gsl, psl, par),
                        deps=mmt_bank[par] + [ex2])
                    pt2 = S.op("v", (lambda g_, p_, pa: lambda e:
                        e.tensor_tensor(
                            out=ev4(t_full)[:, g_, p_, :],
                            in0=ev4(e_ts)[:, g_, p_, :],
                            in1=pbv[2 + pa][:, :, :, :],
                            op=ALU.mult))(gsl, psl, par),
                        deps=mmt_bank[2 + par] + [ex1])
                    pt3 = S.op("v", (lambda g_, p_, pa: lambda e:
                        e.tensor_tensor(
                            out=rz4[:, g_, p_, 0:128],
                            in0=ev4(e_sts)[:, g_, p_, :],
                            in1=pbv[4 + pa][:, :, :, :],
                            op=ALU.mult))(gsl, psl, par),
                        deps=mmt_bank[4 + par] + [ex2])
                    dg4 = dg_scr.rearrange("p (k one c) -> p k one c",
                                           one=1, c=C)
                    dg = S.op("v", (lambda pa: lambda e: e.tensor_tensor(
                        out=dg4, in0=pbv[4 + pa][:, :, :, :],
                        in1=self.ident_bf.unsqueeze(1).unsqueeze(1)
                            .broadcast_to([128, 4, 1, C]),
                        op=ALU.mult))(par),
                        deps=mmt_bank[4 + par] + [self.t_idb] + prev_fix)
                    fx = S.op("v", (lambda g_, p_: lambda e: e.tensor_tensor(
                        out=rz4[:, g_, p_, 0:128],
                        in0=rz4[:, g_, p_, 0:128],
                        in1=dg4, op=ALU.add))(gsl, psl),
                        deps=[dg, pt3])
                    prev_fix = [fx]
                    prod_toks += [pt1, pt2, pt3, fx]
            if CHUNK_PARTS < 2:
                prev_chunk_done = [self.S.op("v", lambda e: e.engine_nop(),
                                   deps=prod_toks + [tkd, tkbg] + lbv)]
                state_tok = prev_chunk_done
                continue
            prev_prods = list(prod_toks)
            if c + 1 < NT:
                ex_cache[c + 1] = emit_E(c + 1, prev_prods)
            self._cur_c = c
            pre_solve = prod_toks + [tkd]
            td = self.dbgout("tdump", (NT, 128, H * C), BF16)
            rzd = self.dbgout("rzdump", (NT, 128, H * 192), BF16)
            if td is not None:
                pre_solve.append(S.dma("s", (lambda c_: lambda e: e.dma_start(
                    out=td[c_], in_=t_full))(c), deps=prod_toks))
                pre_solve.append(S.dma("s", (lambda c_: lambda e: e.dma_start(
                    out=rzd[c_], in_=rz_all))(c), deps=prod_toks + [tkd]))
            z_toks = self.solve_both(t_full, that_full, rz_all, pw_ts, pw_st,
                                     g_all, that_bd, z_all,
                                     pre_solve)
            zd = self.dbgout("zdump", (NT, 128, H * 192), BF16)
            if zd is not None:
                z_toks = z_toks + [S.dma("s", (lambda c_: lambda e:
                    e.dma_start(out=zd[c_], in_=z_all))(c), deps=z_toks)]
            if CHUNK_PARTS < 3:
                prev_chunk_done = [self.S.op("v", lambda e: e.engine_nop(),
                                   deps=z_toks + [tkbg] + lbv)]
                state_tok = prev_chunk_done
                continue
            prev_chunk_done = self.g2_state(
                c, z_all, w_all, qgT, qpt_sb, tmp_odd, bc_sb, bct_sb, mcs,
                z_toks, lbv + [tkbg], state_tok, cg)
            state_tok = prev_chunk_done

        sts = []
        for p in range(H // 2):
            for par in range(2):
                h = 2 * p + par
                sts.append(S.dma("s", (lambda p_, par_, h_: lambda e:
                    e.dma_start(
                        out=io["ags_in"][:, 2 * DH * h_:2 * DH * h_ + DH],
                        in_=self.prun[p_][64 * par_:64 * par_ + 64, :]))(p, par, h),
                    deps=prev_chunk_done))
                sts.append(S.dma("s", (lambda p_, par_, h_: lambda e:
                    e.dma_start(
                        out=io["ags_in"][:, 2 * DH * h_ + DH:2 * DH * (h_ + 1)],
                        in_=self.stloc[p_][64 * par_:64 * par_ + 64, :]))(p, par, h),
                    deps=prev_chunk_done))
        self.ag3 = S.cop(lambda e: e.collective_compute(
            "AllGather", ALU.bypass, replica_groups=[list(range(NCORE))],
            ins=[io["ags_in"].opt()], outs=[io["ags_out"].opt()]),
            deps=sts + [self.ag2])

    def solve_both(self, t_full, that_full, rz_all, pw_ts, pw_st,
                   g_all, that_bd, z_all, pre_toks):
        S, pb = self.S, self.pb
        rzv = rz_all.rearrange("p (h x) -> p h x", x=192)
        zv = z_all.rearrange("p (h x) -> p h x", x=192)
        wave_tok = []
        for wv in range(2):
            hs = [8 * wv + i for i in range(8)]
            tbd = S.op("v", (lambda w_: lambda e: e.tensor_tensor(
                out=that_bd.rearrange("p (g c) -> p g c", c=C),
                in0=that_full.rearrange("p (g c) -> p g c", c=C)
                    [:, 8 * w_:8 * w_ + 8, :],
                in1=self.bdm_bf.unsqueeze(1).broadcast_to([128, 8, C]),
                op=ALU.mult))(wv),
                deps=list(pre_toks) + [self.t_bdm] + wave_tok)
            sq_dep = [list(pre_toks) + wave_tok, list(pre_toks) + wave_tok]
            for lv in range(5):
                mms_ts = [[], []]
                mms_st = [[], []]
                for hh in range(8):
                    h = hs[hh]
                    hf = hh // 4
                    bts = pb[6 + hf]
                    bst = pb[4 + hf]
                    co = 128 * (hh % 4)
                    if lv == 0:
                        for b_ in range(2):
                            r = slice(64 * b_, 64 * b_ + 64)
                            a_ts = t_full[r, C * h + 64 * b_:
                                          C * h + 64 * b_ + 64]
                            a_st = that_full[r, C * h + 64 * b_:
                                             C * h + 64 * b_ + 64]
                            mms_ts[hf].append(S.op("t", (lambda a1, a2, b2, r_,
                                c_: lambda e: e.matmul(
                                    b2[r_, c_:c_ + 64], a1, a2, start=True,
                                    stop=True))(a_st, a_ts, bts, r,
                                                co + 64 * b_),
                                deps=sq_dep[hf]))
                            mms_st[hf].append(S.op("t", (lambda a1, a2, b2, r_,
                                c_: lambda e: e.matmul(
                                    b2[r_, c_:c_ + 64], a1, a2, start=True,
                                    stop=True))(a_ts, a_st, bst, r,
                                                co + 64 * b_),
                                deps=sq_dep[hf]))
                    else:
                        src_ts = pw_ts[(lv - 1) % 2][:, 128 * hh:128 * hh + 128]
                        src_st = pw_st[lv - 1][:, 128 * hh:128 * hh + 128]
                        mms_ts[hf].append(S.op("t", (lambda a1, a2, b2, c_:
                            lambda e: e.matmul(
                                b2[:, c_:c_ + 128], a1, a2, start=True,
                                stop=True))(src_st, src_ts, bts, co),
                            deps=sq_dep[hf]))
                        if lv < 4:
                            mms_st[hf].append(S.op("t", (lambda a1, a2, b2, c_:
                                lambda e: e.matmul(
                                    b2[:, c_:c_ + 128], a1, a2, start=True,
                                    stop=True))(src_ts, src_st, bst, co),
                                deps=sq_dep[hf]))
                for half in range(2):
                    ev = []
                    if lv == 0:
                        ev.append(S.op("v", (lambda hf: lambda e:
                            e.tensor_tensor(
                                out=pw_ts[0][:, 512 * hf:512 * hf + 512]
                                    .rearrange("p (k c) -> p k c", c=C),
                                in0=pb[6 + hf].rearrange(
                                    "p (k c) -> p k c", c=C),
                                in1=self.bdm_bf.unsqueeze(1)
                                    .broadcast_to([128, 4, C]),
                                op=ALU.mult))(half),
                            deps=mms_ts[half] + [self.t_bdm]))
                        ev.append(S.op("v", (lambda hf: lambda e:
                            e.tensor_tensor(
                                out=pw_st[0][:, 512 * hf:512 * hf + 512]
                                    .rearrange("p (k c) -> p k c", c=C),
                                in0=pb[4 + hf].rearrange(
                                    "p (k c) -> p k c", c=C),
                                in1=self.bdm_bf.unsqueeze(1)
                                    .broadcast_to([128, 4, C]),
                                op=ALU.mult))(half),
                            deps=mms_st[half] + [self.t_bdm]))
                    else:
                        ev.append(S.op("a", (lambda hf, lv_: lambda e: e.copy(
                            out=pw_ts[lv_ % 2][:, 512 * hf:512 * hf + 512],
                            in_=pb[6 + hf]))(half, lv), deps=mms_ts[half]))
                        if lv < 4:
                            ev.append(S.op("v", (lambda hf, lv_: lambda e:
                                e.tensor_copy(
                                    out=pw_st[lv_][:, 512 * hf:512 * hf + 512],
                                    in_=pb[4 + hf]))(half, lv),
                                deps=mms_st[half]))
                    sq_dep[half] = ev
            g0h = []
            for half in range(2):
                g0h.append(S.op("v", (lambda hf: lambda e: e.tensor_tensor(
                    out=g_all[:, 512 * hf:512 * hf + 512]
                        .rearrange("p (g c) -> p g c", c=C),
                    in0=pw_ts[0][:, 512 * hf:512 * hf + 512]
                        .rearrange("p (g c) -> p g c", c=C),
                    in1=self.ident_bf.unsqueeze(1).broadcast_to([128, 4, C]),
                    op=ALU.add))(half), deps=sq_dep[half] + [self.t_idb]))
            gh = [[g0h[0]], [g0h[1]]]
            for lvk in (3, 2, 1, 0):
                for half in range(2):
                    mms = []
                    for hh in range(4 * half, 4 * half + 4):
                        co = 128 * (hh % 4)
                        mms.append(S.op("t", (lambda hh_, co_, lv_: lambda e:
                            e.matmul(pb[6 + hh_ // 4][:, co_:co_ + 128],
                                     pw_st[lv_][:, 128 * hh_:128 * hh_ + 128],
                                     g_all[:, 128 * hh_:128 * hh_ + 128],
                                     start=True, stop=True))(hh, co, lvk),
                            deps=gh[half]))
                    gh[half] = [S.op("v", (lambda hf: lambda e:
                        e.tensor_tensor(
                            out=g_all[:, 512 * hf:512 * hf + 512],
                            in0=g_all[:, 512 * hf:512 * hf + 512],
                            in1=pb[6 + hf], op=ALU.add))(half), deps=mms)]
            for half in range(2):
                mms = []
                for hh in range(4 * half, 4 * half + 4):
                    co = 128 * (hh % 4)
                    mms.append(S.op("t", (lambda hh_, co_: lambda e:
                        e.matmul(pb[6 + hh_ // 4][:, co_:co_ + 128],
                                 that_bd[:, 128 * hh_:128 * hh_ + 128],
                                 g_all[:, 128 * hh_:128 * hh_ + 128],
                                 start=True, stop=True))(hh, co),
                        deps=gh[half] + [tbd]))
                gh[half] = [S.op("v", (lambda hf: lambda e: e.tensor_tensor(
                    out=g_all[:, 512 * hf:512 * hf + 512],
                    in0=g_all[:, 512 * hf:512 * hf + 512],
                    in1=pb[6 + hf], op=ALU.subtract))(half), deps=mms)]
            gd = self.dbgout("gdump", (NT, 2, 128, 1024), BF16)
            if gd is not None:
                gtok = gtok + [S.dma("s", (lambda c_, w_: lambda e:
                    e.dma_start(out=gd[c_][w_], in_=g_all))(self._cur_c, wv),
                    deps=gtok)]
            bot = {}
            for hh in range(8):
                h = hs[hh]
                bank, co = 2 + hh // 2, 256 * (hh % 2)
                bot.setdefault(bank, []).append(S.op("t",
                    (lambda h_, bk, co_, hh_: lambda e: e.matmul(
                        self.pb[bk][64:128, co_:co_ + 192],
                        g_all[64:128, 128 * hh_ + 64:128 * hh_ + 128],
                        rzv[:, h_, :][64:128, :], start=True, stop=True))(
                        h, bank, co, hh),
                    deps=gh[hh // 4] + list(pre_toks)))
            evb = {}
            for bank in range(4):
                evb[2 + bank] = [S.op("a", (lambda bk, w_: lambda e: e.copy(
                    out=zv[:, 2 * bk + 8 * w_:2 * bk + 8 * w_ + 2, :][64:128],
                    in_=self.pb[2 + bk][64:128, :]
                        .rearrange("p (h x) -> p h x", x=256)[:, :, 0:192]))(
                        bank, wv), deps=bot[2 + bank])]
            off = {}
            for hh in range(8):
                h = hs[hh]
                bank, co = 2 + hh // 2, 256 * (hh % 2)
                off.setdefault(bank, []).append(S.op("t",
                    (lambda h_, bk, co_: lambda e: e.matmul(
                        self.pb[bk][0:64, co_:co_ + 192],
                        t_full[64:128, C * h_:C * h_ + 64],
                        zv[:, h_, :][64:128, :], start=True, stop=True))(
                        h, bank, co), deps=evb[bank]))
            evy = {}
            for bank in range(4):
                evy[2 + bank] = [S.op("v", (lambda bk, w_: lambda e:
                    e.tensor_tensor(
                    out=rzv[:, 2 * bk + 8 * w_:2 * bk + 8 * w_ + 2, :][0:64],
                    in0=rzv[:, 2 * bk + 8 * w_:2 * bk + 8 * w_ + 2, :][0:64],
                    in1=self.pb[2 + bk][0:64, :]
                        .rearrange("p (h x) -> p h x", x=256)[:, :, 0:192],
                    op=ALU.subtract))(bank, wv), deps=off[2 + bank])]
            top = {}
            for hh in range(8):
                h = hs[hh]
                bank, co = 2 + hh // 2, 256 * (hh % 2)
                top.setdefault(bank, []).append(S.op("t",
                    (lambda h_, bk, co_, hh_: lambda e: e.matmul(
                        self.pb[bk][0:64, co_:co_ + 192],
                        g_all[0:64, 128 * hh_:128 * hh_ + 64],
                        rzv[:, h_, :][0:64, :], start=True, stop=True))(
                        h, bank, co, hh), deps=evy[bank]))
            evt = []
            for bank in range(4):
                evt.append(S.op("a", (lambda bk, w_: lambda e: e.copy(
                    out=zv[:, 2 * bk + 8 * w_:2 * bk + 8 * w_ + 2, :][0:64],
                    in_=self.pb[2 + bk][0:64, :]
                        .rearrange("p (h x) -> p h x", x=256)[:, :, 0:192]))(
                        bank, wv), deps=top[2 + bank]))
            wave_tok = [S.op("v", lambda e: e.engine_nop(),
                            deps=evt + [t for v in evb.values() for t in v])]
        return wave_tok

    def g2_state(self, c, z_all, w_all, qgT, qpt_sb, tmp_odd, bc_sb, bct_sb,
                 mcs, z_toks, w_toks, state_tok, cg):
        S, pb, io = self.S, self.pb, self.io
        zv = z_all.rearrange("p (h x) -> p h x", x=192)
        wav = w_all.rearrange("p (h x) -> p h x", x=128)
        qptv = qpt_sb.rearrange("p (pr x) -> p pr x", x=C)
        mcsv = mcs.rearrange("p (j x) -> p j x", x=DH)
        bcv = bc_sb.rearrange("p (j x) -> p j x", x=DH)
        tmpv = tmp_odd.rearrange("p (j x) -> p j x", x=C)
        bctv = bct_sb.rearrange("p (j x) -> p j x", x=DH)
        wave_done = list(z_toks) + list(w_toks) + list(state_tok)
        all_done = []
        for W in range(2):
            prs = [4 * W + j for j in range(4)]
            s4 = []
            s1l = []
            for j, p in enumerate(prs):
                bA = pb[j]
                bB = pb[4 + j]
                s0 = []
                for par in range(2):
                    h = 2 * p + par
                    r = slice(64 * par, 64 * par + 64)
                    s0.append(S.op("t", (lambda h_, r_, b_: lambda e:
                        e.matmul(b_[r_, 0:192], wav[:, h_, 0:64],
                                 zv[:, h_, :], start=True, stop=True))(
                                 h, r, bA), deps=wave_done))
                    s0.append(S.op("t", (lambda h_, r_, b_: lambda e:
                        e.matmul(b_[r_, 192:384], wav[:, h_, 64:128],
                                 zv[:, h_, :], start=True, stop=True))(
                                 h, r, bA), deps=wave_done))
                s1 = S.op("v", (lambda p_, j_, b_, cg_: lambda e:
                    e.tensor_tensor(
                        out=qptv[:, j_, :],
                        in0=qgT[p_][:, 128 * cg_:128 * (cg_ + 1)],
                        in1=b_[:, 192:320], op=ALU.subtract))(
                        p, j, bA, cg), deps=s0)
                s1l.append(s1)
                s2 = S.dma("s", (lambda c_, p_, j_: lambda e: e.dma_start(
                    out=io["qpt_dram"][c_ * H + 2 * p_:c_ * H + 2 * p_ + 2, :]
                        .rearrange("two (d t) -> (two d) t", d=DH),
                    in_=qptv[:, j_, :]))(c, p, j), deps=[s1])
                s3 = []
                for par in range(2):
                    r = slice(64 * par, 64 * par + 64)
                    s3.append(S.op("t", (lambda p_, r_, j_, b_: lambda e:
                        e.matmul(b_[r_, 0:128],
                                 self.stloc[p_][r_, :], qptv[:, j_, :][r_, :],
                                 start=True, stop=True))(p, r, j, bB),
                        deps=[s1] + list(state_tok)))
                t1 = S.op("a", (lambda j_, b_: lambda e: e.copy(
                    out=tmpv[:, j_, :], in_=b_[:, 0:128]))(j, bB), deps=s3)
                tot = S.op("v", (lambda p_, j_, b_, c_: lambda e:
                    e.tensor_tensor(
                        out=self.Ot[p_][:, C * c_:C * (c_ + 1)],
                        in0=tmpv[:, j_, :], in1=b_[:, 0:128],
                        op=ALU.add))(p, j, bA, c), deps=[t1] + s0)
                ac = S.op("v", (lambda c_, p_, j_, b_: lambda e:
                    e.scalar_tensor_tensor(
                        out=mcsv[:, j_, :], in0=self.i64two_sb,
                        scalar=self.gcp[:, c_ * 8 + p_:c_ * 8 + p_ + 1],
                        in1=b_[:, 320:384], op0=ALU.mult,
                        op1=ALU.subtract))(c, p, j, bA),
                    deps=s0 + [s1, self.t_gcp])
                bcx = S.op("v", (lambda j_, b_: lambda e: e.tensor_copy(
                    out=bcv[:, j_, :], in_=b_[:, 128:192]))(
                    j, bA), deps=s0)
                spr = S.op("v", (lambda p_, c_: lambda e: e.tensor_copy(
                    out=self.prun_store[:, (c_ * (H // 2) + p_) * DH:
                                        (c_ * (H // 2) + p_ + 1) * DH],
                    in_=self.prun[p_]))(p, c), deps=list(state_tok))
                bcts = []
                for par in range(2):
                    r = slice(64 * par, 64 * par + 64)
                    bcts.append(S.op("t", (lambda r_, j_, b_: lambda e:
                        e.transpose(
                            b_[r_, 128:160].bitcast(BF16),
                            bcv[:, j_, :][r_, :], self.i64two_bf[r_, :]))(
                            r, j, bB), deps=[bcx, t1]))
                ebt = S.op("a", (lambda j_, b_: lambda e: e.copy(
                    out=bctv[:, j_, :], in_=b_[:, 128:160]
                    .bitcast(BF16)))(j, bB), deps=bcts)
                mstc = []
                for par in range(2):
                    r = slice(64 * par, 64 * par + 64)
                    mstc.append(S.op("t", (lambda p_, r_, j_, b_: lambda e:
                        e.matmul(b_[r_, 192:256],
                                 mcsv[:, j_, :][r_, :],
                                 self.stloc[p_][r_, :],
                                 start=True, stop=True))(p, r, j, bB),
                        deps=[ac, ebt] + list(state_tok) + s3))
                evst = S.op("v", (lambda p_, j_, b_: lambda e: e.tensor_tensor(
                    out=self.stloc[p_], in0=bctv[:, j_, :],
                    in1=b_[:, 192:256],
                    op=ALU.add))(p, j, bB), deps=mstc + [ebt])
                mprs = []
                for par in range(2):
                    r = slice(64 * par, 64 * par + 64)
                    mprs.append(S.op("t", (lambda p_, r_, j_, b_: lambda e:
                        e.matmul(b_[r_, 384:448],
                                 self.prunT[p_][r_, :], mcsv[:, j_, :][r_, :],
                                 start=True, stop=True))(p, r, j, bA),
                        deps=[ac, spr, tot, bcx]))
                    mprs.append(S.op("t", (lambda p_, r_, j_, b_: lambda e:
                        e.matmul(b_[r_, 448:512],
                                 mcsv[:, j_, :][r_, :], self.prunT[p_][r_, :],
                                 start=True, stop=True))(p, r, j, bA),
                        deps=[ac, spr, tot, bcx]))
                evpr = S.op("a", (lambda p_, j_, b_: lambda e: e.copy(
                    out=self.prun[p_], in_=b_[:, 384:448]))(
                    p, j, bA), deps=mprs + [spr])
                evprt = S.op("a", (lambda p_, j_, b_: lambda e: e.copy(
                    out=self.prunT[p_], in_=b_[:, 448:512]))(p, j, bA),
                    deps=mprs)
                s4 += [tot, evst, evpr, evprt, s2, ac, bcx, ebt]
            wave_done = s4 + s1l
            all_done += s4
        return [S.op("v", lambda e: e.engine_nop(), deps=all_done)]

    def corrphase(self):
        S, io, pb = self.S, self.io, self.pb
        arB, arF = self.arB, self.arF
        ragf = [arB.bf16(128, H * 2 * DH) for _ in range(2)]
        ragb = [arB.bf16(128, H * 2 * DH) for _ in range(2)]
        ipat_f = arF.f32(128, H * 2 * DH)
        sint = [arB.bf16(128, DH) for _ in range(H // 2)]
        mc = arB.bf16(128, DH)
        qptl = arB.bf16(128, 8 * C)
        qv = qptl.rearrange("p (pr t) -> p pr t", t=C)
        lip = S.dma("s", lambda e: e.dma_start(out=ipat_f, in_=io["ipat"]))
        zi = [S.op("v", (lambda p_: lambda e: e.memset(sint[p_], 0.0))(p))
              for p in range(H // 2)]
        ptoks = list(zi)
        rag_busy = [None, None]
        for j in range(NCORE):
            pp = j % 2
            lt = [S.dma("s", (lambda j_, p_: lambda e: e.dma_start(
                      out=ragf[p_][0:64, :],
                      in_=io["ags_out"][64 * j_:64 * (j_ + 1), :]))(j, pp),
                  deps=[self.ag3] + ([rag_busy[pp]] if rag_busy[pp] else [])),
                  S.dma("s", (lambda j_, p_: lambda e: e.dma_start(
                      out=ragf[p_][64:128, :],
                      in_=io["ags_out"][64 * j_:64 * (j_ + 1), :]))(j, pp),
                  deps=[self.ag3] + ([rag_busy[pp]] if rag_busy[pp] else []))]
            t1 = S.op("v", (lambda j_, p_: lambda e: e.tensor_scalar(
                out=ragf[p_], in0=ragf[p_],
                scalar1=self.hm_sb[:, j_:j_ + 1], scalar2=None,
                op0=ALU.mult))(j, pp), deps=lt + [self.pt["hm"]])
            t2 = S.op("v", (lambda j_, p_: lambda e: e.scalar_tensor_tensor(
                out=ragb[p_], in0=ipat_f,
                scalar=self.hm1_sb[:, j_:j_ + 1], in1=ragf[p_],
                op0=ALU.mult, op1=ALU.add))(j, pp),
                deps=[t1, lip, self.t_hm1])
            mms = []
            for p in range(H // 2):
                for par in range(2):
                    h = 2 * p + par
                    r = slice(64 * par, 64 * par + 64)
                    mms.append(S.op("t", (lambda p2, h_, p_, r_: lambda e:
                        e.matmul(
                            pb[0][r_, 64 * p_:64 * (p_ + 1)],
                            ragb[p2][r_, 2 * DH * h_:2 * DH * h_ + DH],
                            sint[p_][r_, :], start=True, stop=True))(pp, h, p, r),
                        deps=[t2] + ptoks))
            ntoks = []
            for p in range(H // 2):
                for par in range(2):
                    h = 2 * p + par
                    r = slice(64 * par, 64 * par + 64)
                    ntoks.append(S.op("v", (lambda p2, h_, p_, r_: lambda e:
                        e.tensor_tensor(
                            out=sint[p_][r_, :],
                            in0=ragb[p2][r_, 2 * DH * h_ + DH:2 * DH * (h_ + 1)],
                            in1=pb[0][r_, 64 * p_:64 * (p_ + 1)],
                            op=ALU.add))(pp, h, p, r),
                        deps=mms))
            rag_busy[pp] = ntoks[-1]
            ptoks = ntoks
        mcl = arB.bf16(128, 512)
        mclv = mcl.rearrange("p (pr x) -> p pr x", x=DH)
        prev = ptoks
        for c in range(NT):
            lq = S.dma("s", (lambda c_: lambda e: e.dma_start(
                out=qptl,
                in_=io["qpt_dram"][16 * c_:16 * (c_ + 1), :]
                .rearrange("(pr two) (d t) -> two d pr t", two=2, d=DH)))(c),
                deps=prev)
            mm1s = []
            for p in range(H // 2):
                for par in range(2):
                    r = slice(64 * par, 64 * par + 64)
                    mm1s.append(S.op("t", (lambda c_, p_, r_: lambda e:
                        e.matmul(
                            pb[1][r_, 64 * p_:64 * (p_ + 1)],
                            self.prun_store[r_, (c_ * (H // 2) + p_) * DH:
                                            (c_ * (H // 2) + p_ + 1) * DH],
                            sint[p_][r_, :], start=True, stop=True))(c, p, r),
                        deps=prev))
            evm = S.op("a", lambda e: e.copy(out=mcl, in_=pb[1]),
                       deps=mm1s)
            mm2s = []
            for p in range(H // 2):
                for par in range(2):
                    r = slice(64 * par, 64 * par + 64)
                    mm2s.append(S.op("t", (lambda p_, r_: lambda e: e.matmul(
                        pb[2 + p_ // 4][r_, 128 * (p_ % 4):128 * (p_ % 4) + 128],
                        mclv[:, p_, :][r_, :], qv[:, p_, :][r_, :],
                        start=True, stop=True))(p, r), deps=[evm, lq]))
            ntoks = []
            for p in range(H // 2):
                ntoks.append(S.op("v", (lambda c_, p_: lambda e:
                    e.tensor_tensor(
                        out=self.Ot[p_][:, C * c_:C * (c_ + 1)],
                        in0=self.Ot[p_][:, C * c_:C * (c_ + 1)],
                        in1=pb[2 + p_ // 4][:, 128 * (p_ % 4):
                                            128 * (p_ % 4) + 128],
                        op=ALU.add))(c, p), deps=mm2s))
            prev = ntoks
        self.corr_done = prev

    def woutphase(self):
        S, io, pb = self.S, self.io, self.pb
        arB = self.arB
        woutT = [arB.bf16(128, D) for _ in range(NKC)]
        wl = []
        for k in range(NKC):
            wl.append(S.dma("s", (lambda k_, d_: lambda e: e.dma_start(
                out=d_, in_=io["wout_dram"][k_], transpose=True))(k, woutT[k]),
                deps=self.w_wout))
        wf = []
        for k in range(NKC):
            wf.append(S.op("v", (lambda k_: lambda e: e.tensor_tensor(
                out=woutT[k_], in0=woutT[k_], in1=self.rep["ga"],
                op=ALU.mult))(k), deps=[wl[k], self.t_ga]))
        x_t = self.x_t
        x2_t = self.ln_xs
        x2_toks = []
        prev = list(self.corr_done)
        for i in range(NT):
            lx = S.dma("s", (lambda i_: lambda e: e.dma_start(
                out=x_t, in_=io["x_in"][128 * i_:128 * (i_ + 1), :]))(i),
                deps=prev)
            mms = [None, None]
            for k in range(NKC):
                for n in range(2):
                    dd = [wf[k]] + list(self.corr_done)
                    if mms[n]:
                        dd.append(mms[n])
                    if k == 0:
                        dd += x2_toks[-3:]
                    mms[n] = S.op("t", (lambda i_, k_, n_: lambda e: e.matmul(
                        pb[n_], self.Ot[k_][:, 128 * i_:128 * (i_ + 1)],
                        woutT[k_][:, 512 * n_:512 * (n_ + 1)],
                        start=(k_ == 0), stop=(k_ == NKC - 1)))(i, k, n),
                        deps=dd)
            ev = []
            for n in range(2):
                ev.append(S.op("v", (lambda n_: lambda e: e.tensor_tensor(
                    out=x2_t[:, 512 * n_:512 * (n_ + 1)],
                    in0=x_t[:, 512 * n_:512 * (n_ + 1)], in1=pb[n_],
                    op=ALU.add))(n), deps=[mms[n], lx]))
            st = S.dma("s", (lambda i_: lambda e: e.dma_start(
                out=io["x2_dram"][128 * i_:128 * (i_ + 1), :], in_=x2_t))(i),
                deps=ev)
            x2_toks = ev + [st]
            prev = [st]
        self.x2_done = prev

    def ffnphase(self):
        S, io, pb = self.S, self.io, self.pb
        arB, arF = self.arB, self.arF
        self.set_qscr(1408, dual=True)
        x_t = self.x_t
        silu_sb = arF.f32(128, 1408)
        h2f = silu_sb[:, 0:D]
        h2q = arB.bf16(128, D)
        qtoks = []
        prev = list(self.x2_done)
        for i in range(NT):
            lx = S.dma("s", (lambda i_: lambda e: e.dma_start(
                out=x_t, in_=io["x2_dram"][128 * i_:128 * (i_ + 1), :]))(i),
                deps=prev)
            th = self.ln_mod(x_t, self.rep["cb"], self.rep["sb2"], h2f, [lx])
            tq = self.quant_seq(h2f, h2q, [th] + prev, eng="v")
            st = S.dma("s", (lambda i_: lambda e: e.dma_start(
                out=io["h2q_dram"][:, 128 * i_:128 * (i_ + 1), :]
                    .rearrange("c p k -> p c k"), in_=h2q))(i), deps=[tq])
            qtoks.append(st)
            prev = [st, tq]
        h2qT_1 = [arB.bf16(128, 512) for _ in range(NKC)]
        h2qT = [h2qT_1, h2qT_1]
        wgu = [arB.bf16(128, 2816) for _ in range(NKC)]
        hm_f = arF.f32(128, 1408)
        hmq = arB.bf16(128, 1408)
        hql_tok = [[None] * NKC, [None] * NKC]
        gstore = []
        prev_readers = []
        prev_q = []
        prev_mult = []
        nsub = ((0, 512), (512, 512), (1024, 384))
        for G in range(2):
            wls = []
            for k in range(NKC):
                wls.append(S.dma("s", (lambda k_, G_, d_: lambda e: e.dma_start(
                    out=d_[:, 0:1408],
                    in_=io["agw_out"][128 * k_:128 * (k_ + 1),
                                      1408 * G_:1408 * (G_ + 1)]))(k, G, wgu[k]),
                    deps=[self.ag1] + gstore[-2:]))
                wls.append(S.dma("s", (lambda k_, G_, d_: lambda e: e.dma_start(
                    out=d_[:, 1408:2816],
                    in_=io["agw_out"][128 * k_:128 * (k_ + 1),
                                      IF + 1408 * G_:IF + 1408 * (G_ + 1)]))(k, G, wgu[k]),
                    deps=[self.ag1] + gstore[-2:]))
            for i in range(NT):
                qtr, buf = i // 4, (i // 4) % 2
                if i % 4 == 0:
                    for k in range(NKC):
                        hql_tok[buf][k] = S.dma("s", (lambda k_, d_, q_:
                            lambda e: e.dma_start(
                                out=d_,
                                in_=io["h2q_dram"][k_][512 * q_:512 * (q_ + 1), :],
                                transpose=True))(k, h2qT[buf][k], qtr),
                            deps=qtoks + gstore[-12:] +
                                 ([hql_tok[buf][k]] if hql_tok[buf][k] else []))
                io_i = 128 * (i % 4)
                mms = [None] * 6
                for k in range(NKC):
                    for n in range(6):
                        half_gu, n3 = divmod(n, 3)
                        src_c = 1408 * half_gu + nsub[n3][0]
                        w_ = nsub[n3][1]
                        dd = [wls[2 * k], wls[2 * k + 1], hql_tok[buf][k]]
                        if mms[n]:
                            dd.append(mms[n])
                        if k == 0:
                            dd += prev_readers
                        mms[n] = S.op("t", (lambda k_, n_, sc_, w2_, b2, o_:
                            lambda e: e.matmul(
                                pb[n_][:, 0:w2_],
                                h2qT[b2][k_][:, o_:o_ + 128],
                                wgu[k_][:, sc_:sc_ + w2_],
                                start=(k_ == 0), stop=(k_ == NKC - 1)))(
                                k, n, src_c, w_, buf, io_i), deps=dd)
                sil = []
                for n3, (c0, w_) in enumerate(nsub):
                    sil.append(S.op("a", (lambda n_, c0_, w2_: lambda e:
                        e.activation(out=silu_sb[:, c0_:c0_ + w2_],
                                     in_=pb[n_][:, 0:w2_], func=AF.Silu))(
                                     n3, c0, w_), deps=[mms[n3]] + prev_mult))
                mult = []
                for n3, (c0, w_) in enumerate(nsub):
                    mult.append(S.op("v", (lambda n_, c0_, w2_: lambda e:
                        e.tensor_tensor(
                            out=hm_f[:, c0_:c0_ + w2_],
                            in0=silu_sb[:, c0_:c0_ + w2_],
                            in1=pb[3 + n_][:, 0:w2_], op=ALU.mult))(n3, c0, w_),
                        deps=[sil[n3], mms[3 + n3]] + prev_q))
                qeng = "v"
                tq = self.quant_seq(hm_f, hmq, mult + prev_q, eng=qeng)
                prev_q = self._q_src
                st = S.dma("s", (lambda i_, G_: lambda e: e.dma_start(
                    out=io["hmid_dram"][11 * G_:11 * (G_ + 1),
                                        128 * i_:128 * (i_ + 1), :]
                        .rearrange("c p k -> p c k"), in_=hmq))(i, G),
                    deps=[tq])
                gstore.append(st)
                prev_q = prev_q + [st]
                prev_readers = sil + mult
                prev_mult = mult
        self.epoch()
        NIF = IF // 128
        wdnT = [arB.bf16(128, D) for _ in range(NIF)]
        wdl = []
        for k in range(NIF):
            wdl.append(S.dma("s", (lambda k_, d_: lambda e: e.dma_start(
                out=d_, in_=io["agd_out"][128 * k_:128 * (k_ + 1), :]))(k, wdnT[k]),
                deps=[self.ag2]))
        wdf = []
        for k in range(NIF):
            wdf.append(S.op("v", (lambda k_: lambda e: e.tensor_tensor(
                out=wdnT[k_], in0=wdnT[k_], in1=self.rep["gb"],
                op=ALU.mult))(k), deps=[wdl[k], self.t_gb]))
        hmT = [arB.bf16(128, 512) for _ in range(NIF)]
        out_toks = []
        prev_ev = []
        for qtr in range(NT // 4):
            htl = []
            for k in range(NIF):
                htl.append(S.dma("s", (lambda k_, d_, q_: lambda e: e.dma_start(
                    out=d_,
                    in_=io["hmid_dram"][k_][512 * q_:512 * (q_ + 1), :],
                    transpose=True))(k, hmT[k], qtr),
                    deps=gstore + out_toks[-3:]))
            for i2 in range(4):
                i = 4 * qtr + i2
                lx = S.dma("s", (lambda i_: lambda e: e.dma_start(
                    out=x_t, in_=io["x2_dram"][128 * i_:128 * (i_ + 1), :]))(i),
                    deps=out_toks[-1:] if out_toks else gstore[-1:])
                mms = [None, None]
                for k in range(NIF):
                    for n in range(2):
                        dd = [htl[k], wdf[k]]
                        if mms[n]:
                            dd.append(mms[n])
                        if k == 0:
                            dd += prev_ev
                        mms[n] = S.op("t", (lambda i2_, k_, n_: lambda e:
                            e.matmul(pb[6 + n_],
                                     hmT[k_][:, 128 * i2_:128 * (i2_ + 1)],
                                     wdnT[k_][:, 512 * n_:512 * (n_ + 1)],
                                     start=(k_ == 0), stop=(k_ == NIF - 1)))(
                                     i2, k, n), deps=dd)
                ev = []
                for n in range(2):
                    ev.append(S.op("v", (lambda n_: lambda e: e.tensor_tensor(
                        out=x_t[:, 512 * n_:512 * (n_ + 1)],
                        in0=x_t[:, 512 * n_:512 * (n_ + 1)], in1=pb[6 + n_],
                        op=ALU.add))(n), deps=[mms[n], lx]))
                st = S.dma("s", (lambda i_: lambda e: e.dma_start(
                    out=io["out_x"][128 * i_:128 * (i_ + 1), :], in_=x_t))(i),
                    deps=ev)
                out_toks.append(st)
                prev_ev = list(ev)
        self.out_toks = out_toks

    def finish(self):
        S = self.S
        S.op("v", lambda e: e.engine_nop(), deps=self.out_toks)
        S.op("s", lambda e: e.nop(), deps=self.out_toks)


_NC_CACHE = {}


def _get_nc(debug=()):
    key = tuple(sorted(debug))
    if key not in _NC_CACHE:
        _NC_CACHE[key] = build_nc(key)
    return _NC_CACHE[key]


def host_inputs(full):
    f32 = np.float32
    x = np.ascontiguousarray(full["L0_attn_x"], dtype=f32)
    s_idx = np.arange(128)[:, None]
    t_idx = np.arange(128)[None, :]
    mask_ts = np.where(s_idx > t_idx, 0.0, BIG).astype(f32)
    mask_sts = np.where(s_idx < t_idx, 0.0, -BIG).astype(f32)
    mask_sti = np.where(s_idx <= t_idx, 0.0, -BIG).astype(f32)
    i64 = np.eye(64, dtype=f32)
    i64two = np.concatenate([i64, i64], axis=0)
    i64rep8 = np.tile(i64two, (1, 8))
    ipat = np.zeros((128, H * 2 * DH), f32)
    for hh in range(H):
        ipat[0:64, 2 * DH * hh:2 * DH * hh + DH] = i64
        ipat[64:128, 2 * DH * hh:2 * DH * hh + DH] = i64
    shared = {
        "Wqkv": np.ascontiguousarray(full["L0_attn_Wqkv"], dtype=f32),
        "Wg": np.ascontiguousarray(full["L0_attn_Wg"], dtype=f32),
        "bg": np.ascontiguousarray(full["L0_attn_bg"], dtype=f32).reshape(1, H),
        "Wb": np.ascontiguousarray(full["L0_attn_Wb"], dtype=f32),
        "bb": np.ascontiguousarray(full["L0_attn_bb"], dtype=f32).reshape(1, H),
        "Wout": np.ascontiguousarray(full["L0_attn_Wout"], dtype=f32),
        "c_vec": np.ascontiguousarray(full["L0_c"], dtype=f32),
        "wmod": np.ascontiguousarray(full["L0_wmod"], dtype=f32),
        "ident128": np.eye(128, dtype=f32),
        "mask_ts": mask_ts, "mask_sts": mask_sts, "mask_sti": mask_sti,
        "i64two": i64two, "i64rep8": i64rep8, "ipat": ipat,
        "bdmask": ((s_idx < 64) == (t_idx < 64)).astype(f32),
    }
    in_maps = []
    for core in range(NCORE):
        b, q = divmod(core, 4)
        m = dict(shared)
        m["x_slice"] = np.ascontiguousarray(x[b, TQ * q:TQ * (q + 1)])
        m["wgate_sh"] = np.ascontiguousarray(
            full["L0_ffn_w_gate"][:, 128 * core:128 * (core + 1)], dtype=f32)
        m["wup_sh"] = np.ascontiguousarray(
            full["L0_ffn_w_up"][:, 128 * core:128 * (core + 1)], dtype=f32)
        m["wdown_sh"] = np.ascontiguousarray(
            full["L0_ffn_w_down"][:, 352 * core:352 * (core + 1)], dtype=f32)
        sel = np.zeros((B, 128), f32)
        sel[b, :] = 1.0
        m["sel128"] = sel
        hm = np.zeros((1, NCORE), f32)
        hm[0, 4 * b:4 * b + q] = 1.0
        m["hopmask"] = hm
        in_maps.append(m)
    return in_maps


def kernel(**inputs):
    nc, dbg = _get_nc()
    in_maps = host_inputs(inputs)
    res = run_bass_kernel_spmd(nc, in_maps, core_ids=list(range(NCORE)))
    out = np.zeros((B, T, D), np.float32)
    for core in range(NCORE):
        b, q = divmod(core, 4)
        out[b, TQ * q:TQ * (q + 1)] = res.results[core]["out_x"]
    return out
